# version 06
# baseline (speedup 1.0000x reference)
import numpy as np
import ml_dtypes
import concourse.bass as bass
import concourse.mybir as mybir
from concourse.bass_utils import run_bass_kernel_spmd

F32 = mybir.dt.float32
BF16 = mybir.dt.bfloat16
Alu = mybir.AluOpType
Act = mybir.ActivationFunctionType
AX = mybir.AxisListType.X
MAGIC = float(np.float32(1.5 * 2.0 ** 23))
INV_QMAX = float(np.float32(1.0 / 32767.0))

NCORES = 8
T = 2048
HALO = 128
TH = T + HALO
D = 1024
DI = 2048
FF = 3072
NS = 16
EPS = 1e-5

SLOT_OFF = [0, 262144, 524288, 786432, 1048576, 1310720, 1572864]
RANK_BYTES = 2359296


def col_addr(j):
    return j % NCORES, SLOT_OFF[j // NCORES] // 2


DEBUG_SYNC = False


def build_program(debug_sync=False, trunc=None, dbg=False):
    nc = bass.Bass("TRN2", target_bir_lowering=False,
                   detect_race_conditions=debug_sync)

    def din(name, shape, dt=F32):
        return nc.dram_tensor(name, shape, dt, kind="ExternalInput").ap()

    x_h = din("x", [TH, D])
    winT_h = din("winT", [16, 8, 128, 128])
    inb_h = din("inb", [128, 16])
    convw_h = din("convw", [128, 64])
    convb_h = din("convb", [128, 16])
    xpwT_h = din("xpwT", [16, 128, 32])
    xpba_h = din("xpba", [16, 1])
    xpbb_h = din("xpbb", [16, 1])
    outwT_h = din("outwT", [8, 16, 128])
    outb_h = din("outb", [128, 8])
    lpq_rows_gu_h = din("lpq_rows_gu", [6, 128, D])
    lpq_lhsT_gu_h = din("lpq_lhsT_gu", [6, 8, 128, 128])
    lpq_rows_d_h = din("lpq_rows_d", [128, FF])
    lpq_lhsT_d_h = din("lpq_lhsT_d", [24, 128, 128])
    c_h = din("cvec", [128, 8])
    wmod_h = din("wmod", [6, 8, 128, 128])
    ident_f_h = din("identf", [128, 128])
    ident_b_h = din("identb", [128, 128], BF16)
    egu_h = din("egu", [32, 8 * 128])
    ed_h = din("ed", [96, 24 * 128])
    onehot_h = din("onehot", [NS, 8])
    hmask_h = din("hmask", [128, 1])

    out_h = nc.dram_tensor("out", [T, D], F32, kind="ExternalOutput").ap()
    dbg_h = nc.dram_tensor("dbg", [128, 16384], F32,
                           kind="ExternalOutput").ap() if dbg else None

    ag0_in = nc.dram_tensor("ag0_in", [1, 768], F32)
    ag0_out = nc.dram_tensor("ag0_out", [NCORES, 768], F32)
    agw_in = nc.dram_tensor("agw_in", [1, RANK_BYTES // 2], BF16)
    agw_out = nc.dram_tensor("agw_out", [NCORES, RANK_BYTES // 2], BF16)
    ag1_in = nc.dram_tensor("ag1_in", [1, 32], F32)
    ag1_out = nc.dram_tensor("ag1_out", [NCORES, 32], F32)
    ar_in = nc.dram_tensor("ar_in", [1, 2], F32)
    ar_out = nc.dram_tensor("ar_out", [1, 2], F32)
    sumd = nc.dram_tensor("sumd", [1, T], F32)
    mxd = nc.dram_tensor("mxd", [128, 1], F32)
    scld = nc.dram_tensor("scld", [1, 32], F32)
    xspill = nc.dram_tensor("xspill", [16, 128, D], F32)
    upspill = nc.dram_tensor("upspill", [24, 4, 128, 512], BF16)

    SBUF_SZ = nc.SBUF_PARTITION_SIZE_BYTES
    AR_S = SBUF_SZ - 34816 - 32
    AR_H = AR_S - 69632
    AR_X = AR_H - 65536

    def at(name, shape, dt, off):
        return nc.alloc_sbuf_tensor_at(name, shape, dt, offset=off)

    x_tm = [at(f"x{i}", [128, D], F32, AR_X + 4096 * i) for i in range(16)]
    gP_lo = [at(f"gPl{m}", [128, T], BF16, AR_X + 4096 * m) for m in range(16)]
    wrow = at("wrow", [128, FF], F32, AR_H + 0)
    wlhs = at("wlhs", [128, FF], F32, AR_H + 12288)
    wqcol = at("wqcol", [128, FF], BF16, AR_H + 24576)
    wmodb = at("wmodb", [128, 6 * 1024], F32, AR_H + 30720)
    ed_s = at("ed_s", [96, 24 * 128], F32, AR_H + 55296)
    smax = at("smax", [128, 96], F32, AR_H + 67584)
    sval = at("sval", [128, 96], F32, AR_H + 67968)
    sinv = at("sinv", [128, 96], F32, AR_H + 68352)
    h_bf = [at(f"h{m}", [128, TH], BF16, AR_H + 4352 * m) for m in range(16)]
    gP_hi = [at(f"gPh{m}", [128, T], BF16, AR_H + 4096 * m) for m in range(8)]
    hx = [at(f"hx{j}", [128, T], BF16, AR_H + 32768 + 4096 * j) for j in range(8)]
    xrr = [at(f"xrr{q}", [128, D], F32, AR_H + 32768 + 4096 * q) for q in range(4)]
    h1_bf = [at(f"h1_{j}", [128, TH], BF16, AR_S + 4352 * j) for j in range(8)]
    scanA = at("scanA", [32, T], F32, AR_S + 0)
    scanB = at("scanB", [16, T], F32, AR_S + 8192)
    scanC = at("scanC", [16, T], F32, AR_S + 16384)
    scanD = at("scanD", [16, T], F32, AR_S + 24576)
    wffcol = [at(f"wff{p}", [128, FF], BF16, AR_S + 6144 * p) for p in range(2)]
    uprng = [at(f"upr{p}", [128, 512], BF16, AR_S + 12288 + 1024 * p) for p in range(8)]
    rtring = [at(f"rt{p}", [128, 1024], F32, AR_S + 20480 + 4096 * p) for p in range(2)]
    ctile = [at(f"ct{p}", [128, 512], BF16, AR_S + 28672 + 1024 * p) for p in range(4)]

    def gP(m):
        return gP_lo[m] if m < 16 else gP_hi[m - 16]

    A = nc.alloc_sbuf_tensor
    win_bf = [A(f"winb{p}", [128, 1024], BF16) for p in range(2)]
    hc = A("hc", [128, T], F32)
    sq1 = A("sq1", [128, D], F32)
    xstage = A("xstage", [128, D], F32)
    xhat = [A(f"xhat{p}", [128, D], BF16) for p in range(2)]
    y_bf = A("ybf", [16, T], BF16)
    convw = A("convw_s", [128, 64], F32)
    convb = A("convb_s", [128, 16], F32)
    xpw_bf = A("xpwb", [128, 512], BF16)
    xpba_s = A("xpba_s", [16, 1], F32)
    xpbb_s = A("xpbb_s", [16, 1], F32)
    outw_bf = A("outwb", [16, 1024], BF16)
    outb_s = A("outb_s", [128, 8], F32)
    adp = A("adp", [128, 64], F32)
    silc = A("silc", [128, 8], F32)
    inb_s = A("inb_s", [128, 16], F32)
    id_f = A("id_f", [128, 128], F32)
    id_b = A("id_b", [128, 128], BF16)
    egu_s = A("egu_s", [32, 8 * 128], F32)
    sT_s = A("sT_s", [96, 128], F32)
    invT_s = A("invT_s", [96, 128], F32)
    ones_c = A("ones_c", [128, 1], BF16)
    mxc = A("mxc", [128, 128], F32)
    mxc2 = A("mxc2", [128, 64], F32)
    mxrow = A("mxrow", [1, 128], F32)
    sumstg = A("sumstg", [1, 512], F32)
    stats = A("stats", [128, 16], F32)
    scl = A("scl", [1, 32], F32)
    sclb = A("sclb", [128, 32], F32)
    onehot_s = A("onehot_s", [NS, 8], F32)
    sin_s = A("sin_s", [NS, 1], F32)
    cgat = A("cgat", [NS, 8], F32)
    cgee = A("cgee", [NS, 8], F32)
    csc = A("csc", [NS, 8], F32)
    hmask_s = A("hmask_s", [128, 1], F32)
    modloc = A("modloc", [128, 6], F32)



    ps_mm = [nc.alloc_psum_tensor(f"psmm{i}", [128, 512], F32) for i in range(4)]
    ps_tr = nc.alloc_psum_tensor("pstr", [128, 1024], BF16)
    ps_tr2 = nc.alloc_psum_tensor("pstr2", [128, 1024], BF16)
    ps_a = nc.alloc_psum_tensor("psa", [128, 512], F32)
    ps_b = nc.alloc_psum_tensor("psb", [128, 512], F32)

    sem_h = {n: nc.alloc_semaphore(n) for n in ["sT", "sV", "sS"]}
    cnt = {n: 0 for n in sem_h}
    ENG_SEM = {"vector": "sV", "scalar": "sS", "tensor": "sT"}
    ops = []
    watermark = {e: {} for e in ["sync", "gpsimd", "vector", "scalar", "tensor"]}

    def sem_for(key):
        if key not in sem_h:
            sem_h[key] = nc.alloc_semaphore(key)
            cnt[key] = 0
        return sem_h[key]

    class Tok:
        __slots__ = ("eng", "sem", "val", "tiny")

        def __init__(self, eng, sem, val, tiny=False):
            self.eng, self.sem, self.val, self.tiny = eng, sem, val, tiny

    def _flat(deps):
        for d in deps:
            if d is None:
                continue
            if isinstance(d, (list, tuple)):
                yield from _flat(d)
            else:
                yield d

    def emit(eng, fn, deps=(), pub=False, dkey=None, cc=None, tiny=False):
        need = {}
        for d in _flat(deps):
            if (d.sem in ("sT", "sV", "sS") and d.eng == eng
                    and not debug_sync and not d.tiny):
                continue
            if need.get(d.sem, -1) < d.val:
                need[d.sem] = d.val
        waits = []
        wm = watermark[eng]
        for s, v in need.items():
            if wm.get(s, -1) >= v:
                continue
            wm[s] = v
            waits.append((sem_h[s], v))
        incs = []
        tok = None
        if cc is not None:
            sem_for(cc)
            cnt[cc] += 1
            incs.append((sem_h[cc], 1))
            tok = Tok(eng, cc, cnt[cc])
        elif dkey is not None:
            sem_for(dkey)
            cnt[dkey] += 16
            incs.append((sem_h[dkey], 16))
            tok = Tok(eng, dkey, cnt[dkey])
        elif pub or (debug_sync and eng in ENG_SEM):
            s = ENG_SEM[eng]
            cnt[s] += 1
            incs.append((sem_h[s], 1))
            tok = Tok(eng, s, cnt[s], tiny)
        ops.append({"eng": eng, "fn": fn, "waits": waits, "incs": incs})
        return tok

    last_dma = {}
    dbg_off = [0]
    dbg_names = []

    def dbg_cap(name, ap, deps, cols, parts=128):
        if not dbg:
            return
        o = dbg_off[0]
        dbg_names.append((name, o, cols, parts))
        dbg_off[0] += cols
        dma("gpsimd", "dbgk", dbg_h[0:parts, o:o + cols], ap, deps=list(deps),
            slow=True)
    build_program.dbg_names = dbg_names

    def dma(q, key, out, in_, deps=(), slow=False, chain=True):
        deps = list(deps)
        if chain and key in last_dma:
            deps.append(last_dma[key])
        t = emit(q, lambda e, o=out, i=in_, s=slow: e.dma_start(
            out=o, in_=i, allow_slow_non_contiguous=s), deps=deps, dkey=key)
        last_dma[key] = t
        return t

    def V(fn, deps=(), pub=False, tiny=False):
        return emit("vector", fn, deps, pub=pub, tiny=tiny)

    def S(fn, deps=(), pub=False, tiny=False):
        return emit("scalar", fn, deps, pub=pub, tiny=tiny)

    def TT(fn, deps=(), pub=False):
        return emit("tensor", fn, deps, pub=pub)

    l_convw = dma("sync", "ini0", convw[:, :], convw_h)
    l_convb = dma("sync", "ini1", convb[:, :], convb_h)
    l_inb = dma("sync", "ini2", inb_s[:, :], inb_h)
    l_xpb = dma("sync", "ini3", xpba_s[:, :], xpba_h)
    l_xpb2 = dma("sync", "ini1", xpbb_s[:, :], xpbb_h)
    l_idf = dma("sync", "ini0", id_f[:, :], ident_f_h)
    l_idb = dma("sync", "ini1", id_b[:, :], ident_b_h)
    l_egu = dma("sync", "ini2", egu_s[:, :], egu_h)
    l_ed = dma("sync", "ini3", ed_s[:, :], ed_h)
    l_outb = dma("sync", "ini0", outb_s[:, :], outb_h)
    l_oh = dma("sync", "ini1", onehot_s[:, :], onehot_h)
    l_hm = dma("sync", "ini2", hmask_s[:, :], hmask_h)
    l_c = dma("sync", "ini3", silc[:, :], c_h)
    l_wmod = dma("sync", "ini0", wmodb[:, :].rearrange("p (m k q) -> p m k q", m=6, k=8),
                  wmod_h.rearrange("m k p q -> p m k q"))
    l_xpw = dma("gpsimd", "gini0", xpw_bf[:, :].rearrange("p (k t) -> p k t", t=32),
                 xpwT_h.rearrange("k p t -> p k t"))
    l_outw = dma("gpsimd", "gini1", outw_bf[:, :].rearrange("p (m t) -> p m t", t=128),
                  outwT_h.rearrange("m p t -> p m t"))
    t_ones = V(lambda e: e.memset(ones_c[:, :], 1.0), pub=True)
    t_scl0 = V(lambda e: e.memset(scl[:, :], 0.0), pub=True)

    t_silc = S(lambda e: e.activation(silc[:, :], silc[:, :], Act.Silu),
               deps=[l_c], pub=True, tiny=True)

    mod_mm = []
    for m in range(6):
        for k in range(8):
            tk = TT(lambda e, m=m, k=k: e.matmul(
                ps_b[:, m:m + 1],
                wmodb[:, (m * 8 + k) * 128:(m * 8 + k) * 128 + 128],
                silc[:, k:k + 1], start=(k == 0), stop=(k == 7)),
                deps=[l_wmod, t_silc], pub=(k == 7))
        mod_mm.append(tk)
    mod_ev = []
    for m in range(6):
        mod_ev.append(V(lambda e, m=m: e.tensor_copy(
            modloc[:, m:m + 1], ps_b[:, m:m + 1]), deps=mod_mm, pub=True, tiny=True))
    t_modw = None
    for m in range(6):
        t_modw = dma("gpsimd", "ag0w", ag0_in.ap()[0:1, 128 * m:128 * m + 128]
                     .rearrange("one p -> p one"), modloc[:, m:m + 1],
                     deps=[mod_ev[m]])
    t_ag0 = emit("gpsimd", lambda e: e.collective_compute(
        "AllGather", Alu.bypass, replica_groups=[list(range(NCORES))],
        ins=[ag0_in.ap().opt()], outs=[ag0_out.ap().opt()]),
        deps=[t_modw], cc="cc0")
    flat0 = ag0_out.ap().rearrange("r v -> () (r v)")
    t_adp = dma("gpsimd", "adpu", adp[:, 0:48],
                flat0[0:1, 0:6144].rearrange("one (j p) -> p (one j)", p=128),
                deps=[t_ag0], slow=True)
    t_ad = [t_adp]
    t_ca1 = V(lambda e: e.tensor_scalar_add(adp[:, 8:16], adp[:, 8:16], 1.0),
              deps=t_ad, pub=True, tiny=True)
    t_cb1 = V(lambda e: e.tensor_scalar_add(adp[:, 32:40], adp[:, 32:40], 1.0),
              deps=t_ad, pub=True, tiny=True)
    gq_toks = []
    prev_gq = None
    for src0 in (16, 40):
        tsg = S(lambda e, s=src0: e.activation(adp[:, 48:56], adp[:, s:s + 8],
                                               Act.Sign),
                deps=t_ad + ([prev_gq] if prev_gq else []), pub=True, tiny=True)
        tab = S(lambda e, s=src0: e.activation(adp[:, 56:64], adp[:, s:s + 8],
                                               Act.Abs), deps=t_ad, pub=True, tiny=True)
        tmx = V(lambda e: e.tensor_scalar(adp[:, 0:8].bitcast(F32) if False else
                                          stats[:, 8:16], adp[:, 56:64], 0.25,
                                          None, Alu.max),
                deps=[tab], pub=True, tiny=True)
        prev_gq = V(lambda e, s=src0: e.tensor_tensor(
            adp[:, s:s + 8], adp[:, 48:56], stats[:, 8:16], Alu.mult),
            deps=[tsg, tmx], pub=True, tiny=True)

        gq_toks.append(prev_gq)
    t_gaq, t_gbq = gq_toks

    dbg_cap("adp", adp[:, :], [t_gaq, t_gbq, t_ca1, t_cb1], 64)
    tr_guard = [None, None]

    def tr_slot(si):
        return (ps_tr if si % 2 == 0 else ps_tr2)[
            :, 128 * ((si // 2) % 4):128 * ((si // 2) % 4) + 128]

    h1_toks = []
    x_toks = [None] * 16
    lnprev = None
    tr_ctr = [0]
    xhat_guard = [None, None]

    def ln_tile(i, dst, ap_dst_tok, affine_c1, affine_c0, out_chunk_fn, idtile,
                extra_deps=()):
        nonlocal lnprev
        t_sum = V(lambda e: e.tensor_reduce(stats[:, 0:1], dst[:, :], AX, Alu.add),
                  deps=[ap_dst_tok, lnprev] + list(extra_deps), pub=True, tiny=True)
        t_sq0 = S(lambda e: e.activation(sq1[:, :], dst[:, :], Act.Square),
                  deps=[ap_dst_tok, lnprev] + list(extra_deps), pub=True)
        t_sq = V(lambda e: e.tensor_reduce(stats[:, 1:2], sq1[:, :], AX, Alu.add),
                 deps=[t_sq0], pub=True, tiny=True)
        t_mn = V(lambda e: e.tensor_scalar_mul(stats[:, 2:3], stats[:, 0:1],
                                               1.0 / D), deps=[t_sum], pub=True, tiny=True)
        t_ms = V(lambda e: e.tensor_scalar_mul(stats[:, 3:4], stats[:, 1:2],
                                               1.0 / D), deps=[t_sq], pub=True, tiny=True)
        t_m2 = V(lambda e: e.tensor_tensor(stats[:, 4:5], stats[:, 2:3],
                                           stats[:, 2:3], Alu.mult),
                 deps=[t_mn], pub=True, tiny=True)
        t_vr = V(lambda e: e.tensor_tensor(stats[:, 5:6], stats[:, 3:4],
                                           stats[:, 4:5], Alu.subtract),
                 deps=[t_ms, t_m2], pub=True, tiny=True)
        t_ve = V(lambda e: e.tensor_scalar_add(stats[:, 6:7], stats[:, 5:6], EPS),
                 deps=[t_vr], pub=True, tiny=True)
        t_rc = V(lambda e: e.reciprocal(stats[:, 7:8], stats[:, 6:7]),
                 deps=[t_ve], pub=True, tiny=True)
        t_rs = S(lambda e: e.activation(stats[:, 8:9], stats[:, 7:8], Act.Sqrt),
                 deps=[t_rc], pub=True, tiny=True)
        if i == 2:
            dbg_cap("stats2", stats[:, :], [t_rs, t_mn, t_m2], 16)
            dbg_cap("xin2", dst[:, :], [t_rs], 1024)
        xh = xhat[i % 2]
        t_xh = V(lambda e: e.tensor_scalar(xh[:, :], dst[:, :], stats[:, 2:3],
                                           stats[:, 8:9], Alu.subtract, Alu.mult),
                 deps=[t_rs, xhat_guard[i % 2]] + list(extra_deps), pub=True)
        if i == 2:
            dbg_cap("xhat2", xh[:, :], [t_xh], 1024)
        lnprev = t_xh
        evs = []
        t_tr = None
        for j in range(8):
            si = tr_ctr[0] % 8
            tr_ctr[0] += 1
            t_tr = TT(lambda e, j=j, si=si: e.transpose(
                tr_slot(si), xh[:, 128 * j:128 * j + 128], id_b[:, :]),
                deps=[t_xh, l_idb, tr_guard[si % 2]], pub=True)
            ev = V(lambda e, j=j, si=si: e.tensor_scalar(
                out_chunk_fn(j), tr_slot(si), adp[:, affine_c1 + j:affine_c1 + j + 1],
                adp[:, affine_c0 + j:affine_c0 + j + 1], Alu.mult, Alu.add),
                deps=[t_tr, t_ca1, t_cb1], pub=True, tiny=True)
            tr_guard[si % 2] = ev
            evs.append(ev)
        xhat_guard[i % 2] = t_tr
        return evs

    agw_writes = []
    lp_guard = [None]
    lp_rm_guard = [None]

    def lpbq_slot(s):
        I = D if s < 6 else FF
        B = I // 32
        if s < 6:
            t_row = dma("sync", "lpr", wrow[:, 0:I], lpq_rows_gu_h[s],
                        deps=[lp_rm_guard[0]])
            t_lh = dma("sync", "lpl",
                       wlhs[:, 0:I].rearrange("p (k q) -> p k q", q=128),
                       lpq_lhsT_gu_h[s].rearrange("k p q -> p k q"),
                       deps=[lp_guard[0]])
        else:
            t_row = dma("sync", "lpr", wrow[:, 0:I], lpq_rows_d_h,
                        deps=[lp_rm_guard[0]])
            t_lh = dma("sync", "lpl",
                       wlhs[:, 0:I].rearrange("p (k q) -> p k q", q=128),
                       lpq_lhsT_d_h.rearrange("k p q -> p k q"),
                       deps=[lp_guard[0]])
        t_rm = V(lambda e, I=I, B=B: e.tensor_reduce(
            smax[:, 0:B].rearrange("p b -> p b ()"),
            wrow[:, 0:I].rearrange("p (b t) -> p b t", t=32),
            AX, Alu.max, apply_absolute_value=True), deps=[t_row], pub=True, tiny=True)
        lp_rm_guard[0] = t_rm
        t_s = V(lambda e, B=B: e.tensor_scalar(
            sval[:, 0:B], smax[:, 0:B], 1.0 / 7.0, 1e-12, Alu.mult, Alu.max),
            deps=[t_rm], pub=True, tiny=True)
        t_i = V(lambda e, B=B: e.reciprocal(sinv[:, 0:B], sval[:, 0:B]),
                deps=[t_s], pub=True, tiny=True)
        t_t1 = TT(lambda e, B=B: e.transpose(ps_mm[2][0:B, 0:128], sval[:, 0:B],
                                             id_f[:, :]),
                  deps=[t_s, l_idf, lp_guard[0]], pub=True)
        t_t2 = TT(lambda e, B=B: e.transpose(ps_mm[3][0:B, 0:128], sinv[:, 0:B],
                                             id_f[:, :]),
                  deps=[t_i, l_idf, lp_guard[0]], pub=True)
        B4 = B
        ev1 = V(lambda e, B=B: e.tensor_copy(sT_s[0:B, :], ps_mm[2][0:B, 0:128]),
                deps=[t_t1], pub=True, tiny=True)
        ev2 = V(lambda e, B=B: e.tensor_copy(invT_s[0:B, :], ps_mm[3][0:B, 0:128]),
                deps=[t_t2], pub=True, tiny=True)
        E_s = egu_s if s < 6 else ed_s
        l_E = l_egu if s < 6 else l_ed
        wq_toks = []
        piece_guard = [agw_writes[-1]] if agw_writes else []
        for p in range(I // 512):
            exps = []
            for q in range(4):
                kc = 4 * p + q
                exps.append(TT(lambda e, q=q, kc=kc, B=B, E_s=E_s: e.matmul(
                    ps_a[:, 128 * q:128 * q + 128],
                    E_s[0:B, 128 * kc:128 * kc + 128], sT_s[0:B, :],
                    start=True, stop=True),
                    deps=[ev1, ev2, l_E] + mod_ev + piece_guard, pub=(q == 3)))
                exps.append(TT(lambda e, q=q, kc=kc, B=B, E_s=E_s: e.matmul(
                    ps_b[:, 128 * q:128 * q + 128],
                    E_s[0:B, 128 * kc:128 * kc + 128], invT_s[0:B, :],
                    start=True, stop=True),
                    deps=[ev1, ev2, l_E] + mod_ev + piece_guard, pub=(q == 3)))
            c0 = 512 * p
            t_u = V(lambda e, c0=c0: e.tensor_tensor(
                hc[:, 0:512], wlhs[:, c0:c0 + 512], ps_b[:, 0:512], Alu.mult),
                deps=[t_lh, exps[-1], exps[-2]], pub=True)
            t_t = V(lambda e: e.tensor_scalar_add(hc[:, 512:1024], hc[:, 0:512],
                                                  MAGIC), deps=[t_u], pub=True)
            wq_toks.append(V(lambda e, c0=c0: e.scalar_tensor_tensor(
                wqcol[:, c0:c0 + 512], hc[:, 512:1024], MAGIC, ps_a[:, 0:512],
                Alu.subtract, Alu.mult), deps=[t_t], pub=True))
            piece_guard = [t_u, wq_toks[-1]]
        lp_guard[0] = wq_toks[-1]
        eo = SLOT_OFF[s] // 2
        agw_writes.append(dma(
            "gpsimd", "agww",
            agw_in.ap()[0:1, eo:eo + 128 * I].rearrange(
                "one (p q) -> (one p) q", p=128),
            wqcol[:, 0:I], deps=wq_toks))

    for i in range(17):
        dst = xstage if i == 0 else x_tm[i - 1]
        t_x = dma("sync", f"xl{i % 4}", dst[:, :], x_h[128 * i:128 * i + 128, :])
        if i >= 1:
            x_toks[i - 1] = t_x
        evs = ln_tile(i, dst, t_x, 8, 0,
                      lambda j, i=i: h1_bf[j][:, 128 * i:128 * i + 128], id_b)
        h1_toks.extend(evs)
        if i < 7:
            lpbq_slot(i)
    dbg_cap("h1_0", h1_bf[0][:, 0:512], h1_toks, 512)
    t_agw = emit("gpsimd", lambda e: e.collective_compute(
        "AllGather", Alu.bypass, replica_groups=[list(range(NCORES))],
        ins=[agw_in.ap().opt()], outs=[agw_out.ap().opt()]),
        deps=agw_writes, cc="ccw")

    NTILES_IN = [(0, 128), (128, 512), (640, 512), (1152, 512), (1664, 512)]
    win_tok = [None, None]
    win_free = [None, None]
    bank_free = [None] * 4
    hprime = []
    for m in range(16):
        pb = m % 2
        win_tok[pb] = dma("gpsimd", f"win{pb}",
                          win_bf[pb][:, :].rearrange("p (k q) -> p k q", q=128),
                          winT_h[m].rearrange("k p q -> p k q"),
                          deps=[win_free[pb]])
        ev_toks = []
        for ni, (c0, w) in enumerate(NTILES_IN):
            bk = (m * 5 + ni) % 4
            mmt = None
            for k in range(8):
                mmt = TT(lambda e, pb=pb, bk=bk, k=k, c0=c0, w=w: e.matmul(
                    ps_mm[bk][:, 0:w], win_bf[pb][:, 128 * k:128 * k + 128],
                    h1_bf[k][:, c0:c0 + w], start=(k == 0), stop=(k == 7)),
                    deps=[win_tok[pb], bank_free[bk]] + (h1_toks if k == 0 else []),
                    pub=(k == 7))
            ev = V(lambda e, m=m, bk=bk, c0=c0, w=w: e.tensor_scalar(
                h_bf[m][:, c0:c0 + w], ps_mm[bk][:, 0:w], inb_s[:, m:m + 1], None,
                Alu.add),
                deps=[mmt, l_inb] + (agw_writes if m <= 7 else []), pub=True)
            bank_free[bk] = ev
            ev_toks.append(ev)
        win_free[pb] = ev_toks[-1]
        t_msk = V(lambda e, m=m: e.tensor_scalar(
            h_bf[m][:, HALO - 3:HALO], h_bf[m][:, HALO - 3:HALO],
            hmask_s[:, 0:1], None, Alu.mult), deps=ev_toks + [l_hm], pub=True)
        t_c = V(lambda e, m=m: e.tensor_scalar(
            hc[:, 0:T], h_bf[m][:, HALO - 3:HALO - 3 + T], convw[:, 4 * m:4 * m + 1],
            None, Alu.mult),
            deps=[t_msk, l_convw, hprime[-1] if hprime else None], pub=True)
        for j in (1, 2, 3):
            t_c = V(lambda e, m=m, j=j: e.scalar_tensor_tensor(
                hc[:, 0:T], h_bf[m][:, HALO - 3 + j:HALO - 3 + j + T],
                convw[:, 4 * m + j:4 * m + j + 1], hc[:, 0:T], Alu.mult, Alu.add),
                deps=[t_c], pub=True)
        hprime.append(S(lambda e, m=m: e.activation(
            h_bf[m][:, HALO:HALO + T], hc[:, 0:T], Act.Silu,
            bias=convb[:, m:m + 1]), deps=[t_c, l_convb], pub=True))

    dbg_cap("hprime0", h_bf[0][:, HALO:HALO + 512], hprime, 512)
    xp_toks = []
    for n in range(4):
        mmta = None
        mmtb = None
        for k in range(16):
            mmta = TT(lambda e, n=n, k=k: e.matmul(
                ps_a[0:16, 0:512], xpw_bf[:, 32 * k:32 * k + 16],
                h_bf[k][:, HALO + 512 * n:HALO + 512 * n + 512],
                start=(k == 0), stop=(k == 15)),
                deps=[l_xpw, hprime[k]] + (xp_toks[-2:] if xp_toks else []),
                pub=(k == 15))
            mmtb = TT(lambda e, n=n, k=k: e.matmul(
                ps_mm[0][0:16, 0:512], xpw_bf[:, 32 * k + 16:32 * k + 32],
                h_bf[k][:, HALO + 512 * n:HALO + 512 * n + 512],
                start=(k == 0), stop=(k == 15)),
                deps=[l_xpw, hprime[k], bank_free[0]]
                + (xp_toks[-2:] if xp_toks else []),
                pub=(k == 15))
        xp_toks.append(V(lambda e, n=n: e.tensor_scalar(
            scanA[0:16, 512 * n:512 * n + 512], ps_a[0:16, 0:512],
            xpba_s[0:16, 0:1], None, Alu.add), deps=[mmta, l_xpb], pub=True))
        xp_toks.append(V(lambda e, n=n: e.tensor_scalar(
            scanC[0:16, 512 * n:512 * n + 512], ps_mm[0][0:16, 0:512],
            xpbb_s[0:16, 0:1], None, Alu.add), deps=[mmtb, l_xpb2], pub=True))
    sum_toks = []
    for n in range(4):
        mmt = None
        for k in range(16):
            mmt = TT(lambda e, n=n, k=k: e.matmul(
                ps_b[0:1, 0:512], ones_c[:, :],
                h_bf[k][:, HALO + 512 * n:HALO + 512 * n + 512],
                start=(k == 0), stop=(k == 15)),
                deps=[t_ones, hprime[k]] + (sum_toks[-1:] if sum_toks else []),
                pub=(k == 15))
        t_se = V(lambda e: e.tensor_copy(sumstg[:, :], ps_b[0:1, 0:512]),
                 deps=[mmt, sum_toks[-1] if sum_toks else None], pub=True, tiny=True)
        sum_toks.append(dma("sync", "sumw", sumd.ap()[0:1, 512 * n:512 * n + 512],
                            sumstg[:, :], deps=[t_se]))
    t_e = S(lambda e: e.activation(scanB[0:16, :], scanA[0:16, :], Act.Exp),
            deps=xp_toks + sum_toks, pub=True)
    t_a = S(lambda e: e.activation(scanB[0:16, :], scanB[0:16, :], Act.Exp,
                                   scale=-1.0), deps=[t_e], pub=True)
    t_sl = S(lambda e: e.activation(scanC[:, :], scanC[:, :], Act.Silu),
             deps=xp_toks, pub=True)
    t_sb = dma("gpsimd", "sumb", scanD[:, :], sumd.ap().partition_broadcast(NS),
               deps=sum_toks)
    t_b = V(lambda e: e.tensor_tensor(scanC[:, :], scanC[:, :], scanD[:, :],
                                      Alu.mult), deps=[t_sl, t_sb], pub=True)
    t_y0 = V(lambda e: e.tensor_tensor_scan(scanD[:, :], scanB[0:16, :],
                                            scanC[:, :], 0.0, Alu.mult, Alu.add),
             deps=[t_a, t_b], pub=True)
    t_P = V(lambda e: e.tensor_tensor_scan(scanA[0:16, :], scanB[0:16, :],
                                           scanB[0:16, :], 1.0, Alu.mult,
                                           Alu.bypass), deps=[t_a], pub=True)
    t_c1 = dma("gpsimd", "cry", ag1_in.ap()[0:1, 0:16].rearrange("one n -> n one"),
               scanA[0:16, T - 1:T], deps=[t_P])
    t_c2 = dma("gpsimd", "cry", ag1_in.ap()[0:1, 16:32].rearrange("one n -> n one"),
               scanD[:, T - 1:T], deps=[t_y0])
    t_ag1 = emit("gpsimd", lambda e: e.collective_compute(
        "AllGather", Alu.bypass, replica_groups=[list(range(NCORES))],
        ins=[ag1_in.ap().opt()], outs=[ag1_out.ap().opt()]),
        deps=[t_c1, t_c2], cc="cc1")
    t_g1 = dma("gpsimd", "cryo", cgat[:, :], ag1_out.ap()[:, 0:16].rearrange("r n -> n r"),
               deps=[t_ag1], slow=True)
    t_g2 = dma("gpsimd", "cryo", cgee[:, :], ag1_out.ap()[:, 16:32].rearrange("r n -> n r"),
               deps=[t_ag1], slow=True)
    t_cs = V(lambda e: e.tensor_tensor_scan(csc[:, :], cgat[:, :], cgee[:, :],
                                            0.0, Alu.mult, Alu.add),
             deps=[t_g1, t_g2], pub=True, tiny=True)
    t_sel = V(lambda e: e.tensor_tensor(csc[:, :], csc[:, :], onehot_s[:, :],
                                        Alu.mult), deps=[t_cs, l_oh], pub=True, tiny=True)
    t_sin = V(lambda e: e.tensor_reduce(sin_s[:, 0:1], csc[:, :], AX, Alu.add),
              deps=[t_sel], pub=True, tiny=True)
    t_y = V(lambda e: e.tensor_tensor_scan(y_bf[0:16, :], scanB[0:16, :],
                                           scanC[:, :], sin_s[0:16, 0:1],
                                           Alu.mult, Alu.add),
            deps=[t_sin], pub=True)

    dbg_cap("a_", scanB[0:16, 0:512], [t_a], 512, parts=16)
    dbg_cap("b_", scanC[0:16, 0:512], [t_b, t_y0], 512, parts=16)
    dbg_cap("sin", sin_s[:, 0:1], [t_sin], 1, parts=16)
    dbg_cap("y", y_bf[0:16, 0:512], [t_y], 512, parts=16)
    res_toks = []
    ctile_free = [None] * 4
    for n in range(4):
        for mo in range(8):
            bk = (n * 8 + mo) % 4
            mmt = TT(lambda e, bk=bk, mo=mo, n=n: e.matmul(
                ps_mm[bk][:, 0:512], outw_bf[0:16, 128 * mo:128 * mo + 128],
                y_bf[0:16, 512 * n:512 * n + 512], start=True, stop=True),
                deps=[l_outw, t_y, bank_free[bk]], pub=True)
            sl = (n * 8 + mo) % 4
            ev = V(lambda e, bk=bk, mo=mo, sl=sl: e.tensor_scalar(
                ctile[sl][:, :], ps_mm[bk][:, 0:512], outb_s[:, mo:mo + 1],
                adp[:, 16 + mo:17 + mo], Alu.add, Alu.mult),
                deps=[mmt, t_gaq, l_outb, ctile_free[sl], t_c2, t_g1, t_g2],
                pub=True)
            bank_free[bk] = ev
            for q in range(4):
                si = tr_ctr[0] % 8
                tr_ctr[0] += 1
                t_tr = TT(lambda e, sl=sl, q=q, si=si: e.transpose(
                    tr_slot(si), ctile[sl][:, 128 * q:128 * q + 128], id_b[:, :]),
                    deps=[ev, tr_guard[si % 2]], pub=True)
                ra = V(lambda e, n=n, q=q, mo=mo, si=si: e.tensor_tensor(
                    x_tm[4 * n + q][:, 128 * mo:128 * mo + 128],
                    x_tm[4 * n + q][:, 128 * mo:128 * mo + 128],
                    tr_slot(si), Alu.add),
                    deps=[t_tr, x_toks[4 * n + q]], pub=True, tiny=True)
                tr_guard[si % 2] = ra
                res_toks.append(ra)
                ctile_free[sl] = t_tr

    dbg_cap("x2_0", x_tm[0][:, 0:512], res_toks, 512)
    h2_toks = []
    xsp_toks = []
    for i in range(16):
        evs = ln_tile(100 + i, x_tm[i], None, 32, 24,
                      lambda j, i=i: hx[j][:, 128 * i:128 * i + 128], id_b,
                      extra_deps=res_toks)
        h2_toks.extend(evs)
        xsp_toks.append(dma("sync", f"xsp{i % 4}", xspill[i], x_tm[i][:, :],
                            deps=evs))
    mx_h2 = []
    for j in range(8):
        mx_h2.append(V(lambda e, j=j: e.tensor_reduce(
            mxc[:, j:j + 1], hx[j][:, :], AX, Alu.max,
            apply_absolute_value=True), deps=h2_toks, pub=True, tiny=True))

    def global_max(src_col_range, dst_slot, deps):
        a, b = src_col_range
        t1 = V(lambda e: e.tensor_reduce(mxc[:, 127:128], mxc[:, a:b], AX,
                                         Alu.max), deps=deps, pub=True, tiny=True)
        t2 = dma("gpsimd", "mxb", mxd.ap(), mxc[:, 127:128], deps=[t1])
        t3 = dma("gpsimd", "mxb", mxrow[:, :],
                 mxd.ap().rearrange("p one -> one p"), deps=[t2])
        return V(lambda e, d=dst_slot: e.tensor_reduce(
            scl[0:1, d:d + 1], mxrow[:, :], AX, Alu.max), deps=[t3], pub=True, tiny=True)

    def allreduce_pair(src_slot, deps):
        t1 = dma("gpsimd", "arb", ar_in.ap(), scl[0:1, src_slot:src_slot + 2],
                 deps=deps)
        arid = ar_ctr[0]
        ar_ctr[0] += 1
        t2 = emit("gpsimd", lambda e: e.collective_compute(
            "AllReduce", Alu.max, replica_groups=[list(range(NCORES))],
            ins=[ar_in.ap().opt()], outs=[ar_out.ap().opt()]),
            deps=[t1], cc=f"ccr{arid}")
        return dma("gpsimd", "arb", scl[0:1, src_slot:src_slot + 2], ar_out.ap(),
                   deps=[t2])

    ar_ctr = [0]

    def bcast_scl(deps):
        t1 = dma("gpsimd", "bch", scld.ap(), scl[:, :], deps=deps)
        return dma("gpsimd", "bch", sclb[:, :],
                   scld.ap().partition_broadcast(128), deps=[t1])

    dbg_cap("h2_0", hx[0][:, 0:512], h2_toks, 512)
    t_mx1 = global_max((0, 8), 0, mx_h2 + [t_scl0])
    t_ar1 = allreduce_pair(0, [t_mx1])
    t_s1 = V(lambda e: e.tensor_scalar(scl[0:1, 1:2], scl[0:1, 0:1], INV_QMAX,
                                       1e-12, Alu.mult, Alu.max),
             deps=[t_ar1], pub=True, tiny=True)
    t_i1 = V(lambda e: e.reciprocal(scl[0:1, 2:3], scl[0:1, 1:2]),
             deps=[t_s1], pub=True, tiny=True)
    t_bc1 = bcast_scl([t_i1])

    dbg_cap("scl1", scl[:, :], [t_i1, t_bc1], 32, parts=1)
    dbg_cap("sclb1", sclb[:, :], [t_bc1], 32)
    xq_toks = []
    for j in range(8):
        for hhalf in range(2):
            rs = rtring[hhalf]
            c0 = 1024 * hhalf
            t_r = V(lambda e, j=j, c0=c0, rs=rs: e.tensor_scalar(
                rs[:, :], hx[j][:, c0:c0 + 1024], sclb[:, 2:3], MAGIC,
                Alu.mult, Alu.add), deps=[t_bc1] + h2_toks, pub=True)
            xq_toks.append(V(lambda e, j=j, c0=c0, rs=rs: e.tensor_scalar(
                hx[j][:, c0:c0 + 1024], rs[:, :], MAGIC, None, Alu.subtract),
                deps=[t_r], pub=True))

    scan_dead = [t_y, t_c1, t_c2, hprime[-1], xp_toks[-1], sum_toks[-1]]

    def ffn_col_load(j, pb, deps):
        r, eo = col_addr(j)
        I = D if j < 48 else FF
        return dma("sync", f"wff{pb}", wffcol[pb][:, 0:I],
                   agw_out.ap()[r:r + 1, eo:eo + 128 * I].rearrange(
                       "one (p q) -> (one p) q", p=128),
                   deps=deps + [t_agw] + scan_dead)

    wff_tok = [None, None]
    wff_free = [None, None]
    dbg_cap("xq_0", hx[0][:, 0:512], xq_toks, 512)
    gate_ev = []
    for m in range(24):
        pb = m % 2
        wff_tok[pb] = ffn_col_load(m, pb, [wff_free[pb]])
        for n in range(4):
            bk = (m * 4 + n) % 4
            mmt = None
            for k in range(8):
                mmt = TT(lambda e, pb=pb, bk=bk, k=k, n=n: e.matmul(
                    ps_mm[bk][:, :], wffcol[pb][:, 128 * k:128 * k + 128],
                    hx[k][:, 512 * n:512 * n + 512],
                    start=(k == 0), stop=(k == 7)),
                    deps=[wff_tok[pb], bank_free[bk]] + (xq_toks if k == 0 else []),
                    pub=(k == 7))
            ev = V(lambda e, m=m, n=n, bk=bk: e.tensor_scalar(
                gP(m)[:, 512 * n:512 * n + 512], ps_mm[bk][:, :], sclb[:, 1:2],
                None, Alu.mult),
                deps=[mmt, t_bc1] + ([xsp_toks[m]] if m < 16 else []), pub=True)
            bank_free[bk] = ev
            gate_ev.append(ev)
            gate_ev.append(V(lambda e, m=m, n=n: e.tensor_reduce(
                mxc[:, 4 * m + n:4 * m + n + 1], gP(m)[:, 512 * n:512 * n + 512],
                AX, Alu.max, apply_absolute_value=True), deps=[ev], pub=True, tiny=True))
        wff_free[pb] = gate_ev[-2]
    dbg_cap("gate_0", gP(0)[:, 0:512], gate_ev, 512)
    t_mxg = global_max((0, 96), 3, gate_ev)
    t_ar2 = allreduce_pair(3, [t_mxg])
    t_sg = V(lambda e: e.tensor_scalar(scl[0:1, 4:5], scl[0:1, 3:4], INV_QMAX,
                                       1e-12, Alu.mult, Alu.max),
             deps=[t_ar2], pub=True, tiny=True)
    t_ig = V(lambda e: e.reciprocal(scl[0:1, 5:6], scl[0:1, 4:5]),
             deps=[t_sg], pub=True, tiny=True)
    t_bc2 = bcast_scl([t_ig])

    up_ev = []
    up_spill_toks = {}
    upr_free = [None] * 8
    up_last_mxe = [None] * 8
    w1_toks = []
    w1_rt_guard = [None, None]

    def window1_chunk(m):
        for hh in range(2):
            c0 = 1024 * hh
            rs = rtring[hh]
            xh = xhat[hh]
            t1 = S(lambda e, m=m, c0=c0, rs=rs: e.activation(
                rs[:, :], gP(m)[:, c0:c0 + 1024], Act.Copy, bias=MAGIC,
                scale=sclb[:, 5:6]),
                deps=[t_bc2, w1_rt_guard[hh]] + gate_ev, pub=True)
            t2 = V(lambda e, m=m, c0=c0, rs=rs: e.tensor_scalar(
                gP(m)[:, c0:c0 + 1024], rs[:, :], MAGIC, None, Alu.subtract),
                deps=[t1], pub=True)
            t3 = S(lambda e, m=m, c0=c0, xh=xh: e.activation(
                xh[:, :], gP(m)[:, c0:c0 + 1024], Act.Sigmoid,
                scale=sclb[:, 4:5]), deps=[t2], pub=True)
            t4 = S(lambda e, xh=xh, rs=rs: e.activation(
                rs[:, :], xh[:, :], Act.Copy, bias=MAGIC, scale=65536.0),
                deps=[t3], pub=True)
            t5 = V(lambda e, rs=rs: e.tensor_scalar(
                rs[:, :], rs[:, :], MAGIC, 65535.0, Alu.subtract, Alu.min),
                deps=[t4], pub=True)
            t6 = V(lambda e, m=m, c0=c0, rs=rs: e.tensor_tensor(
                gP(m)[:, c0:c0 + 1024], rs[:, :], gP(m)[:, c0:c0 + 1024],
                Alu.mult), deps=[t5], pub=True)
            w1_rt_guard[hh] = t6
            w1_toks.append(V(lambda e, m=m, c0=c0, hh=hh: e.tensor_reduce(
                mxc2[:, (2 * m + hh):(2 * m + hh) + 1],
                gP(m)[:, c0:c0 + 1024], AX, Alu.max,
                apply_absolute_value=True), deps=[t6], pub=True, tiny=True))

    for m in range(24):
        pb = m % 2
        wff_tok[pb] = ffn_col_load(24 + m, pb, [wff_free[pb]])
        for n in range(4):
            bk = (m * 4 + n) % 4
            sl = (m * 4 + n) % 8
            mmt = None
            for k in range(8):
                mmt = TT(lambda e, pb=pb, bk=bk, k=k, n=n: e.matmul(
                    ps_mm[bk][:, :], wffcol[pb][:, 128 * k:128 * k + 128],
                    hx[k][:, 512 * n:512 * n + 512],
                    start=(k == 0), stop=(k == 7)),
                    deps=[wff_tok[pb], bank_free[bk]], pub=(k == 7))
            ev = V(lambda e, sl=sl, bk=bk: e.tensor_scalar(
                uprng[sl][:, :], ps_mm[bk][:, :], sclb[:, 1:2], None, Alu.mult),
                deps=[mmt, upr_free[sl], t_bc2], pub=True)
            bank_free[bk] = ev
            mxe = V(lambda e, m=m, n=n, sl=sl: e.tensor_reduce(
                mxc[:, 4 * m + n:4 * m + n + 1], uprng[sl][:, :], AX, Alu.max,
                apply_absolute_value=True), deps=[ev], pub=True, tiny=True)
            up_ev.append(mxe)
            up_last_mxe[sl] = mxe
            sp = dma("sync", f"upr{sl}", upspill.ap()[m, n], uprng[sl][:, :],
                     deps=[ev])
            up_spill_toks[(m, n)] = sp
            upr_free[sl] = sp
        window1_chunk(m)

    dbg_cap("scl2", scl[:, :], [t_bc2], 32, parts=1)
    dbg_cap("P_0", gP(0)[:, 0:512], w1_toks, 512)
    t_mxu = global_max((0, 96), 6, up_ev)
    t1 = V(lambda e: e.tensor_reduce(mxc2[:, 63:64], mxc2[:, 0:48], AX, Alu.max),
           deps=w1_toks, pub=True, tiny=True)
    t2 = dma("gpsimd", "mxb", mxd.ap(), mxc2[:, 63:64], deps=[t1])
    t3 = dma("gpsimd", "mxb", mxrow[:, :],
             mxd.ap().rearrange("p one -> one p"), deps=[t2])
    t_mxP = V(lambda e: e.tensor_reduce(scl[0:1, 7:8], mxrow[:, :], AX, Alu.max),
              deps=[t3, t_mxu], pub=True, tiny=True)
    t_ar3 = allreduce_pair(6, [t_mxu, t_mxP])
    t_sup = V(lambda e: e.tensor_scalar(scl[0:1, 8:9], scl[0:1, 6:7], INV_QMAX,
                                        1e-12, Alu.mult, Alu.max),
              deps=[t_ar3], pub=True, tiny=True)
    t_iup = V(lambda e: e.reciprocal(scl[0:1, 9:10], scl[0:1, 8:9]),
              deps=[t_sup], pub=True, tiny=True)
    t_cg = V(lambda e: e.tensor_scalar_mul(scl[0:1, 12:13], scl[0:1, 4:5],
                                           1.0 / 65536.0), deps=[t_sg], pub=True, tiny=True)
    t_mg2 = V(lambda e: e.tensor_tensor(scl[0:1, 20:21], scl[0:1, 7:8],
                                        scl[0:1, 12:13], Alu.mult),
              deps=[t_ar3, t_cg], pub=True, tiny=True)
    t_s4 = V(lambda e: e.tensor_scalar(scl[0:1, 10:11], scl[0:1, 20:21],
                                       INV_QMAX, 1e-12, Alu.mult, Alu.max),
             deps=[t_mg2], pub=True, tiny=True)
    t_i4 = V(lambda e: e.reciprocal(scl[0:1, 11:12], scl[0:1, 10:11]),
             deps=[t_s4], pub=True, tiny=True)
    t_c4 = V(lambda e: e.tensor_tensor(scl[0:1, 13:14], scl[0:1, 12:13],
                                       scl[0:1, 11:12], Alu.mult),
             deps=[t_i4], pub=True, tiny=True)
    t_s4u = V(lambda e: e.tensor_tensor(scl[0:1, 14:15], scl[0:1, 10:11],
                                        scl[0:1, 8:9], Alu.mult),
              deps=[t_s4, t_sup], pub=True, tiny=True)
    t_bc3 = bcast_scl([t_iup, t_c4, t_s4u])

    dbg_cap("scl3", scl[:, :], [t_bc3, t_s4u, t_c4], 32, parts=1)
    o_toks = []
    upr_free2 = list(up_last_mxe)
    for m in range(24):
        for n in range(4):
            sl = (m * 4 + n) % 8
            rd = dma("sync", f"upr{sl}", uprng[sl][:, :], upspill.ap()[m, n],
                     deps=[up_spill_toks[(m, n)], upr_free2[sl]])
            t1 = S(lambda e, sl=sl: e.activation(
                rtring[0][:, 0:512], uprng[sl][:, :], Act.Copy, bias=MAGIC,
                scale=sclb[:, 9:10]), deps=[rd, t_bc3, (o_toks[-1] if o_toks else None)], pub=True)
            t2 = V(lambda e, sl=sl: e.tensor_scalar(
                uprng[sl][:, :], rtring[0][:, 0:512], MAGIC, None, Alu.subtract),
                deps=[t1], pub=True)
            t3 = S(lambda e, m=m, n=n: e.activation(
                rtring[1][:, 0:512], gP(m)[:, 512 * n:512 * n + 512], Act.Copy,
                bias=MAGIC, scale=sclb[:, 13:14]),
                deps=[t_bc3] + w1_toks, pub=True)
            t4 = V(lambda e, m=m, n=n, sl=sl: e.scalar_tensor_tensor(
                gP(m)[:, 512 * n:512 * n + 512], rtring[1][:, 0:512], MAGIC,
                uprng[sl][:, :], Alu.subtract, Alu.mult),
                deps=[t2, t3], pub=True)
            upr_free2[sl] = t4
            o_toks.append(V(lambda e, m=m, n=n: e.tensor_reduce(
                mxc[:, 4 * m + n:4 * m + n + 1], gP(m)[:, 512 * n:512 * n + 512],
                AX, Alu.max, apply_absolute_value=True), deps=[t4], pub=True, tiny=True))
    dbg_cap("o_0", gP(0)[:, 0:512], o_toks, 512)
    t_mxo = global_max((0, 96), 15, o_toks)
    t_ar4 = allreduce_pair(15, [t_mxo])
    t_mo = V(lambda e: e.tensor_tensor(scl[0:1, 16:17], scl[0:1, 15:16],
                                       scl[0:1, 14:15], Alu.mult),
             deps=[t_ar4, t_s4u], pub=True, tiny=True)
    t_s5 = V(lambda e: e.tensor_scalar(scl[0:1, 17:18], scl[0:1, 16:17],
                                       INV_QMAX, 1e-12, Alu.mult, Alu.max),
             deps=[t_mo], pub=True, tiny=True)
    t_i5 = V(lambda e: e.reciprocal(scl[0:1, 18:19], scl[0:1, 17:18]),
             deps=[t_s5], pub=True, tiny=True)
    t_t5 = V(lambda e: e.tensor_tensor(scl[0:1, 19:20], scl[0:1, 14:15],
                                       scl[0:1, 18:19], Alu.mult),
             deps=[t_i5], pub=True, tiny=True)
    t_bc4 = bcast_scl([t_t5])
    t_gb5 = V(lambda e: e.tensor_scalar(adp[:, 48:56], adp[:, 40:48],
                                        sclb[:, 17:18], None, Alu.mult),
              deps=[t_bc4, t_gbq], pub=True, tiny=True)

    dbg_cap("scl4", scl[:, :], [t_bc4, t_gb5], 32, parts=1)
    dbg_cap("gb5", adp[:, 48:56], [t_gb5], 8)
    oq_toks = {}
    w3_rt_guard = [None, None]
    for m in range(24):
        for n in range(4):
            hh = (m * 4 + n) % 2
            rs = rtring[hh]
            t1 = S(lambda e, m=m, n=n, rs=rs: e.activation(
                rs[:, 0:512], gP(m)[:, 512 * n:512 * n + 512], Act.Copy,
                bias=MAGIC, scale=sclb[:, 19:20]),
                deps=[t_bc4, w3_rt_guard[hh]] + o_toks, pub=True)
            oq_toks[(m, n)] = V(lambda e, m=m, n=n, rs=rs: e.tensor_scalar(
                gP(m)[:, 512 * n:512 * n + 512], rs[:, 0:512], MAGIC, None,
                Alu.subtract), deps=[t1], pub=True)
            w3_rt_guard[hh] = oq_toks[(m, n)]

    dbg_cap("oq_0", gP(0)[:, 0:512], [oq_toks[(0, 0)], oq_toks[(0, 1)]], 512)
    st_toks = []
    xr_toks = [None] * 4
    up_mm_done = up_ev[-1]
    for n in range(4):
        for q in range(4):
            xr_toks[q] = dma("sync", f"xrr{q}", xrr[q][:, :],
                             xspill.ap()[4 * n + q],
                             deps=xsp_toks + st_toks[-4:] + [xq_toks[-1],
                                                            up_mm_done])
        for mo in range(8):
            pb = mo % 2
            wff_tok[pb] = ffn_col_load(48 + mo, pb, [wff_free[pb]])
            bk = mo % 4
            mmt = None
            for k in range(24):
                mmt = TT(lambda e, pb=pb, bk=bk, k=k, n=n: e.matmul(
                    ps_mm[bk][:, :], wffcol[pb][:, 128 * k:128 * k + 128],
                    gP(k)[:, 512 * n:512 * n + 512],
                    start=(k == 0), stop=(k == 23)),
                    deps=[wff_tok[pb], bank_free[bk], oq_toks[(k, n)]],
                    pub=(k == 23))
            sl = mo % 4
            ev = V(lambda e, bk=bk, mo=mo, sl=sl: e.tensor_scalar(
                ctile[sl][:, :], ps_mm[bk][:, :], adp[:, 48 + mo:49 + mo], None,
                Alu.mult), deps=[mmt, t_gb5, ctile_free[sl]], pub=True)
            bank_free[bk] = ev
            wff_free[pb] = mmt
            for q in range(4):
                si = tr_ctr[0] % 8
                tr_ctr[0] += 1
                t_tr = TT(lambda e, sl=sl, q=q, si=si: e.transpose(
                    tr_slot(si), ctile[sl][:, 128 * q:128 * q + 128], id_b[:, :]),
                    deps=[ev, tr_guard[si % 2]], pub=True)
                ra = V(lambda e, q=q, mo=mo, si=si: e.tensor_tensor(
                    xrr[q][:, 128 * mo:128 * mo + 128],
                    xrr[q][:, 128 * mo:128 * mo + 128], tr_slot(si), Alu.add),
                    deps=[t_tr, xr_toks[q]], pub=True)
                tr_guard[si % 2] = ra
                xr_toks[q] = ra
                ctile_free[sl] = t_tr
        for q in range(4):
            st_toks.append(dma("sync", f"ost{q}",
                               out_h[512 * n + 128 * q:512 * n + 128 * q + 128, :],
                               xrr[q][:, :], deps=[xr_toks[q]]))

    wait_list = st_toks[-4:]
    if dbg:
        wait_list = wait_list + [last_dma.get("dbgk")]
    emit("sync", lambda e: None, deps=wait_list)

    build_program.n_ops = len(ops)
    build_program.ops_dbg = [(o["eng"], o["fn"].__code__.co_firstlineno,
                              o["waits"], o["incs"]) for o in ops]
    if trunc is not None:
        ops[:] = ops[:trunc]
    with nc.Block() as block:
        def mk(eng_name):
            def run(e):
                for op in ops:
                    if op["eng"] != eng_name:
                        continue
                    for s, v in op["waits"]:
                        e.wait_ge(s, v)
                    ins = op["fn"](e)
                    for s, nn in op["incs"]:
                        assert ins is not None, "inc on op returning no instruction"
                        ins = ins.then_inc(s, nn)
            return run

        block.sync(mk("sync"))
        block.gpsimd(mk("gpsimd"))
        block.vector(mk("vector"))
        block.scalar(mk("scalar"))
        block.tensor(mk("tensor"))
    return nc


_NC_CACHE = None


def _get_nc():
    global _NC_CACHE
    if _NC_CACHE is None:
        _NC_CACHE = build_program(debug_sync=DEBUG_SYNC)
    return _NC_CACHE


def make_inmaps(L0_attn_x, L0_attn_in_w, L0_attn_in_b, L0_attn_conv_w,
                L0_attn_conv_b, L0_attn_xp_w, L0_attn_xp_b, L0_attn_out_w,
                L0_attn_out_b, L0_ffn_w_gate, L0_ffn_w_up, L0_ffn_w_down,
                L0_c, L0_wmod):
    f = np.float32
    x = np.ascontiguousarray(L0_attn_x, dtype=f).reshape(16384, D)
    in_w = np.ascontiguousarray(L0_attn_in_w, dtype=f)
    winT = np.ascontiguousarray(
        in_w.reshape(16, 128, 8, 128).transpose(0, 2, 3, 1))
    inb = np.ascontiguousarray(
        np.asarray(L0_attn_in_b, dtype=f).reshape(16, 128).T)
    convw = np.ascontiguousarray(
        np.asarray(L0_attn_conv_w, dtype=f).reshape(16, 128, 4)
        .transpose(1, 0, 2).reshape(128, 64))
    convb = np.ascontiguousarray(
        np.asarray(L0_attn_conv_b, dtype=f).reshape(16, 128).T)
    xpwT = np.ascontiguousarray(
        np.ascontiguousarray(L0_attn_xp_w, dtype=f).T.reshape(16, 128, 32))
    xpba = np.ascontiguousarray(np.asarray(L0_attn_xp_b, dtype=f)[:16].reshape(16, 1))
    xpbb = np.ascontiguousarray(np.asarray(L0_attn_xp_b, dtype=f)[16:].reshape(16, 1))
    outwT = np.ascontiguousarray(
        np.asarray(L0_attn_out_w, dtype=f).T.reshape(16, 8, 128).transpose(1, 0, 2))
    outb = np.ascontiguousarray(
        np.asarray(L0_attn_out_b, dtype=f).reshape(8, 128).T)
    wg = np.ascontiguousarray(L0_ffn_w_gate, dtype=f)
    wu = np.ascontiguousarray(L0_ffn_w_up, dtype=f)
    wd = np.ascontiguousarray(L0_ffn_w_down, dtype=f)
    cvec = np.ascontiguousarray(np.asarray(L0_c, dtype=f).reshape(8, 128).T)
    wmod = np.ascontiguousarray(L0_wmod, dtype=f)

    identf = np.eye(128, dtype=f)
    identb = np.eye(128, dtype=ml_dtypes.bfloat16)
    egu = np.zeros((32, 8 * 128), dtype=f)
    for kc in range(8):
        for kp in range(128):
            egu[4 * kc + kp // 32, 128 * kc + kp] = 1.0
    ed = np.zeros((96, 24 * 128), dtype=f)
    for kc in range(24):
        for kp in range(128):
            ed[4 * kc + kp // 32, 128 * kc + kp] = 1.0

    in_maps = []
    for r in range(NCORES):
        xs = np.empty((TH, D), dtype=f)
        t0 = r * T
        if r == 0:
            xs[:HALO] = 0.0
        else:
            xs[:HALO] = x[t0 - HALO:t0]
        xs[HALO:] = x[t0:t0 + T]
        rows_gu = np.empty((6, 128, D), dtype=f)
        lhsT_gu = np.empty((6, 8, 128, 128), dtype=f)
        for s in range(6):
            j = r + 8 * s
            if j < 24:
                W, m = wg, j
            else:
                W, m = wu, j - 24
            blk = W[128 * m:128 * m + 128, :]
            rows_gu[s] = blk
            lhsT_gu[s] = blk.reshape(128, 8, 128).transpose(1, 2, 0)
        md = (r + 48) - 48
        blkd = wd[128 * md:128 * md + 128, :]
        rows_d = np.ascontiguousarray(blkd)
        lhsT_d = np.ascontiguousarray(blkd.reshape(128, 24, 128).transpose(1, 2, 0))
        wms = wmod[:, 768 * r:768 * r + 768]
        wmod_t = np.ascontiguousarray(
            wms.reshape(8, 128, 6, 128).transpose(2, 0, 1, 3))
        onehot = np.zeros((NS, 8), dtype=f)
        if r > 0:
            onehot[:, r - 1] = 1.0
        hmask = np.full((128, 1), 0.0 if r == 0 else 1.0, dtype=f)
        in_maps.append({
            "x": np.ascontiguousarray(xs),
            "winT": winT, "inb": inb, "convw": convw, "convb": convb,
            "xpwT": xpwT, "xpba": xpba, "xpbb": xpbb, "outwT": outwT, "outb": outb,
            "lpq_rows_gu": np.ascontiguousarray(rows_gu),
            "lpq_lhsT_gu": np.ascontiguousarray(lhsT_gu),
            "lpq_rows_d": rows_d, "lpq_lhsT_d": lhsT_d,
            "cvec": cvec, "wmod": wmod_t,
            "identf": identf, "identb": identb, "egu": egu, "ed": ed,
            "onehot": onehot, "hmask": hmask,
        })

    return in_maps


def kernel(**inputs):
    in_maps = make_inmaps(**inputs)
    nc = _get_nc()
    res = run_bass_kernel_spmd(nc, in_maps, core_ids=list(range(NCORES)))
    out = np.concatenate([np.asarray(r["out"], dtype=np.float32)
                          for r in res.results], axis=0)
    return out.reshape(1, 16384, D)


# version 07
# speedup vs baseline: 1.0430x; 1.0430x over previous
import numpy as np
import ml_dtypes
import concourse.bass as bass
import concourse.mybir as mybir
from concourse.bass_utils import run_bass_kernel_spmd

F32 = mybir.dt.float32
BF16 = mybir.dt.bfloat16
Alu = mybir.AluOpType
Act = mybir.ActivationFunctionType
AX = mybir.AxisListType.X
MAGIC = float(np.float32(1.5 * 2.0 ** 23))
INV_QMAX = float(np.float32(1.0 / 32767.0))

NCORES = 8
T = 2048
HALO = 128
TH = T + HALO
D = 1024
DI = 2048
FF = 3072
NS = 16
EPS = 1e-5

SLOT_OFF = [0, 262144, 524288, 786432, 1048576, 1310720, 1572864]
RANK_BYTES = 2359296


def col_addr(j):
    return j % NCORES, SLOT_OFF[j // NCORES] // 2


DEBUG_SYNC = False


def build_program(debug_sync=False, trunc=None, dbg=False):
    nc = bass.Bass("TRN2", target_bir_lowering=False,
                   detect_race_conditions=debug_sync)

    def din(name, shape, dt=F32):
        return nc.dram_tensor(name, shape, dt, kind="ExternalInput").ap()

    x_h = din("x", [TH, D])
    winT_h = din("winT", [16, 128, 8, 128])
    inb_h = din("inb", [128, 16])
    convw_h = din("convw", [128, 64])
    convb_h = din("convb", [128, 16])
    xpwT_h = din("xpwT", [16, 128, 32])
    xpba_h = din("xpba", [16, 1])
    xpbb_h = din("xpbb", [16, 1])
    outwT_h = din("outwT", [8, 16, 128])
    outb_h = din("outb", [128, 8])
    lpq_rows_gu_h = din("lpq_rows_gu", [6, 128, D])
    lpq_lhsT_gu_h = din("lpq_lhsT_gu", [6, 128, 8, 128])
    lpq_rows_d_h = din("lpq_rows_d", [128, FF])
    lpq_lhsT_d_h = din("lpq_lhsT_d", [128, 24, 128])
    c_h = din("cvec", [128, 8])
    wmod_h = din("wmod", [6, 8, 128, 128])
    ident_f_h = din("identf", [128, 128])
    ident_b_h = din("identb", [128, 128], BF16)
    egu_h = din("egu", [32, 8 * 128])
    ed_h = din("ed", [96, 24 * 128])
    onehot_h = din("onehot", [NS, 8])
    hmask_h = din("hmask", [128, 1])
    ones1_h = din("ones1", [1, 128])

    out_h = nc.dram_tensor("out", [T, D], F32, kind="ExternalOutput").ap()
    dbg_h = nc.dram_tensor("dbg", [128, 16384], F32,
                           kind="ExternalOutput").ap() if dbg else None

    ag0_in = nc.dram_tensor("ag0_in", [1, 768], F32)
    ag0_out = nc.dram_tensor("ag0_out", [NCORES, 768], F32)
    agw_in = nc.dram_tensor("agw_in", [1, RANK_BYTES // 2], BF16)
    agw_out = nc.dram_tensor("agw_out", [NCORES, RANK_BYTES // 2], BF16)
    ag1_in = nc.dram_tensor("ag1_in", [1, 32], F32)
    ag1_out = nc.dram_tensor("ag1_out", [NCORES, 32], F32)
    ar_in = nc.dram_tensor("ar_in", [1, 2], F32)
    ar_out = nc.dram_tensor("ar_out", [1, 2], F32)
    sumd = nc.dram_tensor("sumd", [1, T], F32)
    mxd = nc.dram_tensor("mxd", [128, 1], F32)
    scld = nc.dram_tensor("scld", [1, 32], F32)
    xspill = nc.dram_tensor("xspill", [16, 128, D], F32)
    upspill = nc.dram_tensor("upspill", [24, 4, 128, 512], BF16)

    SBUF_SZ = nc.SBUF_PARTITION_SIZE_BYTES
    AR_S = SBUF_SZ - 34816 - 32
    AR_H = AR_S - 69632
    AR_X = AR_H - 65536

    def at(name, shape, dt, off):
        return nc.alloc_sbuf_tensor_at(name, shape, dt, offset=off)

    x_tm = [at(f"x{i}", [128, D], F32, AR_X + 4096 * i) for i in range(16)]
    gP_lo = [at(f"gPl{m}", [128, T], BF16, AR_X + 4096 * m) for m in range(16)]
    wrow = at("wrow", [128, FF], F32, AR_H + 0)
    wlhs = at("wlhs", [128, FF], F32, AR_H + 12288)
    wqcol = at("wqcol", [128, FF], BF16, AR_H + 24576)
    wmodb = at("wmodb", [128, 6 * 1024], F32, AR_H + 30720)
    ed_s = at("ed_s", [96, 24 * 128], F32, AR_H + 55296)
    smax = at("smax", [128, 96], F32, AR_H + 67584)
    sval = at("sval", [128, 96], F32, AR_H + 67968)
    sinv = at("sinv", [128, 96], F32, AR_H + 68352)
    h_bf = [at(f"h{m}", [128, TH], BF16, AR_H + 4352 * m) for m in range(16)]
    gP_hi = [at(f"gPh{m}", [128, T], BF16, AR_H + 4096 * m) for m in range(8)]
    hx = [at(f"hx{j}", [128, T], BF16, AR_H + 32768 + 4096 * j) for j in range(8)]
    xrr = [at(f"xrr{q}", [128, D], F32, AR_H + 32768 + 4096 * q) for q in range(4)]
    h1_bf = [at(f"h1_{j}", [128, TH], BF16, AR_S + 4352 * j) for j in range(8)]
    scanA = at("scanA", [32, T], F32, AR_S + 0)
    scanB = at("scanB", [16, T], F32, AR_S + 8192)
    scanC = at("scanC", [16, T], F32, AR_S + 16384)
    scanD = at("scanD", [16, T], F32, AR_S + 24576)
    wffcol = [at(f"wff{p}", [128, FF], BF16, AR_S + 6144 * p) for p in range(2)]
    uprng = [at(f"upr{p}", [128, 512], BF16, AR_S + 12288 + 1024 * p) for p in range(8)]
    rtring = [at(f"rt{p}", [128, 1024], F32, AR_S + 20480 + 4096 * p) for p in range(2)]
    ctile = [at(f"ct{p}", [128, 512], BF16, AR_S + 28672 + 1024 * p) for p in range(4)]

    def gP(m):
        return gP_lo[m] if m < 16 else gP_hi[m - 16]

    A = nc.alloc_sbuf_tensor
    win_bf = [A(f"winb{p}", [128, 1024], BF16) for p in range(2)]
    hc = A("hc", [128, T], F32)
    sq1 = A("sq1", [128, D], F32)
    xstage = A("xstage", [128, D], F32)
    xhat = [A(f"xhat{p}", [128, D], BF16) for p in range(2)]
    y_bf = A("ybf", [16, T], BF16)
    convw = A("convw_s", [128, 64], F32)
    convb = A("convb_s", [128, 16], F32)
    xpw_bf = A("xpwb", [128, 512], BF16)
    xpba_s = A("xpba_s", [16, 1], F32)
    xpbb_s = A("xpbb_s", [16, 1], F32)
    outw_bf = A("outwb", [16, 1024], BF16)
    outb_s = A("outb_s", [128, 8], F32)
    gaob = A("gaob", [128, 8], F32)
    ones1 = A("ones1_s", [1, 128], F32)
    adp = A("adp", [128, 64], F32)
    silc = A("silc", [128, 8], F32)
    inb_s = A("inb_s", [128, 16], F32)
    id_f = A("id_f", [128, 128], F32)
    id_b = A("id_b", [128, 128], BF16)
    egu_s = A("egu_s", [32, 8 * 128], F32)
    sT_s = A("sT_s", [96, 128], F32)
    invT_s = A("invT_s", [96, 128], F32)
    ones_c = A("ones_c", [128, 1], BF16)
    mxc = A("mxc", [128, 128], F32)
    mxc2 = A("mxc2", [128, 64], F32)
    mxrow = A("mxrow", [1, 128], F32)
    sumstg = A("sumstg", [1, 512], F32)
    stats = A("stats", [128, 16], F32)
    scl = A("scl", [1, 32], F32)
    sclb = A("sclb", [128, 32], F32)
    onehot_s = A("onehot_s", [NS, 8], F32)
    sin_s = A("sin_s", [NS, 1], F32)
    cgat = A("cgat", [NS, 8], F32)
    cgee = A("cgee", [NS, 8], F32)
    csc = A("csc", [NS, 8], F32)
    hmask_s = A("hmask_s", [128, 1], F32)
    modloc = A("modloc", [128, 6], F32)



    ps_mm = [nc.alloc_psum_tensor(f"psmm{i}", [128, 512], F32) for i in range(4)]
    ps_tr = nc.alloc_psum_tensor("pstr", [128, 1024], BF16)
    ps_tr2 = nc.alloc_psum_tensor("pstr2", [128, 1024], BF16)
    ps_a = nc.alloc_psum_tensor("psa", [128, 512], F32)
    ps_b = nc.alloc_psum_tensor("psb", [128, 512], F32)

    sem_h = {n: nc.alloc_semaphore(n) for n in ["sT", "sV", "sS", "sG"]}
    cnt = {n: 0 for n in sem_h}
    ENG_SEM = {"vector": "sV", "scalar": "sS", "tensor": "sT", "gpsimd": "sG"}
    ops = []
    watermark = {e: {} for e in ["sync", "gpsimd", "vector", "scalar", "tensor"]}

    def sem_for(key):
        if key not in sem_h:
            sem_h[key] = nc.alloc_semaphore(key)
            cnt[key] = 0
        return sem_h[key]

    class Tok:
        __slots__ = ("eng", "sem", "val", "tiny")

        def __init__(self, eng, sem, val, tiny=False):
            self.eng, self.sem, self.val, self.tiny = eng, sem, val, tiny

    def _flat(deps):
        for d in deps:
            if d is None:
                continue
            if isinstance(d, (list, tuple)):
                yield from _flat(d)
            else:
                yield d

    def emit(eng, fn, deps=(), pub=False, dkey=None, cc=None, tiny=False):
        need = {}
        for d in _flat(deps):
            if (d.sem in ("sT", "sV", "sS", "sG") and d.eng == eng
                    and not debug_sync and not d.tiny):
                continue
            if need.get(d.sem, -1) < d.val:
                need[d.sem] = d.val
        waits = []
        wm = watermark[eng]
        for s, v in need.items():
            if wm.get(s, -1) >= v:
                continue
            wm[s] = v
            waits.append((sem_h[s], v))
        incs = []
        tok = None
        if cc is not None:
            sem_for(cc)
            cnt[cc] += 1
            incs.append((sem_h[cc], 1))
            tok = Tok(eng, cc, cnt[cc])
        elif dkey is not None:
            sem_for(dkey)
            cnt[dkey] += 16
            incs.append((sem_h[dkey], 16))
            tok = Tok(eng, dkey, cnt[dkey])
        elif pub or (debug_sync and eng in ENG_SEM):
            s = ENG_SEM[eng]
            cnt[s] += 1
            incs.append((sem_h[s], 1))
            tok = Tok(eng, s, cnt[s], tiny)
        ops.append({"eng": eng, "fn": fn, "waits": waits, "incs": incs})
        return tok

    last_dma = {}
    dbg_off = [0]
    dbg_names = []

    def dbg_cap(name, ap, deps, cols, parts=128):
        if not dbg:
            return
        o = dbg_off[0]
        dbg_names.append((name, o, cols, parts))
        dbg_off[0] += cols
        dma("gpsimd", "dbgk", dbg_h[0:parts, o:o + cols], ap, deps=list(deps),
            slow=True)
    build_program.dbg_names = dbg_names

    def dma(q, key, out, in_, deps=(), slow=False, chain=True):
        deps = list(deps)
        if chain and key in last_dma:
            deps.append(last_dma[key])
        t = emit(q, lambda e, o=out, i=in_, s=slow: e.dma_start(
            out=o, in_=i, allow_slow_non_contiguous=s), deps=deps, dkey=key)
        last_dma[key] = t
        return t

    def V(fn, deps=(), pub=False, tiny=False):
        return emit("vector", fn, deps, pub=pub, tiny=tiny)

    def S(fn, deps=(), pub=False, tiny=False):
        return emit("scalar", fn, deps, pub=pub, tiny=tiny)

    def TT(fn, deps=(), pub=False):
        return emit("tensor", fn, deps, pub=pub)

    l_convw = dma("sync", "ini0", convw[:, :], convw_h)
    l_convb = dma("sync", "ini1", convb[:, :], convb_h)
    l_inb = dma("sync", "ini2", inb_s[:, :], inb_h)
    l_xpb = dma("sync", "ini3", xpba_s[:, :], xpba_h)
    l_xpb2 = dma("sync", "ini1", xpbb_s[:, :], xpbb_h)
    l_idf = dma("sync", "ini0", id_f[:, :], ident_f_h)
    l_idb = dma("sync", "ini1", id_b[:, :], ident_b_h)
    l_egu = dma("sync", "ini2", egu_s[:, :], egu_h)
    l_ed = dma("sync", "ini3", ed_s[:, :], ed_h)
    l_outb = dma("sync", "ini0", outb_s[:, :], outb_h)
    l_oh = dma("sync", "ini1", onehot_s[:, :], onehot_h)
    l_hm = dma("sync", "ini2", hmask_s[:, :], hmask_h)
    l_ones1 = dma("sync", "ini3", ones1[:, :], ones1_h)
    l_c = dma("sync", "ini3", silc[:, :], c_h)
    l_wmod = dma("sync", "ini0", wmodb[:, :].rearrange("p (m k q) -> p m k q", m=6, k=8),
                  wmod_h.rearrange("m k p q -> p m k q"))
    l_xpw = dma("gpsimd", "gini0", xpw_bf[:, :].rearrange("p (k t) -> p k t", t=32),
                 xpwT_h.rearrange("k p t -> p k t"))
    l_outw = dma("gpsimd", "gini1", outw_bf[:, :].rearrange("p (m t) -> p m t", t=128),
                  outwT_h.rearrange("m p t -> p m t"))
    t_ones = V(lambda e: e.memset(ones_c[:, :], 1.0), pub=True)
    t_scl0 = V(lambda e: e.memset(scl[:, :], 0.0), pub=True)

    t_silc = S(lambda e: e.activation(silc[:, :], silc[:, :], Act.Silu),
               deps=[l_c], pub=True, tiny=True)

    mod_mm = []
    for m in range(6):
        for k in range(8):
            tk = TT(lambda e, m=m, k=k: e.matmul(
                ps_b[:, m:m + 1],
                wmodb[:, (m * 8 + k) * 128:(m * 8 + k) * 128 + 128],
                silc[:, k:k + 1], start=(k == 0), stop=(k == 7)),
                deps=[l_wmod, t_silc], pub=(k == 7))
        mod_mm.append(tk)
    mod_ev = []
    for m in range(6):
        mod_ev.append(V(lambda e, m=m: e.tensor_copy(
            modloc[:, m:m + 1], ps_b[:, m:m + 1]), deps=mod_mm, pub=True, tiny=True))
    t_modw = None
    for m in range(6):
        t_modw = dma("gpsimd", "ag0w", ag0_in.ap()[0:1, 128 * m:128 * m + 128]
                     .rearrange("one p -> p one"), modloc[:, m:m + 1],
                     deps=[mod_ev[m]])
    t_ag0 = emit("gpsimd", lambda e: e.collective_compute(
        "AllGather", Alu.bypass, replica_groups=[list(range(NCORES))],
        ins=[ag0_in.ap().opt()], outs=[ag0_out.ap().opt()]),
        deps=[t_modw], cc="cc0")
    flat0 = ag0_out.ap().rearrange("r v -> () (r v)")
    t_adp = dma("gpsimd", "adpu", adp[:, 0:48],
                flat0[0:1, 0:6144].rearrange("one (j p) -> p (one j)", p=128),
                deps=[t_ag0], slow=True)
    t_ad = [t_adp]
    t_ca1 = V(lambda e: e.tensor_scalar_add(adp[:, 8:16], adp[:, 8:16], 1.0),
              deps=t_ad, pub=True, tiny=True)
    t_cb1 = V(lambda e: e.tensor_scalar_add(adp[:, 32:40], adp[:, 32:40], 1.0),
              deps=t_ad, pub=True, tiny=True)
    gq_toks = []
    prev_gq = None
    for src0 in (16, 40):
        tsg = S(lambda e, s=src0: e.activation(adp[:, 48:56], adp[:, s:s + 8],
                                               Act.Sign),
                deps=t_ad + ([prev_gq] if prev_gq else []), pub=True, tiny=True)
        tab = S(lambda e, s=src0: e.activation(adp[:, 56:64], adp[:, s:s + 8],
                                               Act.Abs), deps=t_ad, pub=True, tiny=True)
        tmx = V(lambda e: e.tensor_scalar(adp[:, 0:8].bitcast(F32) if False else
                                          stats[:, 8:16], adp[:, 56:64], 0.25,
                                          None, Alu.max),
                deps=[tab], pub=True, tiny=True)
        prev_gq = V(lambda e, s=src0: e.tensor_tensor(
            adp[:, s:s + 8], adp[:, 48:56], stats[:, 8:16], Alu.mult),
            deps=[tsg, tmx], pub=True, tiny=True)

        gq_toks.append(prev_gq)
    t_gaq, t_gbq = gq_toks

    dbg_cap("adp", adp[:, :], [t_gaq, t_gbq, t_ca1, t_cb1], 64)
    tr_guard = [None, None]

    def tr_slot(si):
        return (ps_tr if si % 2 == 0 else ps_tr2)[
            :, 128 * ((si // 2) % 4):128 * ((si // 2) % 4) + 128]

    h1_toks = []
    x_toks = [None] * 16
    lnprev = None
    tr_ctr = [0]
    xhat_guard = [None, None]

    def ln_tile(i, dst, ap_dst_tok, affine_c1, affine_c0, out_chunk_fn, idtile,
                extra_deps=()):
        nonlocal lnprev
        t_sum = V(lambda e: e.tensor_reduce(stats[:, 0:1], dst[:, :], AX, Alu.add),
                  deps=[ap_dst_tok, lnprev] + list(extra_deps), pub=True, tiny=True)
        t_sq0 = S(lambda e: e.activation(sq1[:, :], dst[:, :], Act.Square),
                  deps=[ap_dst_tok, lnprev] + list(extra_deps), pub=True)
        t_sq = V(lambda e: e.tensor_reduce(stats[:, 1:2], sq1[:, :], AX, Alu.add),
                 deps=[t_sq0], pub=True, tiny=True)
        t_mn = V(lambda e: e.tensor_scalar_mul(stats[:, 2:3], stats[:, 0:1],
                                               1.0 / D), deps=[t_sum], pub=True, tiny=True)
        t_ms = V(lambda e: e.tensor_scalar_mul(stats[:, 3:4], stats[:, 1:2],
                                               1.0 / D), deps=[t_sq], pub=True, tiny=True)
        t_m2 = V(lambda e: e.tensor_tensor(stats[:, 4:5], stats[:, 2:3],
                                           stats[:, 2:3], Alu.mult),
                 deps=[t_mn], pub=True, tiny=True)
        t_vr = V(lambda e: e.tensor_tensor(stats[:, 5:6], stats[:, 3:4],
                                           stats[:, 4:5], Alu.subtract),
                 deps=[t_ms, t_m2], pub=True, tiny=True)
        t_ve = V(lambda e: e.tensor_scalar_add(stats[:, 6:7], stats[:, 5:6], EPS),
                 deps=[t_vr], pub=True, tiny=True)
        t_rc = V(lambda e: e.reciprocal(stats[:, 7:8], stats[:, 6:7]),
                 deps=[t_ve], pub=True, tiny=True)
        t_rs = S(lambda e: e.activation(stats[:, 8:9], stats[:, 7:8], Act.Sqrt),
                 deps=[t_rc], pub=True, tiny=True)
        if i == 2:
            dbg_cap("stats2", stats[:, :], [t_rs, t_mn, t_m2], 16)
            dbg_cap("xin2", dst[:, :], [t_rs], 1024)
        xh = xhat[i % 2]
        t_xh = V(lambda e: e.tensor_scalar(xh[:, :], dst[:, :], stats[:, 2:3],
                                           stats[:, 8:9], Alu.subtract, Alu.mult),
                 deps=[t_rs, xhat_guard[i % 2]] + list(extra_deps), pub=True)
        if i == 2:
            dbg_cap("xhat2", xh[:, :], [t_xh], 1024)
        lnprev = t_xh
        evs = []
        t_tr = None
        for j in range(8):
            si = tr_ctr[0] % 8
            tr_ctr[0] += 1
            t_tr = TT(lambda e, j=j, si=si: e.transpose(
                tr_slot(si), xh[:, 128 * j:128 * j + 128], id_b[:, :]),
                deps=[t_xh, l_idb, tr_guard[si % 2]], pub=True)
            ev = V(lambda e, j=j, si=si: e.tensor_scalar(
                out_chunk_fn(j), tr_slot(si), adp[:, affine_c1 + j:affine_c1 + j + 1],
                adp[:, affine_c0 + j:affine_c0 + j + 1], Alu.mult, Alu.add),
                deps=[t_tr, t_ca1, t_cb1], pub=True, tiny=True)
            tr_guard[si % 2] = ev
            evs.append(ev)
        xhat_guard[i % 2] = t_tr
        return evs

    agw_writes = []
    lp_guard = [None]
    lp_rm_guard = [None]

    def lpbq_slot(s):
        I = D if s < 6 else FF
        B = I // 32
        if s < 6:
            t_row = dma("sync", "lpr", wrow[:, 0:I], lpq_rows_gu_h[s],
                        deps=[lp_rm_guard[0]])
            t_lh = dma("sync", "lpl",
                       wlhs[:, 0:I].rearrange("p (k q) -> p k q", q=128),
                       lpq_lhsT_gu_h[s], deps=[lp_guard[0]])
        else:
            t_row = dma("sync", "lpr", wrow[:, 0:I], lpq_rows_d_h,
                        deps=[lp_rm_guard[0]])
            t_lh = dma("sync", "lpl",
                       wlhs[:, 0:I].rearrange("p (k q) -> p k q", q=128),
                       lpq_lhsT_d_h, deps=[lp_guard[0]])
        t_rm = V(lambda e, I=I, B=B: e.tensor_reduce(
            smax[:, 0:B].rearrange("p b -> p b ()"),
            wrow[:, 0:I].rearrange("p (b t) -> p b t", t=32),
            AX, Alu.max, apply_absolute_value=True), deps=[t_row], pub=True, tiny=True)
        lp_rm_guard[0] = t_rm
        t_s = V(lambda e, B=B: e.tensor_scalar(
            sval[:, 0:B], smax[:, 0:B], 1.0 / 7.0, 1e-12, Alu.mult, Alu.max),
            deps=[t_rm], pub=True, tiny=True)
        t_i = V(lambda e, B=B: e.reciprocal(sinv[:, 0:B], sval[:, 0:B]),
                deps=[t_s], pub=True, tiny=True)
        t_t1 = TT(lambda e, B=B: e.transpose(ps_mm[2][0:B, 0:128], sval[:, 0:B],
                                             id_f[:, :]),
                  deps=[t_s, l_idf, lp_guard[0]], pub=True)
        t_t2 = TT(lambda e, B=B: e.transpose(ps_mm[3][0:B, 0:128], sinv[:, 0:B],
                                             id_f[:, :]),
                  deps=[t_i, l_idf, lp_guard[0]], pub=True)
        B4 = B
        ev1 = V(lambda e, B=B: e.tensor_copy(sT_s[0:B, :], ps_mm[2][0:B, 0:128]),
                deps=[t_t1], pub=True, tiny=True)
        ev2 = V(lambda e, B=B: e.tensor_copy(invT_s[0:B, :], ps_mm[3][0:B, 0:128]),
                deps=[t_t2], pub=True, tiny=True)
        E_s = egu_s if s < 6 else ed_s
        l_E = l_egu if s < 6 else l_ed
        wq_toks = []
        piece_guard = [agw_writes[-1]] if agw_writes else []
        for p in range(I // 512):
            exps = []
            for q in range(4):
                kc = 4 * p + q
                exps.append(TT(lambda e, q=q, kc=kc, B=B, E_s=E_s: e.matmul(
                    ps_a[:, 128 * q:128 * q + 128],
                    E_s[0:B, 128 * kc:128 * kc + 128], sT_s[0:B, :],
                    start=True, stop=True),
                    deps=[ev1, ev2, l_E] + mod_ev + piece_guard, pub=(q == 3)))
                exps.append(TT(lambda e, q=q, kc=kc, B=B, E_s=E_s: e.matmul(
                    ps_b[:, 128 * q:128 * q + 128],
                    E_s[0:B, 128 * kc:128 * kc + 128], invT_s[0:B, :],
                    start=True, stop=True),
                    deps=[ev1, ev2, l_E] + mod_ev + piece_guard, pub=(q == 3)))
            c0 = 512 * p
            t_u = V(lambda e, c0=c0: e.tensor_tensor(
                hc[:, 0:512], wlhs[:, c0:c0 + 512], ps_b[:, 0:512], Alu.mult),
                deps=[t_lh, exps[-1], exps[-2]], pub=True)
            t_t = V(lambda e: e.tensor_scalar_add(hc[:, 512:1024], hc[:, 0:512],
                                                  MAGIC), deps=[t_u], pub=True)
            wq_toks.append(V(lambda e, c0=c0: e.scalar_tensor_tensor(
                wqcol[:, c0:c0 + 512], hc[:, 512:1024], MAGIC, ps_a[:, 0:512],
                Alu.subtract, Alu.mult), deps=[t_t], pub=True))
            piece_guard = [t_u, wq_toks[-1]]
        lp_guard[0] = wq_toks[-1]
        eo = SLOT_OFF[s] // 2
        agw_writes.append(dma(
            "gpsimd", "agww",
            agw_in.ap()[0:1, eo:eo + 128 * I].rearrange(
                "one (p q) -> (one p) q", p=128),
            wqcol[:, 0:I], deps=wq_toks))

    for i in range(17):
        dst = xstage if i == 0 else x_tm[i - 1]
        t_x = dma("sync", f"xl{i % 4}", dst[:, :], x_h[128 * i:128 * i + 128, :])
        if i >= 1:
            x_toks[i - 1] = t_x
        evs = ln_tile(i, dst, t_x, 8, 0,
                      lambda j, i=i: h1_bf[j][:, 128 * i:128 * i + 128], id_b)
        h1_toks.extend(evs)
        if i < 7:
            lpbq_slot(i)
    dbg_cap("h1_0", h1_bf[0][:, 0:512], h1_toks, 512)
    t_agw = emit("gpsimd", lambda e: e.collective_compute(
        "AllGather", Alu.bypass, replica_groups=[list(range(NCORES))],
        ins=[agw_in.ap().opt()], outs=[agw_out.ap().opt()]),
        deps=agw_writes, cc="ccw")

    NTILES_IN = [(0, 128), (128, 512), (640, 512), (1152, 512), (1664, 512)]
    win_tok = [None, None]
    win_free = [None, None]
    bank_free = [None] * 4
    hprime = []
    for m in range(16):
        pb = m % 2
        win_tok[pb] = dma("gpsimd", f"win{pb}",
                          win_bf[pb][:, :].rearrange("p (k q) -> p k q", q=128),
                          winT_h[m], deps=[win_free[pb]])
        ev_toks = []
        for ni, (c0, w) in enumerate(NTILES_IN):
            bk = (m * 5 + ni) % 4
            last_tile = (c0 + w - 1) // 128
            h1_dep = h1_toks[:8 * (last_tile + 1)]
            mmt = None
            for k in range(8):
                mmt = TT(lambda e, pb=pb, bk=bk, k=k, c0=c0, w=w: e.matmul(
                    ps_mm[bk][:, 0:w], win_bf[pb][:, 128 * k:128 * k + 128],
                    h1_bf[k][:, c0:c0 + w], start=(k == 0), stop=(k == 7)),
                    deps=[win_tok[pb], bank_free[bk]] + (h1_dep if k == 0 else []),
                    pub=(k == 7))
            ev = S(lambda e, m=m, bk=bk, c0=c0, w=w: e.activation(
                h_bf[m][:, c0:c0 + w], ps_mm[bk][:, 0:w], Act.Identity,
                bias=inb_s[:, m:m + 1]),
                deps=[mmt, l_inb] + (agw_writes if m <= 7 else []), pub=True,
                tiny=True)
            bank_free[bk] = ev
            ev_toks.append(ev)
        win_free[pb] = ev_toks[-1]
        t_msk = V(lambda e, m=m: e.tensor_scalar(
            h_bf[m][:, HALO - 3:HALO], h_bf[m][:, HALO - 3:HALO],
            hmask_s[:, 0:1], None, Alu.mult), deps=ev_toks + [l_hm], pub=True)
        t_c = V(lambda e, m=m: e.tensor_scalar(
            hc[:, 0:T], h_bf[m][:, HALO - 3:HALO - 3 + T], convw[:, 4 * m:4 * m + 1],
            None, Alu.mult),
            deps=[t_msk, l_convw, hprime[-1] if hprime else None], pub=True)
        for j in (1, 2, 3):
            t_c = V(lambda e, m=m, j=j: e.scalar_tensor_tensor(
                hc[:, 0:T], h_bf[m][:, HALO - 3 + j:HALO - 3 + j + T],
                convw[:, 4 * m + j:4 * m + j + 1], hc[:, 0:T], Alu.mult, Alu.add),
                deps=[t_c], pub=True)
        hprime.append(S(lambda e, m=m: e.activation(
            h_bf[m][:, HALO:HALO + T], hc[:, 0:T], Act.Silu,
            bias=convb[:, m:m + 1]), deps=[t_c, l_convb], pub=True))

    dbg_cap("hprime0", h_bf[0][:, HALO:HALO + 512], hprime, 512)
    xp_toks = []
    for n in range(4):
        mmta = None
        mmtb = None
        for k in range(16):
            mmta = TT(lambda e, n=n, k=k: e.matmul(
                ps_a[0:16, 0:512], xpw_bf[:, 32 * k:32 * k + 16],
                h_bf[k][:, HALO + 512 * n:HALO + 512 * n + 512],
                start=(k == 0), stop=(k == 15)),
                deps=[l_xpw, hprime[k]] + (xp_toks[-2:] if xp_toks else []),
                pub=(k == 15))
            mmtb = TT(lambda e, n=n, k=k: e.matmul(
                ps_mm[0][0:16, 0:512], xpw_bf[:, 32 * k + 16:32 * k + 32],
                h_bf[k][:, HALO + 512 * n:HALO + 512 * n + 512],
                start=(k == 0), stop=(k == 15)),
                deps=[l_xpw, hprime[k], bank_free[0]]
                + (xp_toks[-2:] if xp_toks else []),
                pub=(k == 15))
        xp_toks.append(V(lambda e, n=n: e.tensor_scalar(
            scanA[0:16, 512 * n:512 * n + 512], ps_a[0:16, 0:512],
            xpba_s[0:16, 0:1], None, Alu.add), deps=[mmta, l_xpb], pub=True))
        xp_toks.append(V(lambda e, n=n: e.tensor_scalar(
            scanC[0:16, 512 * n:512 * n + 512], ps_mm[0][0:16, 0:512],
            xpbb_s[0:16, 0:1], None, Alu.add), deps=[mmtb, l_xpb2], pub=True))
    sum_toks = []
    for n in range(4):
        mmt = None
        for k in range(16):
            mmt = TT(lambda e, n=n, k=k: e.matmul(
                ps_b[0:1, 0:512], ones_c[:, :],
                h_bf[k][:, HALO + 512 * n:HALO + 512 * n + 512],
                start=(k == 0), stop=(k == 15)),
                deps=[t_ones, hprime[k]] + (sum_toks[-1:] if sum_toks else []),
                pub=(k == 15))
        t_se = V(lambda e: e.tensor_copy(sumstg[:, :], ps_b[0:1, 0:512]),
                 deps=[mmt, sum_toks[-1] if sum_toks else None], pub=True, tiny=True)
        sum_toks.append(dma("sync", "sumw", sumd.ap()[0:1, 512 * n:512 * n + 512],
                            sumstg[:, :], deps=[t_se]))
    t_e = S(lambda e: e.activation(scanB[0:16, :], scanA[0:16, :], Act.Exp),
            deps=xp_toks + sum_toks, pub=True)
    t_a = S(lambda e: e.activation(scanB[0:16, :], scanB[0:16, :], Act.Exp,
                                   scale=-1.0), deps=[t_e], pub=True)
    t_sl = S(lambda e: e.activation(scanC[:, :], scanC[:, :], Act.Silu),
             deps=xp_toks, pub=True)
    t_sb = dma("gpsimd", "sumb", scanD[:, :], sumd.ap().partition_broadcast(NS),
               deps=sum_toks)
    t_b = V(lambda e: e.tensor_tensor(scanC[:, :], scanC[:, :], scanD[:, :],
                                      Alu.mult), deps=[t_sl, t_sb], pub=True)
    t_y0 = V(lambda e: e.tensor_tensor_scan(scanD[:, :], scanB[0:16, :],
                                            scanC[:, :], 0.0, Alu.mult, Alu.add),
             deps=[t_a, t_b], pub=True)
    t_P = V(lambda e: e.tensor_tensor_scan(scanA[0:16, :], scanB[0:16, :],
                                           scanB[0:16, :], 1.0, Alu.mult,
                                           Alu.bypass), deps=[t_a], pub=True)
    t_c1 = dma("gpsimd", "cry", ag1_in.ap()[0:1, 0:16].rearrange("one n -> n one"),
               scanA[0:16, T - 1:T], deps=[t_P])
    t_c2 = dma("gpsimd", "cry", ag1_in.ap()[0:1, 16:32].rearrange("one n -> n one"),
               scanD[:, T - 1:T], deps=[t_y0])
    t_ag1 = emit("gpsimd", lambda e: e.collective_compute(
        "AllGather", Alu.bypass, replica_groups=[list(range(NCORES))],
        ins=[ag1_in.ap().opt()], outs=[ag1_out.ap().opt()]),
        deps=[t_c1, t_c2], cc="cc1")
    t_g1 = dma("gpsimd", "cryo", cgat[:, :], ag1_out.ap()[:, 0:16].rearrange("r n -> n r"),
               deps=[t_ag1], slow=True)
    t_g2 = dma("gpsimd", "cryo", cgee[:, :], ag1_out.ap()[:, 16:32].rearrange("r n -> n r"),
               deps=[t_ag1], slow=True)
    t_cs = V(lambda e: e.tensor_tensor_scan(csc[:, :], cgat[:, :], cgee[:, :],
                                            0.0, Alu.mult, Alu.add),
             deps=[t_g1, t_g2], pub=True, tiny=True)
    t_sel = V(lambda e: e.tensor_tensor(csc[:, :], csc[:, :], onehot_s[:, :],
                                        Alu.mult), deps=[t_cs, l_oh], pub=True, tiny=True)
    t_sin = V(lambda e: e.tensor_reduce(sin_s[:, 0:1], csc[:, :], AX, Alu.add),
              deps=[t_sel], pub=True, tiny=True)
    t_y = V(lambda e: e.tensor_tensor_scan(y_bf[0:16, :], scanB[0:16, :],
                                           scanC[:, :], sin_s[0:16, 0:1],
                                           Alu.mult, Alu.add),
            deps=[t_sin], pub=True)

    dbg_cap("a_", scanB[0:16, 0:512], [t_a], 512, parts=16)
    dbg_cap("b_", scanC[0:16, 0:512], [t_b, t_y0], 512, parts=16)
    dbg_cap("sin", sin_s[:, 0:1], [t_sin], 1, parts=16)
    dbg_cap("y", y_bf[0:16, 0:512], [t_y], 512, parts=16)
    t_gaob2 = V(lambda e: e.tensor_tensor(gaob[:, :], adp[:, 16:24],
                                          outb_s[:, :], Alu.mult),
                deps=[t_gaq, l_outb], pub=True, tiny=True)
    res_toks = []
    ctile_free = [None] * 4
    for n in range(4):
        for mo in range(8):
            bk = (n * 8 + mo) % 4
            mmt = TT(lambda e, bk=bk, mo=mo, n=n: e.matmul(
                ps_mm[bk][:, 0:512], outw_bf[0:16, 128 * mo:128 * mo + 128],
                y_bf[0:16, 512 * n:512 * n + 512], start=True, stop=True),
                deps=[l_outw, t_y, bank_free[bk]], pub=True)
            sl = (n * 8 + mo) % 4
            ev = S(lambda e, bk=bk, mo=mo, sl=sl: e.activation(
                ctile[sl][:, :], ps_mm[bk][:, 0:512], Act.Identity,
                scale=adp[:, 16 + mo:17 + mo], bias=gaob[:, mo:mo + 1]),
                deps=[mmt, t_gaob2, ctile_free[sl], t_c2, t_g1, t_g2],
                pub=True, tiny=True)
            bank_free[bk] = ev
            for q in range(4):
                si = tr_ctr[0] % 8
                tr_ctr[0] += 1
                t_tr = TT(lambda e, sl=sl, q=q, si=si: e.transpose(
                    tr_slot(si), ctile[sl][:, 128 * q:128 * q + 128], id_b[:, :]),
                    deps=[ev, tr_guard[si % 2]], pub=True)
                ra = V(lambda e, n=n, q=q, mo=mo, si=si: e.tensor_tensor(
                    x_tm[4 * n + q][:, 128 * mo:128 * mo + 128],
                    x_tm[4 * n + q][:, 128 * mo:128 * mo + 128],
                    tr_slot(si), Alu.add),
                    deps=[t_tr, x_toks[4 * n + q]], pub=True, tiny=True)
                tr_guard[si % 2] = ra
                res_toks.append(ra)
                ctile_free[sl] = t_tr

    dbg_cap("x2_0", x_tm[0][:, 0:512], res_toks, 512)
    h2_toks = []
    xsp_toks = []
    for i in range(16):
        evs = ln_tile(100 + i, x_tm[i], None, 32, 24,
                      lambda j, i=i: hx[j][:, 128 * i:128 * i + 128], id_b,
                      extra_deps=res_toks)
        h2_toks.extend(evs)
        xsp_toks.append(dma("sync", f"xsp{i % 4}", xspill[i], x_tm[i][:, :],
                            deps=evs))
    mx_h2 = []
    for j in range(8):
        mx_h2.append(V(lambda e, j=j: e.tensor_reduce(
            mxc[:, j:j + 1], hx[j][:, :], AX, Alu.max,
            apply_absolute_value=True), deps=h2_toks, pub=True, tiny=True))

    def global_max(src_col_range, dst_slot, deps):
        a, b = src_col_range
        return emit("gpsimd", lambda e, d=dst_slot: e.tensor_reduce(
            scl[0:1, d:d + 1], mxc[:, a:b], mybir.AxisListType.XYZWC, Alu.max),
            deps=deps, pub=True, tiny=True)

    def allreduce_pair(src_slot, deps):
        t1 = dma("gpsimd", "arb", ar_in.ap(), scl[0:1, src_slot:src_slot + 2],
                 deps=deps)
        arid = ar_ctr[0]
        ar_ctr[0] += 1
        t2 = emit("gpsimd", lambda e: e.collective_compute(
            "AllReduce", Alu.max, replica_groups=[list(range(NCORES))],
            ins=[ar_in.ap().opt()], outs=[ar_out.ap().opt()]),
            deps=[t1], cc=f"ccr{arid}")
        return dma("gpsimd", "arb", scl[0:1, src_slot:src_slot + 2], ar_out.ap(),
                   deps=[t2])

    ar_ctr = [0]
    bc_guard = [None]

    def bcast_scl(deps):
        tm = TT(lambda e: e.matmul(ps_b[:, 0:32], ones1[:, :], scl[0:1, :],
                                   start=True, stop=True),
                deps=list(deps) + [l_ones1, bc_guard[0], sum_toks[-1] if sum_toks else None],
                pub=True)
        tv = V(lambda e: e.tensor_copy(sclb[:, :], ps_b[:, 0:32]),
               deps=[tm], pub=True, tiny=True)
        bc_guard[0] = tv
        return tv

    dbg_cap("h2_0", hx[0][:, 0:512], h2_toks, 512)
    t_mx1 = global_max((0, 8), 0, mx_h2 + [t_scl0])
    t_ar1 = allreduce_pair(0, [t_mx1])
    t_s1 = V(lambda e: e.tensor_scalar(scl[0:1, 1:2], scl[0:1, 0:1], INV_QMAX,
                                       1e-12, Alu.mult, Alu.max),
             deps=[t_ar1], pub=True, tiny=True)
    t_i1 = V(lambda e: e.reciprocal(scl[0:1, 2:3], scl[0:1, 1:2]),
             deps=[t_s1], pub=True, tiny=True)
    t_bc1 = bcast_scl([t_i1])

    dbg_cap("scl1", scl[:, :], [t_i1, t_bc1], 32, parts=1)
    dbg_cap("sclb1", sclb[:, :], [t_bc1], 32)
    xq_toks = []
    for j in range(8):
        for hhalf in range(2):
            rs = rtring[hhalf]
            c0 = 1024 * hhalf
            t_r = V(lambda e, j=j, c0=c0, rs=rs: e.tensor_scalar(
                rs[:, :], hx[j][:, c0:c0 + 1024], sclb[:, 2:3], MAGIC,
                Alu.mult, Alu.add), deps=[t_bc1] + h2_toks, pub=True)
            xq_toks.append(V(lambda e, j=j, c0=c0, rs=rs: e.tensor_scalar(
                hx[j][:, c0:c0 + 1024], rs[:, :], MAGIC, None, Alu.subtract),
                deps=[t_r], pub=True))

    scan_dead = [t_y, t_c1, t_c2, hprime[-1], xp_toks[-1], sum_toks[-1]]

    def ffn_col_load(j, pb, deps):
        r, eo = col_addr(j)
        I = D if j < 48 else FF
        return dma("sync", f"wff{pb}", wffcol[pb][:, 0:I],
                   agw_out.ap()[r:r + 1, eo:eo + 128 * I].rearrange(
                       "one (p q) -> (one p) q", p=128),
                   deps=deps + [t_agw] + scan_dead)

    wff_tok = [None, None]
    wff_free = [None, None]
    dbg_cap("xq_0", hx[0][:, 0:512], xq_toks, 512)
    gate_ev = []
    for m in range(24):
        pb = m % 2
        wff_tok[pb] = ffn_col_load(m, pb, [wff_free[pb]])
        for n in range(4):
            bk = (m * 4 + n) % 4
            mmt = None
            for k in range(8):
                mmt = TT(lambda e, pb=pb, bk=bk, k=k, n=n: e.matmul(
                    ps_mm[bk][:, :], wffcol[pb][:, 128 * k:128 * k + 128],
                    hx[k][:, 512 * n:512 * n + 512],
                    start=(k == 0), stop=(k == 7)),
                    deps=[wff_tok[pb], bank_free[bk],
                          xq_toks[2 * k + (n // 2)]],
                    pub=(k == 7))
            ev = S(lambda e, m=m, n=n, bk=bk: e.activation(
                gP(m)[:, 512 * n:512 * n + 512], ps_mm[bk][:, :], Act.Copy,
                scale=sclb[:, 1:2]),
                deps=[mmt, t_bc1] + ([xsp_toks[m]] if m < 16 else []), pub=True,
                tiny=True)
            bank_free[bk] = ev
            gate_ev.append(ev)
            gate_ev.append(V(lambda e, m=m, n=n: e.tensor_reduce(
                mxc[:, 4 * m + n:4 * m + n + 1], gP(m)[:, 512 * n:512 * n + 512],
                AX, Alu.max, apply_absolute_value=True), deps=[ev], pub=True, tiny=True))
        wff_free[pb] = gate_ev[-2]
    dbg_cap("gate_0", gP(0)[:, 0:512], gate_ev, 512)
    t_mxg = global_max((0, 96), 3, gate_ev)
    t_ar2 = allreduce_pair(3, [t_mxg])
    t_sg = V(lambda e: e.tensor_scalar(scl[0:1, 4:5], scl[0:1, 3:4], INV_QMAX,
                                       1e-12, Alu.mult, Alu.max),
             deps=[t_ar2], pub=True, tiny=True)
    t_ig = V(lambda e: e.reciprocal(scl[0:1, 5:6], scl[0:1, 4:5]),
             deps=[t_sg], pub=True, tiny=True)
    t_bc2 = bcast_scl([t_ig])

    up_ev = []
    up_spill_toks = {}
    upr_free = [None] * 8
    up_last_mxe = [None] * 8
    w1_toks = []
    w1_rt_guard = [None, None]

    def window1_chunk(m):
        for hh in range(2):
            c0 = 1024 * hh
            rs = rtring[hh]
            xh = xhat[hh]
            t1 = S(lambda e, m=m, c0=c0, rs=rs: e.activation(
                rs[:, :], gP(m)[:, c0:c0 + 1024], Act.Copy, bias=MAGIC,
                scale=sclb[:, 5:6]),
                deps=[t_bc2, w1_rt_guard[hh]] + gate_ev, pub=True)
            t2 = V(lambda e, m=m, c0=c0, rs=rs: e.tensor_scalar(
                gP(m)[:, c0:c0 + 1024], rs[:, :], MAGIC, None, Alu.subtract),
                deps=[t1], pub=True)
            t3 = S(lambda e, m=m, c0=c0, xh=xh: e.activation(
                xh[:, :], gP(m)[:, c0:c0 + 1024], Act.Sigmoid,
                scale=sclb[:, 4:5]), deps=[t2], pub=True)
            t4 = S(lambda e, xh=xh, rs=rs: e.activation(
                rs[:, :], xh[:, :], Act.Copy, bias=MAGIC, scale=65536.0),
                deps=[t3], pub=True)
            t5 = V(lambda e, rs=rs: e.tensor_scalar(
                rs[:, :], rs[:, :], MAGIC, 65535.0, Alu.subtract, Alu.min),
                deps=[t4], pub=True)
            t6 = V(lambda e, m=m, c0=c0, rs=rs: e.tensor_tensor(
                gP(m)[:, c0:c0 + 1024], rs[:, :], gP(m)[:, c0:c0 + 1024],
                Alu.mult), deps=[t5], pub=True)
            w1_rt_guard[hh] = t6
            w1_toks.append(V(lambda e, m=m, c0=c0, hh=hh: e.tensor_reduce(
                mxc2[:, (2 * m + hh):(2 * m + hh) + 1],
                gP(m)[:, c0:c0 + 1024], AX, Alu.max,
                apply_absolute_value=True), deps=[t6], pub=True, tiny=True))

    for m in range(24):
        pb = m % 2
        wff_tok[pb] = ffn_col_load(24 + m, pb, [wff_free[pb]])
        for n in range(4):
            bk = (m * 4 + n) % 4
            sl = (m * 4 + n) % 8
            mmt = None
            for k in range(8):
                mmt = TT(lambda e, pb=pb, bk=bk, k=k, n=n: e.matmul(
                    ps_mm[bk][:, :], wffcol[pb][:, 128 * k:128 * k + 128],
                    hx[k][:, 512 * n:512 * n + 512],
                    start=(k == 0), stop=(k == 7)),
                    deps=[wff_tok[pb], bank_free[bk]], pub=(k == 7))
            ev = S(lambda e, sl=sl, bk=bk: e.activation(
                uprng[sl][:, :], ps_mm[bk][:, :], Act.Copy, scale=sclb[:, 1:2]),
                deps=[mmt, upr_free[sl], t_bc2], pub=True, tiny=True)
            bank_free[bk] = ev
            mxe = V(lambda e, m=m, n=n, sl=sl: e.tensor_reduce(
                mxc[:, 4 * m + n:4 * m + n + 1], uprng[sl][:, :], AX, Alu.max,
                apply_absolute_value=True), deps=[ev], pub=True, tiny=True)
            up_ev.append(mxe)
            up_last_mxe[sl] = mxe
            sp = dma("sync", f"upr{sl}", upspill.ap()[m, n], uprng[sl][:, :],
                     deps=[ev])
            up_spill_toks[(m, n)] = sp
            upr_free[sl] = sp
        window1_chunk(m)

    dbg_cap("scl2", scl[:, :], [t_bc2], 32, parts=1)
    dbg_cap("P_0", gP(0)[:, 0:512], w1_toks, 512)
    t_mxu = global_max((0, 96), 6, up_ev)
    t_mxP = emit("gpsimd", lambda e: e.tensor_reduce(
        scl[0:1, 7:8], mxc2[:, 0:48], mybir.AxisListType.XYZWC, Alu.max),
        deps=w1_toks + [t_mxu], pub=True, tiny=True)
    t_ar3 = allreduce_pair(6, [t_mxu, t_mxP])
    t_sup = V(lambda e: e.tensor_scalar(scl[0:1, 8:9], scl[0:1, 6:7], INV_QMAX,
                                        1e-12, Alu.mult, Alu.max),
              deps=[t_ar3], pub=True, tiny=True)
    t_iup = V(lambda e: e.reciprocal(scl[0:1, 9:10], scl[0:1, 8:9]),
              deps=[t_sup], pub=True, tiny=True)
    t_cg = V(lambda e: e.tensor_scalar_mul(scl[0:1, 12:13], scl[0:1, 4:5],
                                           1.0 / 65536.0), deps=[t_sg], pub=True, tiny=True)
    t_mg2 = V(lambda e: e.tensor_tensor(scl[0:1, 20:21], scl[0:1, 7:8],
                                        scl[0:1, 12:13], Alu.mult),
              deps=[t_ar3, t_cg], pub=True, tiny=True)
    t_s4 = V(lambda e: e.tensor_scalar(scl[0:1, 10:11], scl[0:1, 20:21],
                                       INV_QMAX, 1e-12, Alu.mult, Alu.max),
             deps=[t_mg2], pub=True, tiny=True)
    t_i4 = V(lambda e: e.reciprocal(scl[0:1, 11:12], scl[0:1, 10:11]),
             deps=[t_s4], pub=True, tiny=True)
    t_c4 = V(lambda e: e.tensor_tensor(scl[0:1, 13:14], scl[0:1, 12:13],
                                       scl[0:1, 11:12], Alu.mult),
             deps=[t_i4], pub=True, tiny=True)
    t_s4u = V(lambda e: e.tensor_tensor(scl[0:1, 14:15], scl[0:1, 10:11],
                                        scl[0:1, 8:9], Alu.mult),
              deps=[t_s4, t_sup], pub=True, tiny=True)
    t_bc3 = bcast_scl([t_iup, t_c4, t_s4u])

    dbg_cap("scl3", scl[:, :], [t_bc3, t_s4u, t_c4], 32, parts=1)
    o_toks = []
    upr_free2 = list(up_last_mxe)
    for m in range(24):
        for n in range(4):
            sl = (m * 4 + n) % 8
            rd = dma("sync", f"upr{sl}", uprng[sl][:, :], upspill.ap()[m, n],
                     deps=[up_spill_toks[(m, n)], upr_free2[sl]])
            t1 = S(lambda e, sl=sl: e.activation(
                rtring[0][:, 0:512], uprng[sl][:, :], Act.Copy, bias=MAGIC,
                scale=sclb[:, 9:10]), deps=[rd, t_bc3, (o_toks[-1] if o_toks else None)], pub=True)
            t2 = V(lambda e, sl=sl: e.tensor_scalar(
                uprng[sl][:, :], rtring[0][:, 0:512], MAGIC, None, Alu.subtract),
                deps=[t1], pub=True)
            t3 = S(lambda e, m=m, n=n: e.activation(
                rtring[1][:, 0:512], gP(m)[:, 512 * n:512 * n + 512], Act.Copy,
                bias=MAGIC, scale=sclb[:, 13:14]),
                deps=[t_bc3] + w1_toks, pub=True)
            t4 = V(lambda e, m=m, n=n, sl=sl: e.scalar_tensor_tensor(
                gP(m)[:, 512 * n:512 * n + 512], rtring[1][:, 0:512], MAGIC,
                uprng[sl][:, :], Alu.subtract, Alu.mult),
                deps=[t2, t3], pub=True)
            upr_free2[sl] = t4
            o_toks.append(V(lambda e, m=m, n=n: e.tensor_reduce(
                mxc[:, 4 * m + n:4 * m + n + 1], gP(m)[:, 512 * n:512 * n + 512],
                AX, Alu.max, apply_absolute_value=True), deps=[t4], pub=True, tiny=True))
    dbg_cap("o_0", gP(0)[:, 0:512], o_toks, 512)
    t_mxo = global_max((0, 96), 15, o_toks)
    t_ar4 = allreduce_pair(15, [t_mxo])
    t_mo = V(lambda e: e.tensor_tensor(scl[0:1, 16:17], scl[0:1, 15:16],
                                       scl[0:1, 14:15], Alu.mult),
             deps=[t_ar4, t_s4u], pub=True, tiny=True)
    t_s5 = V(lambda e: e.tensor_scalar(scl[0:1, 17:18], scl[0:1, 16:17],
                                       INV_QMAX, 1e-12, Alu.mult, Alu.max),
             deps=[t_mo], pub=True, tiny=True)
    t_i5 = V(lambda e: e.reciprocal(scl[0:1, 18:19], scl[0:1, 17:18]),
             deps=[t_s5], pub=True, tiny=True)
    t_t5 = V(lambda e: e.tensor_tensor(scl[0:1, 19:20], scl[0:1, 14:15],
                                       scl[0:1, 18:19], Alu.mult),
             deps=[t_i5], pub=True, tiny=True)
    t_bc4 = bcast_scl([t_t5])
    t_gb5 = V(lambda e: e.tensor_scalar(adp[:, 48:56], adp[:, 40:48],
                                        sclb[:, 17:18], None, Alu.mult),
              deps=[t_bc4, t_gbq], pub=True, tiny=True)

    dbg_cap("scl4", scl[:, :], [t_bc4, t_gb5], 32, parts=1)
    dbg_cap("gb5", adp[:, 48:56], [t_gb5], 8)
    oq_toks = {}
    w3_rt_guard = [None, None]
    for m in range(24):
        for n in range(4):
            hh = (m * 4 + n) % 2
            rs = rtring[hh]
            t1 = S(lambda e, m=m, n=n, rs=rs: e.activation(
                rs[:, 0:512], gP(m)[:, 512 * n:512 * n + 512], Act.Copy,
                bias=MAGIC, scale=sclb[:, 19:20]),
                deps=[t_bc4, w3_rt_guard[hh]] + o_toks, pub=True)
            oq_toks[(m, n)] = V(lambda e, m=m, n=n, rs=rs: e.tensor_scalar(
                gP(m)[:, 512 * n:512 * n + 512], rs[:, 0:512], MAGIC, None,
                Alu.subtract), deps=[t1], pub=True)
            w3_rt_guard[hh] = oq_toks[(m, n)]

    dbg_cap("oq_0", gP(0)[:, 0:512], [oq_toks[(0, 0)], oq_toks[(0, 1)]], 512)
    st_toks = []
    xr_toks = [None] * 4
    up_mm_done = up_ev[-1]
    for n in range(4):
        for q in range(4):
            xr_toks[q] = dma("sync", f"xrr{q}", xrr[q][:, :],
                             xspill.ap()[4 * n + q],
                             deps=xsp_toks + st_toks[-4:] + [xq_toks[-1],
                                                            up_mm_done])
        for mo in range(8):
            pb = mo % 2
            wff_tok[pb] = ffn_col_load(48 + mo, pb, [wff_free[pb]])
            bk = mo % 4
            mmt = None
            for k in range(24):
                mmt = TT(lambda e, pb=pb, bk=bk, k=k, n=n: e.matmul(
                    ps_mm[bk][:, :], wffcol[pb][:, 128 * k:128 * k + 128],
                    gP(k)[:, 512 * n:512 * n + 512],
                    start=(k == 0), stop=(k == 23)),
                    deps=[wff_tok[pb], bank_free[bk], oq_toks[(k, n)]],
                    pub=(k == 23))
            sl = mo % 4
            ev = S(lambda e, bk=bk, mo=mo, sl=sl: e.activation(
                ctile[sl][:, :], ps_mm[bk][:, :], Act.Copy,
                scale=adp[:, 48 + mo:49 + mo]),
                deps=[mmt, t_gb5, ctile_free[sl]], pub=True, tiny=True)
            bank_free[bk] = ev
            wff_free[pb] = mmt
            for q in range(4):
                si = tr_ctr[0] % 8
                tr_ctr[0] += 1
                t_tr = TT(lambda e, sl=sl, q=q, si=si: e.transpose(
                    tr_slot(si), ctile[sl][:, 128 * q:128 * q + 128], id_b[:, :]),
                    deps=[ev, tr_guard[si % 2]], pub=True)
                ra = V(lambda e, q=q, mo=mo, si=si: e.tensor_tensor(
                    xrr[q][:, 128 * mo:128 * mo + 128],
                    xrr[q][:, 128 * mo:128 * mo + 128], tr_slot(si), Alu.add),
                    deps=[t_tr, xr_toks[q]], pub=True)
                tr_guard[si % 2] = ra
                xr_toks[q] = ra
                ctile_free[sl] = t_tr
        for q in range(4):
            st_toks.append(dma("sync", f"ost{q}",
                               out_h[512 * n + 128 * q:512 * n + 128 * q + 128, :],
                               xrr[q][:, :], deps=[xr_toks[q]]))

    wait_list = st_toks[-4:]
    if dbg:
        wait_list = wait_list + [last_dma.get("dbgk")]
    emit("sync", lambda e: None, deps=wait_list)

    build_program.n_ops = len(ops)
    build_program.ops_dbg = [(o["eng"], o["fn"].__code__.co_firstlineno,
                              o["waits"], o["incs"]) for o in ops]
    if trunc is not None:
        ops[:] = ops[:trunc]
    with nc.Block() as block:
        def mk(eng_name):
            def run(e):
                for op in ops:
                    if op["eng"] != eng_name:
                        continue
                    for s, v in op["waits"]:
                        e.wait_ge(s, v)
                    ins = op["fn"](e)
                    for s, nn in op["incs"]:
                        assert ins is not None, "inc on op returning no instruction"
                        ins = ins.then_inc(s, nn)
            return run

        block.sync(mk("sync"))
        block.gpsimd(mk("gpsimd"))
        block.vector(mk("vector"))
        block.scalar(mk("scalar"))
        block.tensor(mk("tensor"))
    return nc


_NC_CACHE = None


def _get_nc():
    global _NC_CACHE
    if _NC_CACHE is None:
        _NC_CACHE = build_program(debug_sync=DEBUG_SYNC)
    return _NC_CACHE


def make_inmaps(L0_attn_x, L0_attn_in_w, L0_attn_in_b, L0_attn_conv_w,
                L0_attn_conv_b, L0_attn_xp_w, L0_attn_xp_b, L0_attn_out_w,
                L0_attn_out_b, L0_ffn_w_gate, L0_ffn_w_up, L0_ffn_w_down,
                L0_c, L0_wmod):
    f = np.float32
    x = np.ascontiguousarray(L0_attn_x, dtype=f).reshape(16384, D)
    in_w = np.ascontiguousarray(L0_attn_in_w, dtype=f)
    winT = np.ascontiguousarray(
        in_w.reshape(16, 128, 8, 128).transpose(0, 3, 2, 1))
    inb = np.ascontiguousarray(
        np.asarray(L0_attn_in_b, dtype=f).reshape(16, 128).T)
    convw = np.ascontiguousarray(
        np.asarray(L0_attn_conv_w, dtype=f).reshape(16, 128, 4)
        .transpose(1, 0, 2).reshape(128, 64))
    convb = np.ascontiguousarray(
        np.asarray(L0_attn_conv_b, dtype=f).reshape(16, 128).T)
    xpwT = np.ascontiguousarray(
        np.ascontiguousarray(L0_attn_xp_w, dtype=f).T.reshape(16, 128, 32))
    xpba = np.ascontiguousarray(np.asarray(L0_attn_xp_b, dtype=f)[:16].reshape(16, 1))
    xpbb = np.ascontiguousarray(np.asarray(L0_attn_xp_b, dtype=f)[16:].reshape(16, 1))
    outwT = np.ascontiguousarray(
        np.asarray(L0_attn_out_w, dtype=f).T.reshape(16, 8, 128).transpose(1, 0, 2))
    outb = np.ascontiguousarray(
        np.asarray(L0_attn_out_b, dtype=f).reshape(8, 128).T)
    wg = np.ascontiguousarray(L0_ffn_w_gate, dtype=f)
    wu = np.ascontiguousarray(L0_ffn_w_up, dtype=f)
    wd = np.ascontiguousarray(L0_ffn_w_down, dtype=f)
    cvec = np.ascontiguousarray(np.asarray(L0_c, dtype=f).reshape(8, 128).T)
    wmod = np.ascontiguousarray(L0_wmod, dtype=f)

    identf = np.eye(128, dtype=f)
    identb = np.eye(128, dtype=ml_dtypes.bfloat16)
    egu = np.zeros((32, 8 * 128), dtype=f)
    for kc in range(8):
        for kp in range(128):
            egu[4 * kc + kp // 32, 128 * kc + kp] = 1.0
    ed = np.zeros((96, 24 * 128), dtype=f)
    for kc in range(24):
        for kp in range(128):
            ed[4 * kc + kp // 32, 128 * kc + kp] = 1.0

    in_maps = []
    for r in range(NCORES):
        xs = np.empty((TH, D), dtype=f)
        t0 = r * T
        if r == 0:
            xs[:HALO] = 0.0
        else:
            xs[:HALO] = x[t0 - HALO:t0]
        xs[HALO:] = x[t0:t0 + T]
        rows_gu = np.empty((6, 128, D), dtype=f)
        lhsT_gu = np.empty((6, 128, 8, 128), dtype=f)
        for s in range(6):
            j = r + 8 * s
            if j < 24:
                W, m = wg, j
            else:
                W, m = wu, j - 24
            blk = W[128 * m:128 * m + 128, :]
            rows_gu[s] = blk
            lhsT_gu[s] = blk.reshape(128, 8, 128).transpose(2, 1, 0)
        md = (r + 48) - 48
        blkd = wd[128 * md:128 * md + 128, :]
        rows_d = np.ascontiguousarray(blkd)
        lhsT_d = np.ascontiguousarray(blkd.reshape(128, 24, 128).transpose(2, 1, 0))
        wms = wmod[:, 768 * r:768 * r + 768]
        wmod_t = np.ascontiguousarray(
            wms.reshape(8, 128, 6, 128).transpose(2, 0, 1, 3))
        onehot = np.zeros((NS, 8), dtype=f)
        if r > 0:
            onehot[:, r - 1] = 1.0
        hmask = np.full((128, 1), 0.0 if r == 0 else 1.0, dtype=f)
        in_maps.append({
            "x": np.ascontiguousarray(xs),
            "winT": winT, "inb": inb, "convw": convw, "convb": convb,
            "xpwT": xpwT, "xpba": xpba, "xpbb": xpbb, "outwT": outwT, "outb": outb,
            "lpq_rows_gu": np.ascontiguousarray(rows_gu),
            "lpq_lhsT_gu": np.ascontiguousarray(lhsT_gu),
            "lpq_rows_d": rows_d, "lpq_lhsT_d": lhsT_d,
            "cvec": cvec, "wmod": wmod_t,
            "identf": identf, "identb": identb, "egu": egu, "ed": ed,
            "onehot": onehot, "hmask": hmask,
            "ones1": np.ones((1, 128), dtype=f),
        })

    return in_maps


def kernel(**inputs):
    in_maps = make_inmaps(**inputs)
    nc = _get_nc()
    res = run_bass_kernel_spmd(nc, in_maps, core_ids=list(range(NCORES)))
    out = np.concatenate([np.asarray(r["out"], dtype=np.float32)
                          for r in res.results], axis=0)
    return out.reshape(1, 16384, D)


# version 08
# speedup vs baseline: 1.0858x; 1.0410x over previous
import numpy as np
import ml_dtypes
import concourse.bass as bass
import concourse.mybir as mybir
from concourse.bass_utils import run_bass_kernel_spmd

F32 = mybir.dt.float32
BF16 = mybir.dt.bfloat16
Alu = mybir.AluOpType
Act = mybir.ActivationFunctionType
AX = mybir.AxisListType.X
MAGIC = float(np.float32(1.5 * 2.0 ** 23))
INV_QMAX = float(np.float32(1.0 / 32767.0))

NCORES = 8
T = 2048
HALO = 128
TH = T + HALO
D = 1024
DI = 2048
FF = 3072
NS = 16
EPS = 1e-5

SLOT_OFF = [0, 262144, 524288, 786432, 1048576, 1310720, 1572864]
RANK_BYTES = 2359296


def col_addr(j):
    return j % NCORES, SLOT_OFF[j // NCORES] // 2


DEBUG_SYNC = False


def build_program(debug_sync=False, trunc=None, dbg=False):
    nc = bass.Bass("TRN2", target_bir_lowering=False,
                   detect_race_conditions=debug_sync)

    def din(name, shape, dt=F32):
        return nc.dram_tensor(name, shape, dt, kind="ExternalInput").ap()

    x_h = din("x", [TH, D])
    winT_h = din("winT", [16, 128, 8, 128])
    inb_h = din("inb", [128, 16])
    convw_h = din("convw", [128, 64])
    convb_h = din("convb", [128, 16])
    xpwT_h = din("xpwT", [16, 128, 32])
    xpba_h = din("xpba", [16, 1])
    xpbb_h = din("xpbb", [16, 1])
    outwT_h = din("outwT", [8, 16, 128])
    outb_h = din("outb", [128, 8])
    lpq_rows_gu_h = din("lpq_rows_gu", [6, 128, D])
    lpq_lhsT_gu_h = din("lpq_lhsT_gu", [6, 128, 8, 128])
    lpq_rows_d_h = din("lpq_rows_d", [128, FF])
    lpq_lhsT_d_h = din("lpq_lhsT_d", [128, 24, 128])
    c_h = din("cvec", [128, 8])
    wmod_h = din("wmod", [6, 8, 128, 128])
    ident_f_h = din("identf", [128, 128])
    ident_b_h = din("identb", [128, 128], BF16)
    egu_h = din("egu", [32, 8 * 128])
    ed_h = din("ed", [96, 24 * 128])
    onehot_h = din("onehot", [NS, 8])
    hmask_h = din("hmask", [128, 1])
    ones1_h = din("ones1", [1, 128])

    out_h = nc.dram_tensor("out", [T, D], F32, kind="ExternalOutput").ap()
    dbg_h = nc.dram_tensor("dbg", [128, 16384], F32,
                           kind="ExternalOutput").ap() if dbg else None

    ag0_in = nc.dram_tensor("ag0_in", [1, 768], F32)
    ag0_out = nc.dram_tensor("ag0_out", [NCORES, 768], F32)
    agw_in = nc.dram_tensor("agw_in", [1, RANK_BYTES // 2], BF16)
    agw_out = nc.dram_tensor("agw_out", [NCORES, RANK_BYTES // 2], BF16)
    ag1_in = nc.dram_tensor("ag1_in", [1, 32], F32)
    ag1_out = nc.dram_tensor("ag1_out", [NCORES, 32], F32)
    ar_in = nc.dram_tensor("ar_in", [1, 2], F32)
    ar_out = nc.dram_tensor("ar_out", [1, 2], F32)
    sumd = nc.dram_tensor("sumd", [1, T], F32)
    mxd = nc.dram_tensor("mxd", [128, 1], F32)
    scld = nc.dram_tensor("scld", [1, 32], F32)
    xspill = nc.dram_tensor("xspill", [16, 128, D], F32)
    upspill = nc.dram_tensor("upspill", [24, 4, 128, 512], BF16)

    SBUF_SZ = nc.SBUF_PARTITION_SIZE_BYTES
    AR_S = SBUF_SZ - 34816 - 32
    AR_H = AR_S - 69632
    AR_X = AR_H - 65536

    def at(name, shape, dt, off):
        return nc.alloc_sbuf_tensor_at(name, shape, dt, offset=off)

    x_tm = [at(f"x{i}", [128, D], F32, AR_X + 4096 * i) for i in range(16)]
    gP_lo = [at(f"gPl{m}", [128, T], BF16, AR_X + 4096 * m) for m in range(16)]
    wrow = at("wrow", [128, FF], F32, AR_H + 0)
    wlhs = at("wlhs", [128, FF], F32, AR_H + 12288)
    wqcol = at("wqcol", [128, FF], BF16, AR_H + 24576)
    wmodb = at("wmodb", [128, 6 * 1024], F32, AR_H + 30720)
    ed_s = at("ed_s", [96, 24 * 128], F32, AR_H + 55296)
    smax = at("smax", [128, 96], F32, AR_H + 67584)
    sval = at("sval", [128, 96], F32, AR_H + 67968)
    sinv = at("sinv", [128, 96], F32, AR_H + 68352)
    h_bf = [at(f"h{m}", [128, TH], BF16, AR_H + 4352 * m) for m in range(16)]
    gP_hi = [at(f"gPh{m}", [128, T], BF16, AR_H + 4096 * m) for m in range(8)]
    hx = [at(f"hx{j}", [128, T], BF16, AR_H + 32768 + 4096 * j) for j in range(8)]
    xrr = [at(f"xrr{q}", [128, D], F32, AR_H + 32768 + 4096 * q) for q in range(8)]
    h1_bf = [at(f"h1_{j}", [128, TH], BF16, AR_S + 4352 * j) for j in range(8)]
    scanA = at("scanA", [32, T], F32, AR_S + 0)
    scanB = at("scanB", [16, T], F32, AR_S + 8192)
    scanC = at("scanC", [16, T], F32, AR_S + 16384)
    scanD = at("scanD", [16, T], F32, AR_S + 24576)
    y_bf = at("ybf", [16, T], BF16, AR_S + 24576)
    wffcol = [at(f"wff{p}", [128, FF], BF16, AR_S + 6144 * p) for p in range(2)]
    uprng = [at(f"upr{p}", [128, 512], BF16, AR_S + 12288 + 1024 * p) for p in range(8)]
    rtring = [at(f"rt{p}", [128, 512], F32, AR_S + 20480 + 2048 * p) for p in range(4)]
    ctile = [at(f"ct{p}", [128, 512], BF16, AR_S + 28672 + 1024 * p) for p in range(4)]

    def gP(m):
        return gP_lo[m] if m < 16 else gP_hi[m - 16]

    A = nc.alloc_sbuf_tensor
    win_bf = [A(f"winb{p}", [128, 1024], BF16) for p in range(2)]
    hc = A("hc", [128, T], F32)
    sq1 = A("sq1", [128, D], F32)
    xstage = A("xstage", [128, D], F32)
    xhat = [A(f"xhat{p}", [128, D], BF16) for p in range(2)]

    convw = A("convw_s", [128, 64], F32)
    convb = A("convb_s", [128, 16], F32)
    xpw_bf = A("xpwb", [128, 512], BF16)
    xpba_s = A("xpba_s", [16, 1], F32)
    xpbb_s = A("xpbb_s", [16, 1], F32)
    outw_bf = A("outwb", [16, 1024], BF16)
    outb_s = A("outb_s", [128, 8], F32)
    gaob = A("gaob", [128, 8], F32)
    ones1 = A("ones1_s", [1, 128], F32)
    adp = A("adp", [128, 64], F32)
    silc = A("silc", [128, 8], F32)
    inb_s = A("inb_s", [128, 16], F32)
    id_f = A("id_f", [128, 128], F32)
    id_b = A("id_b", [128, 128], BF16)
    egu_s = A("egu_s", [32, 8 * 128], F32)
    sT_s = A("sT_s", [96, 128], F32)
    invT_s = A("invT_s", [96, 128], F32)
    ones_c = A("ones_c", [128, 1], BF16)
    mxc = A("mxc", [128, 128], F32)
    mxc2 = A("mxc2", [128, 96], F32)
    sumstg = A("sumstg", [1, 512], F32)
    stats = A("stats", [128, 32], F32)
    gqs = A("gqs", [128, 8], F32)
    sq1b = A("sq1b", [128, D], F32)
    scl = A("scl", [1, 32], F32)
    sclb = A("sclb", [128, 32], F32)
    onehot_s = A("onehot_s", [NS, 8], F32)
    sin_s = A("sin_s", [NS, 1], F32)
    cgat = A("cgat", [NS, 8], F32)
    cgee = A("cgee", [NS, 8], F32)
    csc = A("csc", [NS, 8], F32)
    hmask_s = A("hmask_s", [128, 1], F32)
    modloc = A("modloc", [128, 6], F32)



    canary = A("canary", [1, 8], F32)
    can_addr = nc.lookup_mloc(canary).addr
    assert can_addr + 32 <= AR_X, (
        f"auto SBUF allocations overflow into manual arena: {can_addr} > {AR_X}")
    ps_mm = [nc.alloc_psum_tensor(f"psmm{i}", [128, 512], F32) for i in range(4)]
    ps_tr = nc.alloc_psum_tensor("pstr", [128, 1024], BF16)
    ps_tr2 = nc.alloc_psum_tensor("pstr2", [128, 1024], BF16)
    ps_a = nc.alloc_psum_tensor("psa", [128, 512], F32)
    ps_b = nc.alloc_psum_tensor("psb", [128, 512], F32)

    sem_h = {n: nc.alloc_semaphore(n) for n in ["sT", "sV", "sS", "sG"]}
    cnt = {n: 0 for n in sem_h}
    ENG_SEM = {"vector": "sV", "scalar": "sS", "tensor": "sT", "gpsimd": "sG"}
    ops = []
    watermark = {e: {} for e in ["sync", "gpsimd", "vector", "scalar", "tensor"]}

    def sem_for(key):
        if key not in sem_h:
            sem_h[key] = nc.alloc_semaphore(key)
            cnt[key] = 0
        return sem_h[key]

    class Tok:
        __slots__ = ("eng", "sem", "val", "tiny")

        def __init__(self, eng, sem, val, tiny=False):
            self.eng, self.sem, self.val, self.tiny = eng, sem, val, tiny

    def _flat(deps):
        for d in deps:
            if d is None:
                continue
            if isinstance(d, (list, tuple)):
                yield from _flat(d)
            else:
                yield d

    def emit(eng, fn, deps=(), pub=False, dkey=None, cc=None, tiny=False):
        need = {}
        for d in _flat(deps):
            if (d.sem in ("sT", "sV", "sS", "sG") and d.eng == eng
                    and not debug_sync and not d.tiny):
                continue
            if need.get(d.sem, -1) < d.val:
                need[d.sem] = d.val
        waits = []
        wm = watermark[eng]
        for s, v in need.items():
            if wm.get(s, -1) >= v:
                continue
            wm[s] = v
            waits.append((sem_h[s], v))
        incs = []
        tok = None
        if cc is not None:
            sem_for(cc)
            cnt[cc] += 1
            incs.append((sem_h[cc], 1))
            tok = Tok(eng, cc, cnt[cc])
        elif dkey is not None:
            sem_for(dkey)
            cnt[dkey] += 16
            incs.append((sem_h[dkey], 16))
            tok = Tok(eng, dkey, cnt[dkey])
        elif pub or (debug_sync and eng in ENG_SEM):
            s = ENG_SEM[eng]
            cnt[s] += 1
            incs.append((sem_h[s], 1))
            tok = Tok(eng, s, cnt[s], tiny)
        ops.append({"eng": eng, "fn": fn, "waits": waits, "incs": incs})
        return tok

    last_dma = {}
    dbg_off = [0]
    dbg_names = []

    def dbg_cap(name, ap, deps, cols, parts=128):
        if not dbg:
            return
        o = dbg_off[0]
        dbg_names.append((name, o, cols, parts))
        dbg_off[0] += cols
        dma("gpsimd", "dbgk", dbg_h[0:parts, o:o + cols], ap, deps=list(deps),
            slow=True)
    build_program.dbg_names = dbg_names

    def dma(q, key, out, in_, deps=(), slow=False, chain=True):
        deps = list(deps)
        if chain and key in last_dma:
            deps.append(last_dma[key])
        t = emit(q, lambda e, o=out, i=in_, s=slow: e.dma_start(
            out=o, in_=i, allow_slow_non_contiguous=s), deps=deps, dkey=key)
        last_dma[key] = t
        return t

    def V(fn, deps=(), pub=False, tiny=False):
        return emit("vector", fn, deps, pub=pub, tiny=tiny)

    def S(fn, deps=(), pub=False, tiny=False):
        return emit("scalar", fn, deps, pub=pub, tiny=tiny)

    def TT(fn, deps=(), pub=False):
        return emit("tensor", fn, deps, pub=pub)

    l_convw = dma("sync", "ini0", convw[:, :], convw_h)
    l_convb = dma("sync", "ini1", convb[:, :], convb_h)
    l_inb = dma("sync", "ini2", inb_s[:, :], inb_h)
    l_xpb = dma("sync", "ini3", xpba_s[:, :], xpba_h)
    l_xpb2 = dma("sync", "ini1", xpbb_s[:, :], xpbb_h)
    l_idf = dma("sync", "ini0", id_f[:, :], ident_f_h)
    l_idb = dma("sync", "ini1", id_b[:, :], ident_b_h)
    l_egu = dma("sync", "ini2", egu_s[:, :], egu_h)
    l_ed = dma("sync", "ini3", ed_s[:, :], ed_h)
    l_outb = dma("sync", "ini0", outb_s[:, :], outb_h)
    l_oh = dma("sync", "ini1", onehot_s[:, :], onehot_h)
    l_hm = dma("sync", "ini2", hmask_s[:, :], hmask_h)
    l_ones1 = dma("sync", "ini3", ones1[:, :], ones1_h)
    l_c = dma("sync", "ini3", silc[:, :], c_h)
    l_wmod = dma("sync", "ini0", wmodb[:, :].rearrange("p (m k q) -> p m k q", m=6, k=8),
                  wmod_h.rearrange("m k p q -> p m k q"))
    l_xpw = dma("gpsimd", "gini0", xpw_bf[:, :].rearrange("p (k t) -> p k t", t=32),
                 xpwT_h.rearrange("k p t -> p k t"))
    l_outw = dma("gpsimd", "gini1", outw_bf[:, :].rearrange("p (m t) -> p m t", t=128),
                  outwT_h.rearrange("m p t -> p m t"))
    t_ones = V(lambda e: e.memset(ones_c[:, :], 1.0), pub=True)
    t_scl0 = V(lambda e: e.memset(scl[:, :], 0.0), pub=True)

    t_silc = S(lambda e: e.activation(silc[:, :], silc[:, :], Act.Silu),
               deps=[l_c], pub=True, tiny=True)

    mod_mm = []
    for m in range(6):
        for k in range(8):
            tk = TT(lambda e, m=m, k=k: e.matmul(
                ps_b[:, m:m + 1],
                wmodb[:, (m * 8 + k) * 128:(m * 8 + k) * 128 + 128],
                silc[:, k:k + 1], start=(k == 0), stop=(k == 7)),
                deps=[l_wmod, t_silc], pub=(k == 7))
        mod_mm.append(tk)
    mod_ev = []
    for m in range(6):
        mod_ev.append(V(lambda e, m=m: e.tensor_copy(
            modloc[:, m:m + 1], ps_b[:, m:m + 1]), deps=mod_mm, pub=True, tiny=True))
    t_modw = None
    for m in range(6):
        t_modw = dma("gpsimd", "ag0w", ag0_in.ap()[0:1, 128 * m:128 * m + 128]
                     .rearrange("one p -> p one"), modloc[:, m:m + 1],
                     deps=[mod_ev[m]])
    t_ag0 = emit("gpsimd", lambda e: e.collective_compute(
        "AllGather", Alu.bypass, replica_groups=[list(range(NCORES))],
        ins=[ag0_in.ap().opt()], outs=[ag0_out.ap().opt()]),
        deps=[t_modw], cc="cc0")
    flat0 = ag0_out.ap().rearrange("r v -> () (r v)")
    t_adp = dma("gpsimd", "adpu", adp[:, 0:48],
                flat0[0:1, 0:6144].rearrange("one (j p) -> p (one j)", p=128),
                deps=[t_ag0], slow=True)
    t_ad = [t_adp]
    t_ca1 = V(lambda e: e.tensor_scalar_add(adp[:, 8:16], adp[:, 8:16], 1.0),
              deps=t_ad, pub=True, tiny=True)
    t_cb1 = V(lambda e: e.tensor_scalar_add(adp[:, 32:40], adp[:, 32:40], 1.0),
              deps=t_ad, pub=True, tiny=True)
    gq_toks = []
    prev_gq = None
    for src0 in (16, 40):
        tsg = S(lambda e, s=src0: e.activation(adp[:, 48:56], adp[:, s:s + 8],
                                               Act.Sign),
                deps=t_ad + ([prev_gq] if prev_gq else []), pub=True, tiny=True)
        tab = S(lambda e, s=src0: e.activation(adp[:, 56:64], adp[:, s:s + 8],
                                               Act.Abs), deps=t_ad, pub=True, tiny=True)
        tmx = V(lambda e: e.tensor_scalar(gqs[:, :], adp[:, 56:64], 0.25,
                                          None, Alu.max),
                deps=[tab], pub=True, tiny=True)
        prev_gq = V(lambda e, s=src0: e.tensor_tensor(
            adp[:, s:s + 8], adp[:, 48:56], gqs[:, :], Alu.mult),
            deps=[tsg, tmx], pub=True, tiny=True)

        gq_toks.append(prev_gq)
    t_gaq, t_gbq = gq_toks

    dbg_cap("adp", adp[:, :], [t_gaq, t_gbq, t_ca1, t_cb1], 64)
    tr_guard = [None, None]

    def tr_slot(si):
        return (ps_tr if si % 2 == 0 else ps_tr2)[
            :, 128 * ((si // 2) % 4):128 * ((si // 2) % 4) + 128]

    h1_toks = []
    x_toks = [None] * 16
    lnprev = [None, None]
    tr_ctr = [0]
    xhat_guard = [None, None]

    def ln_tile(i, dst, ap_dst_tok, affine_c1, affine_c0, out_chunk_fn, idtile,
                extra_deps=()):
        par = i % 2
        so = 16 * par
        stats_ = stats[:, so:so + 16]
        sq_ = sq1 if par == 0 else sq1b
        t_sum = V(lambda e: e.tensor_reduce(stats_[:, 0:1], dst[:, :], AX,
                                            Alu.add),
                  deps=[ap_dst_tok, lnprev[par]] + list(extra_deps), pub=True,
                  tiny=True)
        t_sq0 = S(lambda e: e.activation(sq_[:, :], dst[:, :], Act.Square),
                  deps=[ap_dst_tok, lnprev[par]] + list(extra_deps), pub=True)
        t_sq = V(lambda e: e.tensor_reduce(stats_[:, 1:2], sq_[:, :], AX,
                                           Alu.add),
                 deps=[t_sq0], pub=True, tiny=True)
        t_mn = V(lambda e: e.tensor_scalar_mul(stats_[:, 2:3], stats_[:, 0:1],
                                               1.0 / D), deps=[t_sum], pub=True, tiny=True)
        t_ms = V(lambda e: e.tensor_scalar_mul(stats_[:, 3:4], stats_[:, 1:2],
                                               1.0 / D), deps=[t_sq], pub=True, tiny=True)
        t_m2 = V(lambda e: e.tensor_tensor(stats_[:, 4:5], stats_[:, 2:3],
                                           stats_[:, 2:3], Alu.mult),
                 deps=[t_mn], pub=True, tiny=True)
        t_vr = V(lambda e: e.tensor_tensor(stats_[:, 5:6], stats_[:, 3:4],
                                           stats_[:, 4:5], Alu.subtract),
                 deps=[t_ms, t_m2], pub=True, tiny=True)
        t_ve = V(lambda e: e.tensor_scalar_add(stats_[:, 6:7], stats_[:, 5:6],
                                               EPS),
                 deps=[t_vr], pub=True, tiny=True)
        t_rc = V(lambda e: e.reciprocal(stats_[:, 7:8], stats_[:, 6:7]),
                 deps=[t_ve], pub=True, tiny=True)
        t_rs = S(lambda e: e.activation(stats_[:, 8:9], stats_[:, 7:8],
                                        Act.Sqrt),
                 deps=[t_rc], pub=True, tiny=True)
        if i == 2:
            dbg_cap("stats2", stats[:, :], [t_rs, t_mn, t_m2], 16)
            dbg_cap("xin2", dst[:, :], [t_rs], 1024)
        xh = xhat[i % 2]
        t_xh = V(lambda e: e.tensor_scalar(xh[:, :], dst[:, :], stats_[:, 2:3],
                                           stats_[:, 8:9], Alu.subtract,
                                           Alu.mult),
                 deps=[t_rs, xhat_guard[i % 2]] + list(extra_deps), pub=True)
        if i == 2:
            dbg_cap("xhat2", xh[:, :], [t_xh], 1024)
        lnprev[par] = t_xh
        evs = []
        t_tr = None
        for j in range(8):
            si = tr_ctr[0] % 8
            tr_ctr[0] += 1
            t_tr = TT(lambda e, j=j, si=si: e.transpose(
                tr_slot(si), xh[:, 128 * j:128 * j + 128], id_b[:, :]),
                deps=[t_xh, l_idb, tr_guard[si % 2]], pub=True)
            ev = V(lambda e, j=j, si=si: e.tensor_scalar(
                out_chunk_fn(j), tr_slot(si), adp[:, affine_c1 + j:affine_c1 + j + 1],
                adp[:, affine_c0 + j:affine_c0 + j + 1], Alu.mult, Alu.add),
                deps=[t_tr, t_ca1, t_cb1], pub=True, tiny=True)
            tr_guard[si % 2] = ev
            evs.append(ev)
        xhat_guard[i % 2] = t_tr
        return evs

    agw_writes = []
    lp_guard = [None]
    lp_rm_guard = [None]

    def lpbq_slot(s):
        I = D if s < 6 else FF
        B = I // 32
        if s < 6:
            t_row = dma("sync", "lpr", wrow[:, 0:I], lpq_rows_gu_h[s],
                        deps=[lp_rm_guard[0]])
            t_lh = dma("sync", "lpl",
                       wlhs[:, 0:I].rearrange("p (k q) -> p k q", q=128),
                       lpq_lhsT_gu_h[s], deps=[lp_guard[0]])
        else:
            t_row = dma("sync", "lpr", wrow[:, 0:I], lpq_rows_d_h,
                        deps=[lp_rm_guard[0]])
            t_lh = dma("sync", "lpl",
                       wlhs[:, 0:I].rearrange("p (k q) -> p k q", q=128),
                       lpq_lhsT_d_h, deps=[lp_guard[0]])
        t_rm = V(lambda e, I=I, B=B: e.tensor_reduce(
            smax[:, 0:B].rearrange("p b -> p b ()"),
            wrow[:, 0:I].rearrange("p (b t) -> p b t", t=32),
            AX, Alu.max, apply_absolute_value=True), deps=[t_row], pub=True, tiny=True)
        lp_rm_guard[0] = t_rm
        t_s = V(lambda e, B=B: e.tensor_scalar(
            sval[:, 0:B], smax[:, 0:B], 1.0 / 7.0, 1e-12, Alu.mult, Alu.max),
            deps=[t_rm], pub=True, tiny=True)
        t_i = V(lambda e, B=B: e.reciprocal(sinv[:, 0:B], sval[:, 0:B]),
                deps=[t_s], pub=True, tiny=True)
        t_t1 = TT(lambda e, B=B: e.transpose(ps_mm[2][0:B, 0:128], sval[:, 0:B],
                                             id_f[:, :]),
                  deps=[t_s, l_idf, lp_guard[0]], pub=True)
        t_t2 = TT(lambda e, B=B: e.transpose(ps_mm[3][0:B, 0:128], sinv[:, 0:B],
                                             id_f[:, :]),
                  deps=[t_i, l_idf, lp_guard[0]], pub=True)
        B4 = B
        ev1 = V(lambda e, B=B: e.tensor_copy(sT_s[0:B, :], ps_mm[2][0:B, 0:128]),
                deps=[t_t1], pub=True, tiny=True)
        ev2 = V(lambda e, B=B: e.tensor_copy(invT_s[0:B, :], ps_mm[3][0:B, 0:128]),
                deps=[t_t2], pub=True, tiny=True)
        E_s = egu_s if s < 6 else ed_s
        l_E = l_egu if s < 6 else l_ed
        wq_toks = []
        piece_guard = [agw_writes[-1]] if agw_writes else []
        for p in range(I // 512):
            exps = []
            for q in range(4):
                kc = 4 * p + q
                exps.append(TT(lambda e, q=q, kc=kc, B=B, E_s=E_s: e.matmul(
                    ps_a[:, 128 * q:128 * q + 128],
                    E_s[0:B, 128 * kc:128 * kc + 128], sT_s[0:B, :],
                    start=True, stop=True),
                    deps=[ev1, ev2, l_E] + mod_ev + piece_guard, pub=(q == 3)))
                exps.append(TT(lambda e, q=q, kc=kc, B=B, E_s=E_s: e.matmul(
                    ps_b[:, 128 * q:128 * q + 128],
                    E_s[0:B, 128 * kc:128 * kc + 128], invT_s[0:B, :],
                    start=True, stop=True),
                    deps=[ev1, ev2, l_E] + mod_ev + piece_guard, pub=(q == 3)))
            c0 = 512 * p
            t_u = V(lambda e, c0=c0: e.tensor_tensor(
                hc[:, 0:512], wlhs[:, c0:c0 + 512], ps_b[:, 0:512], Alu.mult),
                deps=[t_lh, exps[-1], exps[-2]], pub=True)
            t_t = V(lambda e: e.tensor_scalar_add(hc[:, 512:1024], hc[:, 0:512],
                                                  MAGIC), deps=[t_u], pub=True)
            wq_toks.append(V(lambda e, c0=c0: e.scalar_tensor_tensor(
                wqcol[:, c0:c0 + 512], hc[:, 512:1024], MAGIC, ps_a[:, 0:512],
                Alu.subtract, Alu.mult), deps=[t_t], pub=True))
            piece_guard = [t_u, wq_toks[-1]]
        lp_guard[0] = wq_toks[-1]
        eo = SLOT_OFF[s] // 2
        agw_writes.append(dma(
            "gpsimd", "agww",
            agw_in.ap()[0:1, eo:eo + 128 * I].rearrange(
                "one (p q) -> (one p) q", p=128),
            wqcol[:, 0:I], deps=wq_toks))

    for i in range(17):
        dst = xstage if i == 0 else x_tm[i - 1]
        t_x = dma("sync", f"xl{i % 4}", dst[:, :], x_h[128 * i:128 * i + 128, :])
        if i >= 1:
            x_toks[i - 1] = t_x
        evs = ln_tile(i, dst, t_x, 8, 0,
                      lambda j, i=i: h1_bf[j][:, 128 * i:128 * i + 128], id_b)
        h1_toks.extend(evs)
        if i < 7:
            lpbq_slot(i)
    dbg_cap("h1_0", h1_bf[0][:, 0:512], h1_toks, 512)
    t_agw = emit("gpsimd", lambda e: e.collective_compute(
        "AllGather", Alu.bypass, replica_groups=[list(range(NCORES))],
        ins=[agw_in.ap().opt()], outs=[agw_out.ap().opt()]),
        deps=agw_writes, cc="ccw")

    NTILES_IN = [(0, 128), (128, 512), (640, 512), (1152, 512), (1664, 512)]
    win_tok = [None, None]
    win_free = [None, None]
    bank_free = [None] * 4
    hprime = []
    for m in range(16):
        pb = m % 2
        win_tok[pb] = dma("gpsimd", f"win{pb}",
                          win_bf[pb][:, :].rearrange("p (k q) -> p k q", q=128),
                          winT_h[m], deps=[win_free[pb]])
        ev_toks = []
        for ni, (c0, w) in enumerate(NTILES_IN):
            bk = (m * 5 + ni) % 4
            last_tile = (c0 + w - 1) // 128
            h1_dep = h1_toks[:8 * (last_tile + 1)]
            mmt = None
            for k in range(8):
                mmt = TT(lambda e, pb=pb, bk=bk, k=k, c0=c0, w=w: e.matmul(
                    ps_mm[bk][:, 0:w], win_bf[pb][:, 128 * k:128 * k + 128],
                    h1_bf[k][:, c0:c0 + w], start=(k == 0), stop=(k == 7)),
                    deps=[win_tok[pb], bank_free[bk]] + (h1_dep if k == 0 else []),
                    pub=(k == 7))
            ev = S(lambda e, m=m, bk=bk, c0=c0, w=w: e.activation(
                h_bf[m][:, c0:c0 + w], ps_mm[bk][:, 0:w], Act.Identity,
                bias=inb_s[:, m:m + 1]),
                deps=[mmt, l_inb] + (agw_writes if m <= 7 else []), pub=True,
                tiny=True)
            bank_free[bk] = ev
            ev_toks.append(ev)
        win_free[pb] = ev_toks[-1]
        t_msk = V(lambda e, m=m: e.tensor_scalar(
            h_bf[m][:, HALO - 3:HALO], h_bf[m][:, HALO - 3:HALO],
            hmask_s[:, 0:1], None, Alu.mult), deps=ev_toks + [l_hm], pub=True)
        t_c = V(lambda e, m=m: e.tensor_scalar(
            hc[:, 0:T], h_bf[m][:, HALO - 3:HALO - 3 + T], convw[:, 4 * m:4 * m + 1],
            None, Alu.mult),
            deps=[t_msk, l_convw, hprime[-1] if hprime else None], pub=True)
        for j in (1, 2, 3):
            t_c = V(lambda e, m=m, j=j: e.scalar_tensor_tensor(
                hc[:, 0:T], h_bf[m][:, HALO - 3 + j:HALO - 3 + j + T],
                convw[:, 4 * m + j:4 * m + j + 1], hc[:, 0:T], Alu.mult, Alu.add),
                deps=[t_c], pub=True)
        hprime.append(S(lambda e, m=m: e.activation(
            h_bf[m][:, HALO:HALO + T], hc[:, 0:T], Act.Silu,
            bias=convb[:, m:m + 1]), deps=[t_c, l_convb], pub=True))

    dbg_cap("hprime0", h_bf[0][:, HALO:HALO + 512], hprime, 512)
    xp_toks = []
    for n in range(4):
        mmta = None
        mmtb = None
        for k in range(16):
            mmta = TT(lambda e, n=n, k=k: e.matmul(
                ps_a[0:16, 0:512], xpw_bf[:, 32 * k:32 * k + 16],
                h_bf[k][:, HALO + 512 * n:HALO + 512 * n + 512],
                start=(k == 0), stop=(k == 15)),
                deps=[l_xpw, hprime[k]] + (xp_toks[-2:] if xp_toks else []),
                pub=(k == 15))
            mmtb = TT(lambda e, n=n, k=k: e.matmul(
                ps_mm[0][0:16, 0:512], xpw_bf[:, 32 * k + 16:32 * k + 32],
                h_bf[k][:, HALO + 512 * n:HALO + 512 * n + 512],
                start=(k == 0), stop=(k == 15)),
                deps=[l_xpw, hprime[k], bank_free[0]]
                + (xp_toks[-2:] if xp_toks else []),
                pub=(k == 15))
        xp_toks.append(V(lambda e, n=n: e.tensor_scalar(
            scanA[0:16, 512 * n:512 * n + 512], ps_a[0:16, 0:512],
            xpba_s[0:16, 0:1], None, Alu.add), deps=[mmta, l_xpb], pub=True))
        xp_toks.append(V(lambda e, n=n: e.tensor_scalar(
            scanC[0:16, 512 * n:512 * n + 512], ps_mm[0][0:16, 0:512],
            xpbb_s[0:16, 0:1], None, Alu.add), deps=[mmtb, l_xpb2], pub=True))
    sum_toks = []
    for n in range(4):
        mmt = None
        for k in range(16):
            mmt = TT(lambda e, n=n, k=k: e.matmul(
                ps_b[0:1, 0:512], ones_c[:, :],
                h_bf[k][:, HALO + 512 * n:HALO + 512 * n + 512],
                start=(k == 0), stop=(k == 15)),
                deps=[t_ones, hprime[k]] + (sum_toks[-1:] if sum_toks else []),
                pub=(k == 15))
        t_se = V(lambda e: e.tensor_copy(sumstg[:, :], ps_b[0:1, 0:512]),
                 deps=[mmt, sum_toks[-1] if sum_toks else None], pub=True, tiny=True)
        sum_toks.append(dma("sync", "sumw", sumd.ap()[0:1, 512 * n:512 * n + 512],
                            sumstg[:, :], deps=[t_se]))
    t_e = S(lambda e: e.activation(scanB[0:16, :], scanA[0:16, :], Act.Exp),
            deps=xp_toks + sum_toks, pub=True)
    t_a = S(lambda e: e.activation(scanB[0:16, :], scanB[0:16, :], Act.Exp,
                                   scale=-1.0), deps=[t_e], pub=True)
    t_sl = S(lambda e: e.activation(scanC[:, :], scanC[:, :], Act.Silu),
             deps=xp_toks, pub=True)
    t_sb = dma("gpsimd", "sumb", scanD[:, :], sumd.ap().partition_broadcast(NS),
               deps=sum_toks)
    t_b = V(lambda e: e.tensor_tensor(scanC[:, :], scanC[:, :], scanD[:, :],
                                      Alu.mult), deps=[t_sl, t_sb], pub=True)
    t_y0 = V(lambda e: e.tensor_tensor_scan(scanD[:, :], scanB[0:16, :],
                                            scanC[:, :], 0.0, Alu.mult, Alu.add),
             deps=[t_a, t_b], pub=True)
    t_P = V(lambda e: e.tensor_tensor_scan(scanA[0:16, :], scanB[0:16, :],
                                           scanB[0:16, :], 1.0, Alu.mult,
                                           Alu.bypass), deps=[t_a], pub=True)
    t_c1 = dma("gpsimd", "cry", ag1_in.ap()[0:1, 0:16].rearrange("one n -> n one"),
               scanA[0:16, T - 1:T], deps=[t_P])
    t_c2 = dma("gpsimd", "cry", ag1_in.ap()[0:1, 16:32].rearrange("one n -> n one"),
               scanD[:, T - 1:T], deps=[t_y0])
    t_ag1 = emit("gpsimd", lambda e: e.collective_compute(
        "AllGather", Alu.bypass, replica_groups=[list(range(NCORES))],
        ins=[ag1_in.ap().opt()], outs=[ag1_out.ap().opt()]),
        deps=[t_c1, t_c2], cc="cc1")
    t_g1 = dma("gpsimd", "cryo", cgat[:, :], ag1_out.ap()[:, 0:16].rearrange("r n -> n r"),
               deps=[t_ag1], slow=True)
    t_g2 = dma("gpsimd", "cryo", cgee[:, :], ag1_out.ap()[:, 16:32].rearrange("r n -> n r"),
               deps=[t_ag1], slow=True)
    t_cs = V(lambda e: e.tensor_tensor_scan(csc[:, :], cgat[:, :], cgee[:, :],
                                            0.0, Alu.mult, Alu.add),
             deps=[t_g1, t_g2], pub=True, tiny=True)
    t_sel = V(lambda e: e.tensor_tensor(csc[:, :], csc[:, :], onehot_s[:, :],
                                        Alu.mult), deps=[t_cs, l_oh], pub=True, tiny=True)
    t_sin = V(lambda e: e.tensor_reduce(sin_s[:, 0:1], csc[:, :], AX, Alu.add),
              deps=[t_sel], pub=True, tiny=True)
    t_y = V(lambda e: e.tensor_tensor_scan(y_bf[0:16, :], scanB[0:16, :],
                                           scanC[:, :], sin_s[0:16, 0:1],
                                           Alu.mult, Alu.add),
            deps=[t_sin], pub=True)

    dbg_cap("a_", scanB[0:16, 0:512], [t_a], 512, parts=16)
    dbg_cap("b_", scanC[0:16, 0:512], [t_b, t_y0], 512, parts=16)
    dbg_cap("sin", sin_s[:, 0:1], [t_sin], 1, parts=16)
    dbg_cap("y", y_bf[0:16, 0:512], [t_y], 512, parts=16)
    t_gaob2 = V(lambda e: e.tensor_tensor(gaob[:, :], adp[:, 16:24],
                                          outb_s[:, :], Alu.mult),
                deps=[t_gaq, l_outb], pub=True, tiny=True)
    res_toks = []
    ctile_free = [None] * 4
    for n in range(4):
        for mo in range(8):
            bk = (n * 8 + mo) % 4
            mmt = TT(lambda e, bk=bk, mo=mo, n=n: e.matmul(
                ps_mm[bk][:, 0:512], outw_bf[0:16, 128 * mo:128 * mo + 128],
                y_bf[0:16, 512 * n:512 * n + 512], start=True, stop=True),
                deps=[l_outw, t_y, bank_free[bk]], pub=True)
            sl = (n * 8 + mo) % 4
            ev = S(lambda e, bk=bk, mo=mo, sl=sl: e.activation(
                ctile[sl][:, :], ps_mm[bk][:, 0:512], Act.Identity,
                scale=adp[:, 16 + mo:17 + mo], bias=gaob[:, mo:mo + 1]),
                deps=[mmt, t_gaob2, ctile_free[sl], t_c2, t_g1, t_g2],
                pub=True, tiny=True)
            bank_free[bk] = ev
            for q in range(4):
                si = tr_ctr[0] % 8
                tr_ctr[0] += 1
                t_tr = TT(lambda e, sl=sl, q=q, si=si: e.transpose(
                    tr_slot(si), ctile[sl][:, 128 * q:128 * q + 128], id_b[:, :]),
                    deps=[ev, tr_guard[si % 2]], pub=True)
                ra = V(lambda e, n=n, q=q, mo=mo, si=si: e.tensor_tensor(
                    x_tm[4 * n + q][:, 128 * mo:128 * mo + 128],
                    x_tm[4 * n + q][:, 128 * mo:128 * mo + 128],
                    tr_slot(si), Alu.add),
                    deps=[t_tr, x_toks[4 * n + q]], pub=True, tiny=True)
                tr_guard[si % 2] = ra
                res_toks.append(ra)
                ctile_free[sl] = t_tr

    dbg_cap("x2_0", x_tm[0][:, 0:512], res_toks, 512)
    h2_toks = []
    xsp_toks = []
    for i in range(16):
        evs = ln_tile(100 + i, x_tm[i], None, 32, 24,
                      lambda j, i=i: hx[j][:, 128 * i:128 * i + 128], id_b,
                      extra_deps=res_toks)
        h2_toks.extend(evs)
        xsp_toks.append(dma("sync", f"xsp{i % 4}", xspill[i], x_tm[i][:, :],
                            deps=evs))
    mx_h2 = []
    for j in range(8):
        mx_h2.append(V(lambda e, j=j: e.tensor_reduce(
            mxc[:, j:j + 1], hx[j][:, :], AX, Alu.max,
            apply_absolute_value=True), deps=h2_toks, pub=True, tiny=True))

    def global_max(src_col_range, dst_slot, deps):
        a, b = src_col_range
        t1 = V(lambda e: e.tensor_reduce(mxc[:, 126:127], mxc[:, a:b], AX,
                                         Alu.max), deps=deps, pub=True, tiny=True)
        return emit("gpsimd", lambda e, d=dst_slot: e.tensor_reduce(
            scl[0:1, d:d + 1], mxc[:, 126:127], mybir.AxisListType.C, Alu.max),
            deps=[t1], pub=True, tiny=True)

    def allreduce_pair(src_slot, deps):
        t1 = dma("gpsimd", "arb", ar_in.ap(), scl[0:1, src_slot:src_slot + 2],
                 deps=deps)
        arid = ar_ctr[0]
        ar_ctr[0] += 1
        t2 = emit("gpsimd", lambda e: e.collective_compute(
            "AllReduce", Alu.max, replica_groups=[list(range(NCORES))],
            ins=[ar_in.ap().opt()], outs=[ar_out.ap().opt()]),
            deps=[t1], cc=f"ccr{arid}")
        return dma("gpsimd", "arb", scl[0:1, src_slot:src_slot + 2], ar_out.ap(),
                   deps=[t2])

    ar_ctr = [0]
    bc_guard = [None]

    def bcast_scl(deps):
        tm = TT(lambda e: e.matmul(ps_b[:, 0:32], ones1[:, :], scl[0:1, :],
                                   start=True, stop=True),
                deps=list(deps) + [l_ones1, bc_guard[0], sum_toks[-1] if sum_toks else None],
                pub=True)
        tv = V(lambda e: e.tensor_copy(sclb[:, :], ps_b[:, 0:32]),
               deps=[tm], pub=True, tiny=True)
        bc_guard[0] = tv
        return tv

    dbg_cap("h2_0", hx[0][:, 0:512], h2_toks, 512)
    t_mx1 = global_max((0, 8), 0, mx_h2 + [t_scl0])
    t_ar1 = allreduce_pair(0, [t_mx1])
    t_s1 = V(lambda e: e.tensor_scalar(scl[0:1, 1:2], scl[0:1, 0:1], INV_QMAX,
                                       1e-12, Alu.mult, Alu.max),
             deps=[t_ar1], pub=True, tiny=True)
    t_i1 = V(lambda e: e.reciprocal(scl[0:1, 2:3], scl[0:1, 1:2]),
             deps=[t_s1], pub=True, tiny=True)
    t_bc1 = bcast_scl([t_i1])

    dbg_cap("scl1", scl[:, :], [t_i1, t_bc1], 32, parts=1)
    dbg_cap("sclb1", sclb[:, :], [t_bc1], 32)
    xq_toks = [None] * 32
    for j in range(8):
        for qq in range(4):
            rs = rtring[qq]
            c0 = 512 * qq
            t_r = V(lambda e, j=j, c0=c0, rs=rs: e.tensor_scalar(
                rs[:, :], hx[j][:, c0:c0 + 512], sclb[:, 2:3], MAGIC,
                Alu.mult, Alu.add),
                deps=[t_bc1, xq_toks[4 * j + qq - 4] if 4 * j + qq >= 4 else None]
                + h2_toks, pub=True)
            xq_toks[4 * j + qq] = V(lambda e, j=j, c0=c0, rs=rs: e.tensor_scalar(
                hx[j][:, c0:c0 + 512], rs[:, :], MAGIC, None, Alu.subtract),
                deps=[t_r], pub=True)

    scan_dead = [t_y, t_c1, t_c2, hprime[-1], xp_toks[-1], sum_toks[-1]]

    def ffn_col_load(j, pb, deps):
        r, eo = col_addr(j)
        I = D if j < 48 else FF
        return dma("sync", f"wff{pb}", wffcol[pb][:, 0:I],
                   agw_out.ap()[r:r + 1, eo:eo + 128 * I].rearrange(
                       "one (p q) -> (one p) q", p=128),
                   deps=deps + [t_agw] + scan_dead)

    wff_tok = [None, None]
    wff_free = [None, None]
    dbg_cap("xq_0", hx[0][:, 0:512], xq_toks, 512)
    gate_ev = []
    for m in range(24):
        pb = m % 2
        wff_tok[pb] = ffn_col_load(m, pb, [wff_free[pb]])
        for n in range(4):
            bk = (m * 4 + n) % 4
            mmt = None
            for k in range(8):
                mmt = TT(lambda e, pb=pb, bk=bk, k=k, n=n: e.matmul(
                    ps_mm[bk][:, :], wffcol[pb][:, 128 * k:128 * k + 128],
                    hx[k][:, 512 * n:512 * n + 512],
                    start=(k == 0), stop=(k == 7)),
                    deps=[wff_tok[pb], bank_free[bk], xq_toks[4 * k + n]],
                    pub=(k == 7))
            ev = S(lambda e, m=m, n=n, bk=bk: e.activation(
                gP(m)[:, 512 * n:512 * n + 512], ps_mm[bk][:, :], Act.Copy,
                scale=sclb[:, 1:2]),
                deps=[mmt, t_bc1] + ([xsp_toks[m]] if m < 16 else []), pub=True,
                tiny=True)
            bank_free[bk] = ev
            gate_ev.append(ev)
            gate_ev.append(V(lambda e, m=m, n=n: e.tensor_reduce(
                mxc[:, 4 * m + n:4 * m + n + 1], gP(m)[:, 512 * n:512 * n + 512],
                AX, Alu.max, apply_absolute_value=True), deps=[ev], pub=True, tiny=True))
        wff_free[pb] = gate_ev[-2]
    dbg_cap("gate_0", gP(0)[:, 0:512], gate_ev, 512)
    t_mxg = global_max((0, 96), 3, gate_ev)
    t_ar2 = allreduce_pair(3, [t_mxg])
    t_sg = V(lambda e: e.tensor_scalar(scl[0:1, 4:5], scl[0:1, 3:4], INV_QMAX,
                                       1e-12, Alu.mult, Alu.max),
             deps=[t_ar2], pub=True, tiny=True)
    t_ig = V(lambda e: e.reciprocal(scl[0:1, 5:6], scl[0:1, 4:5]),
             deps=[t_sg], pub=True, tiny=True)
    t_bc2 = bcast_scl([t_ig])

    up_ev = []
    up_spill_toks = {}
    upr_free = [None] * 8
    up_last_mxe = [None] * 8
    w1_toks = []
    w1_rt_guard = [None] * 4

    def window1_chunk(m):
        for hh in range(4):
            c0 = 512 * hh
            rs = rtring[hh]
            xh = xhat[hh % 2][:, 512 * (hh // 2):512 * (hh // 2) + 512]
            t1 = S(lambda e, m=m, c0=c0, rs=rs: e.activation(
                rs[:, :], gP(m)[:, c0:c0 + 512], Act.Copy, bias=MAGIC,
                scale=sclb[:, 5:6]),
                deps=[t_bc2, w1_rt_guard[hh]] + gate_ev, pub=True)
            t2 = V(lambda e, m=m, c0=c0, rs=rs: e.tensor_scalar(
                gP(m)[:, c0:c0 + 512], rs[:, :], MAGIC, None, Alu.subtract),
                deps=[t1], pub=True)
            t3 = S(lambda e, m=m, c0=c0, xh=xh: e.activation(
                xh[:, :], gP(m)[:, c0:c0 + 512], Act.Sigmoid,
                scale=sclb[:, 4:5]), deps=[t2], pub=True)
            t4 = S(lambda e, xh=xh, rs=rs: e.activation(
                rs[:, :], xh[:, :], Act.Copy, bias=MAGIC, scale=65536.0),
                deps=[t3], pub=True)
            t5 = V(lambda e, rs=rs: e.tensor_scalar(
                rs[:, :], rs[:, :], MAGIC, 65535.0, Alu.subtract, Alu.min),
                deps=[t4], pub=True)
            t6 = V(lambda e, m=m, c0=c0, rs=rs: e.tensor_tensor(
                gP(m)[:, c0:c0 + 512], rs[:, :], gP(m)[:, c0:c0 + 512],
                Alu.mult), deps=[t5], pub=True)
            w1_rt_guard[hh] = t6
            w1_toks.append(V(lambda e, m=m, c0=c0, hh=hh: e.tensor_reduce(
                mxc2[:, (4 * m + hh):(4 * m + hh) + 1],
                gP(m)[:, c0:c0 + 512], AX, Alu.max,
                apply_absolute_value=True), deps=[t6], pub=True, tiny=True))

    for m in range(24):
        pb = m % 2
        wff_tok[pb] = ffn_col_load(24 + m, pb, [wff_free[pb]])
        for n in range(4):
            bk = (m * 4 + n) % 4
            sl = (m * 4 + n) % 8
            mmt = None
            for k in range(8):
                mmt = TT(lambda e, pb=pb, bk=bk, k=k, n=n: e.matmul(
                    ps_mm[bk][:, :], wffcol[pb][:, 128 * k:128 * k + 128],
                    hx[k][:, 512 * n:512 * n + 512],
                    start=(k == 0), stop=(k == 7)),
                    deps=[wff_tok[pb], bank_free[bk]], pub=(k == 7))
            ev = S(lambda e, sl=sl, bk=bk: e.activation(
                uprng[sl][:, :], ps_mm[bk][:, :], Act.Copy, scale=sclb[:, 1:2]),
                deps=[mmt, upr_free[sl], t_bc2], pub=True, tiny=True)
            bank_free[bk] = ev
            mxe = V(lambda e, m=m, n=n, sl=sl: e.tensor_reduce(
                mxc[:, 4 * m + n:4 * m + n + 1], uprng[sl][:, :], AX, Alu.max,
                apply_absolute_value=True), deps=[ev], pub=True, tiny=True)
            up_ev.append(mxe)
            up_last_mxe[sl] = mxe
            sp = dma("sync", f"upr{sl}", upspill.ap()[m, n], uprng[sl][:, :],
                     deps=[ev])
            up_spill_toks[(m, n)] = sp
            upr_free[sl] = sp
        window1_chunk(m)

    dbg_cap("scl2", scl[:, :], [t_bc2], 32, parts=1)
    dbg_cap("P_0", gP(0)[:, 0:512], w1_toks, 512)
    t_mxu = global_max((0, 96), 6, up_ev)
    t_mxP1 = V(lambda e: e.tensor_reduce(mxc2[:, 95:96]
                                         if False else mxc[:, 125:126],
                                         mxc2[:, 0:96], AX, Alu.max),
               deps=w1_toks, pub=True, tiny=True)
    t_mxP = emit("gpsimd", lambda e: e.tensor_reduce(
        scl[0:1, 7:8], mxc[:, 125:126], mybir.AxisListType.C, Alu.max),
        deps=[t_mxP1, t_mxu], pub=True, tiny=True)
    t_ar3 = allreduce_pair(6, [t_mxu, t_mxP])
    t_sup = V(lambda e: e.tensor_scalar(scl[0:1, 8:9], scl[0:1, 6:7], INV_QMAX,
                                        1e-12, Alu.mult, Alu.max),
              deps=[t_ar3], pub=True, tiny=True)
    t_iup = V(lambda e: e.reciprocal(scl[0:1, 9:10], scl[0:1, 8:9]),
              deps=[t_sup], pub=True, tiny=True)
    t_cg = V(lambda e: e.tensor_scalar_mul(scl[0:1, 12:13], scl[0:1, 4:5],
                                           1.0 / 65536.0), deps=[t_sg], pub=True, tiny=True)
    t_mg2 = V(lambda e: e.tensor_tensor(scl[0:1, 20:21], scl[0:1, 7:8],
                                        scl[0:1, 12:13], Alu.mult),
              deps=[t_ar3, t_cg], pub=True, tiny=True)
    t_s4 = V(lambda e: e.tensor_scalar(scl[0:1, 10:11], scl[0:1, 20:21],
                                       INV_QMAX, 1e-12, Alu.mult, Alu.max),
             deps=[t_mg2], pub=True, tiny=True)
    t_i4 = V(lambda e: e.reciprocal(scl[0:1, 11:12], scl[0:1, 10:11]),
             deps=[t_s4], pub=True, tiny=True)
    t_c4 = V(lambda e: e.tensor_tensor(scl[0:1, 13:14], scl[0:1, 12:13],
                                       scl[0:1, 11:12], Alu.mult),
             deps=[t_i4], pub=True, tiny=True)
    t_s4u = V(lambda e: e.tensor_tensor(scl[0:1, 14:15], scl[0:1, 10:11],
                                        scl[0:1, 8:9], Alu.mult),
              deps=[t_s4, t_sup], pub=True, tiny=True)
    t_bc3 = bcast_scl([t_iup, t_c4, t_s4u])

    dbg_cap("scl3", scl[:, :], [t_bc3, t_s4u, t_c4], 32, parts=1)
    o_toks = []
    upr_free2 = list(up_last_mxe)
    w2_gu = [None] * 2
    w2_gg = [None] * 2
    for m in range(24):
        for n in range(4):
            i = m * 4 + n
            sl = i % 8
            ru = rtring[i % 2]
            rg = rtring[2 + i % 2]
            rd = dma("sync", f"upr{sl}", uprng[sl][:, :], upspill.ap()[m, n],
                     deps=[up_spill_toks[(m, n)], upr_free2[sl]])
            t1 = S(lambda e, sl=sl, ru=ru: e.activation(
                ru[:, :], uprng[sl][:, :], Act.Copy, bias=MAGIC,
                scale=sclb[:, 9:10]),
                deps=[rd, t_bc3, w2_gu[i % 2]], pub=True)
            t2 = V(lambda e, sl=sl, ru=ru: e.tensor_scalar(
                uprng[sl][:, :], ru[:, :], MAGIC, None, Alu.subtract),
                deps=[t1], pub=True)
            w2_gu[i % 2] = t2
            t3 = S(lambda e, m=m, n=n, rg=rg: e.activation(
                rg[:, :], gP(m)[:, 512 * n:512 * n + 512], Act.Copy,
                bias=MAGIC, scale=sclb[:, 13:14]),
                deps=[t_bc3, w2_gg[i % 2]] + w1_toks, pub=True)
            t4 = V(lambda e, m=m, n=n, sl=sl, rg=rg: e.scalar_tensor_tensor(
                gP(m)[:, 512 * n:512 * n + 512], rg[:, :], MAGIC,
                uprng[sl][:, :], Alu.subtract, Alu.mult),
                deps=[t2, t3], pub=True)
            w2_gg[i % 2] = t4
            upr_free2[sl] = t4
            o_toks.append(V(lambda e, m=m, n=n: e.tensor_reduce(
                mxc[:, 4 * m + n:4 * m + n + 1], gP(m)[:, 512 * n:512 * n + 512],
                AX, Alu.max, apply_absolute_value=True), deps=[t4], pub=True, tiny=True))
    dbg_cap("o_0", gP(0)[:, 0:512], o_toks, 512)
    t_mxo = global_max((0, 96), 15, o_toks)
    t_ar4 = allreduce_pair(15, [t_mxo])
    t_mo = V(lambda e: e.tensor_tensor(scl[0:1, 16:17], scl[0:1, 15:16],
                                       scl[0:1, 14:15], Alu.mult),
             deps=[t_ar4, t_s4u], pub=True, tiny=True)
    t_s5 = V(lambda e: e.tensor_scalar(scl[0:1, 17:18], scl[0:1, 16:17],
                                       INV_QMAX, 1e-12, Alu.mult, Alu.max),
             deps=[t_mo], pub=True, tiny=True)
    t_i5 = V(lambda e: e.reciprocal(scl[0:1, 18:19], scl[0:1, 17:18]),
             deps=[t_s5], pub=True, tiny=True)
    t_t5 = V(lambda e: e.tensor_tensor(scl[0:1, 19:20], scl[0:1, 14:15],
                                       scl[0:1, 18:19], Alu.mult),
             deps=[t_i5], pub=True, tiny=True)
    t_bc4 = bcast_scl([t_t5])
    t_gb5 = V(lambda e: e.tensor_scalar(adp[:, 48:56], adp[:, 40:48],
                                        sclb[:, 17:18], None, Alu.mult),
              deps=[t_bc4, t_gbq], pub=True, tiny=True)

    dbg_cap("scl4", scl[:, :], [t_bc4, t_gb5], 32, parts=1)
    dbg_cap("gb5", adp[:, 48:56], [t_gb5], 8)
    oq_toks = {}
    w3_rt_guard = [None] * 4
    for m in range(24):
        for n in range(4):
            hh = (m * 4 + n) % 4
            rs = rtring[hh]
            t1 = S(lambda e, m=m, n=n, rs=rs: e.activation(
                rs[:, 0:512], gP(m)[:, 512 * n:512 * n + 512], Act.Copy,
                bias=MAGIC, scale=sclb[:, 19:20]),
                deps=[t_bc4, w3_rt_guard[hh]] + o_toks, pub=True)
            oq_toks[(m, n)] = V(lambda e, m=m, n=n, rs=rs: e.tensor_scalar(
                gP(m)[:, 512 * n:512 * n + 512], rs[:, 0:512], MAGIC, None,
                Alu.subtract), deps=[t1], pub=True)
            w3_rt_guard[hh] = oq_toks[(m, n)]

    dbg_cap("oq_0", gP(0)[:, 0:512], [oq_toks[(0, 0)], oq_toks[(0, 1)]], 512)
    st_toks = []
    xr_toks = [None] * 8
    up_mm_done = up_ev[-1]
    for n in range(4):
        for q in range(4):
            s8 = 4 * (n % 2) + q
            xr_toks[s8] = dma("sync", f"xrr{s8}", xrr[s8][:, :],
                              xspill.ap()[4 * n + q],
                              deps=xsp_toks + st_toks[-8:-4] + [xq_toks[-1],
                                                               up_mm_done])
        for mo in range(8):
            pb = mo % 2
            wff_tok[pb] = ffn_col_load(48 + mo, pb, [wff_free[pb]])
            bk = mo % 4
            mmt = None
            for k in range(24):
                mmt = TT(lambda e, pb=pb, bk=bk, k=k, n=n: e.matmul(
                    ps_mm[bk][:, :], wffcol[pb][:, 128 * k:128 * k + 128],
                    gP(k)[:, 512 * n:512 * n + 512],
                    start=(k == 0), stop=(k == 23)),
                    deps=[wff_tok[pb], bank_free[bk], oq_toks[(k, n)]],
                    pub=(k == 23))
            sl = mo % 4
            ev = S(lambda e, bk=bk, mo=mo, sl=sl: e.activation(
                ctile[sl][:, :], ps_mm[bk][:, :], Act.Copy,
                scale=adp[:, 48 + mo:49 + mo]),
                deps=[mmt, t_gb5, ctile_free[sl]], pub=True, tiny=True)
            bank_free[bk] = ev
            wff_free[pb] = mmt
            for q in range(4):
                si = tr_ctr[0] % 8
                tr_ctr[0] += 1
                t_tr = TT(lambda e, sl=sl, q=q, si=si: e.transpose(
                    tr_slot(si), ctile[sl][:, 128 * q:128 * q + 128], id_b[:, :]),
                    deps=[ev, tr_guard[si % 2]], pub=True)
                s8 = 4 * (n % 2) + q
                ra = V(lambda e, s8=s8, mo=mo, si=si: e.tensor_tensor(
                    xrr[s8][:, 128 * mo:128 * mo + 128],
                    xrr[s8][:, 128 * mo:128 * mo + 128], tr_slot(si), Alu.add),
                    deps=[t_tr, xr_toks[s8]], pub=True)
                tr_guard[si % 2] = ra
                xr_toks[s8] = ra
                ctile_free[sl] = t_tr
        for q in range(4):
            s8 = 4 * (n % 2) + q
            st_toks.append(dma("sync", f"ost{q}",
                               out_h[512 * n + 128 * q:512 * n + 128 * q + 128, :],
                               xrr[s8][:, :], deps=[xr_toks[s8]]))

    wait_list = st_toks[-4:]
    if dbg:
        wait_list = wait_list + [last_dma.get("dbgk")]
    emit("sync", lambda e: None, deps=wait_list)

    build_program.n_ops = len(ops)
    build_program.ops_dbg = [(o["eng"], o["fn"].__code__.co_firstlineno,
                              o["waits"], o["incs"]) for o in ops]
    if trunc is not None:
        ops[:] = ops[:trunc]
    with nc.Block() as block:
        def mk(eng_name):
            def run(e):
                for op in ops:
                    if op["eng"] != eng_name:
                        continue
                    for s, v in op["waits"]:
                        e.wait_ge(s, v)
                    ins = op["fn"](e)
                    for s, nn in op["incs"]:
                        assert ins is not None, "inc on op returning no instruction"
                        ins = ins.then_inc(s, nn)
            return run

        block.sync(mk("sync"))
        block.gpsimd(mk("gpsimd"))
        block.vector(mk("vector"))
        block.scalar(mk("scalar"))
        block.tensor(mk("tensor"))
    return nc


_NC_CACHE = None


def _get_nc():
    global _NC_CACHE
    if _NC_CACHE is None:
        _NC_CACHE = build_program(debug_sync=DEBUG_SYNC)
    return _NC_CACHE


def make_inmaps(L0_attn_x, L0_attn_in_w, L0_attn_in_b, L0_attn_conv_w,
                L0_attn_conv_b, L0_attn_xp_w, L0_attn_xp_b, L0_attn_out_w,
                L0_attn_out_b, L0_ffn_w_gate, L0_ffn_w_up, L0_ffn_w_down,
                L0_c, L0_wmod):
    f = np.float32
    x = np.ascontiguousarray(L0_attn_x, dtype=f).reshape(16384, D)
    in_w = np.ascontiguousarray(L0_attn_in_w, dtype=f)
    winT = np.ascontiguousarray(
        in_w.reshape(16, 128, 8, 128).transpose(0, 3, 2, 1))
    inb = np.ascontiguousarray(
        np.asarray(L0_attn_in_b, dtype=f).reshape(16, 128).T)
    convw = np.ascontiguousarray(
        np.asarray(L0_attn_conv_w, dtype=f).reshape(16, 128, 4)
        .transpose(1, 0, 2).reshape(128, 64))
    convb = np.ascontiguousarray(
        np.asarray(L0_attn_conv_b, dtype=f).reshape(16, 128).T)
    xpwT = np.ascontiguousarray(
        np.ascontiguousarray(L0_attn_xp_w, dtype=f).T.reshape(16, 128, 32))
    xpba = np.ascontiguousarray(np.asarray(L0_attn_xp_b, dtype=f)[:16].reshape(16, 1))
    xpbb = np.ascontiguousarray(np.asarray(L0_attn_xp_b, dtype=f)[16:].reshape(16, 1))
    outwT = np.ascontiguousarray(
        np.asarray(L0_attn_out_w, dtype=f).T.reshape(16, 8, 128).transpose(1, 0, 2))
    outb = np.ascontiguousarray(
        np.asarray(L0_attn_out_b, dtype=f).reshape(8, 128).T)
    wg = np.ascontiguousarray(L0_ffn_w_gate, dtype=f)
    wu = np.ascontiguousarray(L0_ffn_w_up, dtype=f)
    wd = np.ascontiguousarray(L0_ffn_w_down, dtype=f)
    cvec = np.ascontiguousarray(np.asarray(L0_c, dtype=f).reshape(8, 128).T)
    wmod = np.ascontiguousarray(L0_wmod, dtype=f)

    identf = np.eye(128, dtype=f)
    identb = np.eye(128, dtype=ml_dtypes.bfloat16)
    egu = np.zeros((32, 8 * 128), dtype=f)
    for kc in range(8):
        for kp in range(128):
            egu[4 * kc + kp // 32, 128 * kc + kp] = 1.0
    ed = np.zeros((96, 24 * 128), dtype=f)
    for kc in range(24):
        for kp in range(128):
            ed[4 * kc + kp // 32, 128 * kc + kp] = 1.0

    in_maps = []
    for r in range(NCORES):
        xs = np.empty((TH, D), dtype=f)
        t0 = r * T
        if r == 0:
            xs[:HALO] = 0.0
        else:
            xs[:HALO] = x[t0 - HALO:t0]
        xs[HALO:] = x[t0:t0 + T]
        rows_gu = np.empty((6, 128, D), dtype=f)
        lhsT_gu = np.empty((6, 128, 8, 128), dtype=f)
        for s in range(6):
            j = r + 8 * s
            if j < 24:
                W, m = wg, j
            else:
                W, m = wu, j - 24
            blk = W[128 * m:128 * m + 128, :]
            rows_gu[s] = blk
            lhsT_gu[s] = blk.reshape(128, 8, 128).transpose(2, 1, 0)
        md = (r + 48) - 48
        blkd = wd[128 * md:128 * md + 128, :]
        rows_d = np.ascontiguousarray(blkd)
        lhsT_d = np.ascontiguousarray(blkd.reshape(128, 24, 128).transpose(2, 1, 0))
        wms = wmod[:, 768 * r:768 * r + 768]
        wmod_t = np.ascontiguousarray(
            wms.reshape(8, 128, 6, 128).transpose(2, 0, 1, 3))
        onehot = np.zeros((NS, 8), dtype=f)
        if r > 0:
            onehot[:, r - 1] = 1.0
        hmask = np.full((128, 1), 0.0 if r == 0 else 1.0, dtype=f)
        in_maps.append({
            "x": np.ascontiguousarray(xs),
            "winT": winT, "inb": inb, "convw": convw, "convb": convb,
            "xpwT": xpwT, "xpba": xpba, "xpbb": xpbb, "outwT": outwT, "outb": outb,
            "lpq_rows_gu": np.ascontiguousarray(rows_gu),
            "lpq_lhsT_gu": np.ascontiguousarray(lhsT_gu),
            "lpq_rows_d": rows_d, "lpq_lhsT_d": lhsT_d,
            "cvec": cvec, "wmod": wmod_t,
            "identf": identf, "identb": identb, "egu": egu, "ed": ed,
            "onehot": onehot, "hmask": hmask,
            "ones1": np.ones((1, 128), dtype=f),
        })

    return in_maps


def kernel(**inputs):
    in_maps = make_inmaps(**inputs)
    nc = _get_nc()
    res = run_bass_kernel_spmd(nc, in_maps, core_ids=list(range(NCORES)))
    out = np.concatenate([np.asarray(r["out"], dtype=np.float32)
                          for r in res.results], axis=0)
    return out.reshape(1, 16384, D)


# version 09
# speedup vs baseline: 1.1141x; 1.0261x over previous
import numpy as np
import ml_dtypes
import concourse.bass as bass
import concourse.mybir as mybir
from concourse.bass_utils import run_bass_kernel_spmd

F32 = mybir.dt.float32
BF16 = mybir.dt.bfloat16
Alu = mybir.AluOpType
Act = mybir.ActivationFunctionType
AX = mybir.AxisListType.X
MAGIC = float(np.float32(1.5 * 2.0 ** 23))
INV_QMAX = float(np.float32(1.0 / 32767.0))

NCORES = 8
T = 2048
HALO = 128
TH = T + HALO
D = 1024
DI = 2048
FF = 3072
NS = 16
EPS = 1e-5

SLOT_OFF = [0, 262144, 524288, 786432, 1048576, 1310720, 1572864]
RANK_BYTES = 2359296


def col_addr(j):
    return j % NCORES, SLOT_OFF[j // NCORES] // 2


DEBUG_SYNC = False


def build_program(debug_sync=False, trunc=None, dbg=False):
    nc = bass.Bass("TRN2", target_bir_lowering=False,
                   detect_race_conditions=debug_sync)

    def din(name, shape, dt=F32):
        return nc.dram_tensor(name, shape, dt, kind="ExternalInput").ap()

    x_h = din("x", [TH, D])
    winT_h = din("winT", [16, 128, 8, 128])
    inb_h = din("inb", [128, 16])
    convw_h = din("convw", [128, 64])
    convb_h = din("convb", [128, 16])
    xpwT_h = din("xpwT", [16, 128, 32])
    xpba_h = din("xpba", [16, 1])
    xpbb_h = din("xpbb", [16, 1])
    outwT_h = din("outwT", [8, 16, 128])
    outb_h = din("outb", [128, 8])
    lpq_rows_gu_h = din("lpq_rows_gu", [6, 128, D])
    lpq_lhsT_gu_h = din("lpq_lhsT_gu", [6, 128, 8, 128])
    lpq_rows_d_h = din("lpq_rows_d", [128, FF])
    lpq_lhsT_d_h = din("lpq_lhsT_d", [128, 24, 128])
    c_h = din("cvec", [128, 8])
    wmod_h = din("wmod", [6, 8, 128, 128])
    ident_f_h = din("identf", [128, 128])
    ident_b_h = din("identb", [128, 128], BF16)
    egu_h = din("egu", [32, 8 * 128])
    ed_h = din("ed", [96, 24 * 128])
    onehot_h = din("onehot", [NS, 8])
    hmask_h = din("hmask", [128, 1])
    ones1_h = din("ones1", [1, 128])

    out_h = nc.dram_tensor("out", [T, D], F32, kind="ExternalOutput").ap()
    dbg_h = nc.dram_tensor("dbg", [128, 16384], F32,
                           kind="ExternalOutput").ap() if dbg else None

    ag0_in = nc.dram_tensor("ag0_in", [1, 768], F32)
    ag0_out = nc.dram_tensor("ag0_out", [NCORES, 768], F32)
    agw_in = nc.dram_tensor("agw_in", [1, RANK_BYTES // 2], BF16)
    agw_out = nc.dram_tensor("agw_out", [NCORES, RANK_BYTES // 2], BF16)
    ag1_in = nc.dram_tensor("ag1_in", [1, 32], F32)
    ag1_out = nc.dram_tensor("ag1_out", [NCORES, 32], F32)
    ar_in = nc.dram_tensor("ar_in", [1, 2], F32)
    ar_out = nc.dram_tensor("ar_out", [1, 2], F32)
    sumd = nc.dram_tensor("sumd", [1, T], F32)
    mxd = nc.dram_tensor("mxd", [128, 1], F32)
    scld = nc.dram_tensor("scld", [1, 32], F32)
    xspill = nc.dram_tensor("xspill", [16, 128, D], F32)
    upspill = nc.dram_tensor("upspill", [24, 4, 128, 512], BF16)

    SBUF_SZ = nc.SBUF_PARTITION_SIZE_BYTES
    AR_S = SBUF_SZ - 34816 - 32
    AR_H = AR_S - 69632
    AR_X = AR_H - 65536

    def at(name, shape, dt, off):
        return nc.alloc_sbuf_tensor_at(name, shape, dt, offset=off)

    x_tm = [at(f"x{i}", [128, D], F32, AR_X + 4096 * i) for i in range(16)]
    gP_lo = [at(f"gPl{m}", [128, T], BF16, AR_X + 4096 * m) for m in range(16)]
    wrow = at("wrow", [128, FF], F32, AR_H + 0)
    wlhs = at("wlhs", [128, FF], F32, AR_H + 12288)
    wqcol = at("wqcol", [128, FF], BF16, AR_H + 24576)
    wmodb = at("wmodb", [128, 6 * 1024], F32, AR_H + 30720)
    ed_s = at("ed_s", [96, 24 * 128], F32, AR_H + 55296)
    smax = at("smax", [128, 96], F32, AR_H + 67584)
    sval = at("sval", [128, 96], F32, AR_H + 67968)
    sinv = at("sinv", [128, 96], F32, AR_H + 68352)
    h_bf = [at(f"h{m}", [128, TH], BF16, AR_H + 4352 * m) for m in range(16)]
    gP_hi = [at(f"gPh{m}", [128, T], BF16, AR_H + 4096 * m) for m in range(8)]
    hx = [at(f"hx{j}", [128, T], BF16, AR_H + 32768 + 4096 * j) for j in range(8)]
    xrr = [at(f"xrr{q}", [128, D], F32, AR_H + 32768 + 4096 * q) for q in range(8)]
    h1_bf = [at(f"h1_{j}", [128, TH], BF16, AR_S + 4352 * j) for j in range(8)]
    scanA = at("scanA", [32, T], F32, AR_S + 0)
    scanB = at("scanB", [16, T], F32, AR_S + 8192)
    scanC = at("scanC", [16, T], F32, AR_S + 16384)
    scanD = at("scanD", [16, T], F32, AR_S + 24576)
    y_bf = at("ybf", [16, T], BF16, AR_S + 24576)
    wffcol = [at(f"wff{p}", [128, FF], BF16, AR_S + 6144 * p) for p in range(2)]
    uprng = [at(f"upr{p}", [128, 512], BF16, AR_S + 12288 + 1024 * p) for p in range(8)]
    rtring = [at(f"rt{p}", [128, 512], F32, AR_S + 20480 + 2048 * p) for p in range(4)]
    ctile = [at(f"ct{p}", [128, 512], BF16, AR_S + 28672 + 1024 * p) for p in range(4)]

    def gP(m):
        return gP_lo[m] if m < 16 else gP_hi[m - 16]

    A = nc.alloc_sbuf_tensor
    win_bf = [A(f"winb{p}", [128, 1024], BF16) for p in range(2)]
    hc = A("hc", [128, T], F32)
    sq1 = A("sq1", [128, D], F32)
    xstage = A("xstage", [128, D], F32)
    xhat = [A(f"xhat{p}", [128, D], BF16) for p in range(2)]

    convw = A("convw_s", [128, 64], F32)
    convb = A("convb_s", [128, 16], F32)
    xpw_bf = A("xpwb", [128, 512], BF16)
    xpba_s = A("xpba_s", [16, 1], F32)
    xpbb_s = A("xpbb_s", [16, 1], F32)
    outw_bf = A("outwb", [16, 1024], BF16)
    outb_s = A("outb_s", [128, 8], F32)
    gaob = A("gaob", [128, 8], F32)
    ones1 = A("ones1_s", [1, 128], F32)
    adp = A("adp", [128, 64], F32)
    silc = A("silc", [128, 8], F32)
    inb_s = A("inb_s", [128, 16], F32)
    id_f = A("id_f", [128, 128], F32)
    id_b = A("id_b", [128, 128], BF16)
    egu_s = A("egu_s", [32, 8 * 128], F32)
    sT_s = A("sT_s", [96, 128], F32)
    invT_s = A("invT_s", [96, 128], F32)
    ones_c = A("ones_c", [128, 1], BF16)
    mxc = A("mxc", [128, 128], F32)
    mxc2 = A("mxc2", [128, 96], F32)
    sumstg = A("sumstg", [1, 512], F32)
    stats = A("stats", [128, 32], F32)
    gqs = A("gqs", [128, 8], F32)
    sq1b = A("sq1b", [128, D], F32)
    scl = A("scl", [1, 32], F32)
    sclb = A("sclb", [128, 32], F32)
    onehot_s = A("onehot_s", [NS, 8], F32)
    sin_s = A("sin_s", [NS, 1], F32)
    cgat = A("cgat", [NS, 8], F32)
    cgee = A("cgee", [NS, 8], F32)
    csc = A("csc", [NS, 8], F32)
    hmask_s = A("hmask_s", [128, 1], F32)
    modloc = A("modloc", [128, 6], F32)



    canary = A("canary", [1, 8], F32)
    can_addr = nc.lookup_mloc(canary).addr
    assert can_addr + 32 <= AR_X, (
        f"auto SBUF allocations overflow into manual arena: {can_addr} > {AR_X}")
    ps_mm = [nc.alloc_psum_tensor(f"psmm{i}", [128, 512], F32) for i in range(4)]
    ps_tr = nc.alloc_psum_tensor("pstr", [128, 1024], BF16)
    ps_tr2 = nc.alloc_psum_tensor("pstr2", [128, 1024], BF16)
    ps_a = nc.alloc_psum_tensor("psa", [128, 512], F32)
    ps_b = nc.alloc_psum_tensor("psb", [128, 512], F32)

    sem_h = {n: nc.alloc_semaphore(n) for n in ["sT", "sV", "sS", "sG"]}
    cnt = {n: 0 for n in sem_h}
    ENG_SEM = {"vector": "sV", "scalar": "sS", "tensor": "sT", "gpsimd": "sG"}
    ops = []
    watermark = {e: {} for e in ["sync", "gpsimd", "vector", "scalar", "tensor"]}

    def sem_for(key):
        if key not in sem_h:
            sem_h[key] = nc.alloc_semaphore(key)
            cnt[key] = 0
        return sem_h[key]

    class Tok:
        __slots__ = ("eng", "sem", "val", "tiny")

        def __init__(self, eng, sem, val, tiny=False):
            self.eng, self.sem, self.val, self.tiny = eng, sem, val, tiny

    def _flat(deps):
        for d in deps:
            if d is None:
                continue
            if isinstance(d, (list, tuple)):
                yield from _flat(d)
            else:
                yield d

    def emit(eng, fn, deps=(), pub=False, dkey=None, cc=None, tiny=False):
        need = {}
        for d in _flat(deps):
            if (d.sem in ("sT", "sV", "sS", "sG") and d.eng == eng
                    and not debug_sync and not d.tiny):
                continue
            if need.get(d.sem, -1) < d.val:
                need[d.sem] = d.val
        waits = []
        wm = watermark[eng]
        for s, v in need.items():
            if wm.get(s, -1) >= v:
                continue
            wm[s] = v
            waits.append((sem_h[s], v))
        incs = []
        tok = None
        if cc is not None:
            sem_for(cc)
            cnt[cc] += 1
            incs.append((sem_h[cc], 1))
            tok = Tok(eng, cc, cnt[cc])
        elif dkey is not None:
            sem_for(dkey)
            cnt[dkey] += 16
            incs.append((sem_h[dkey], 16))
            tok = Tok(eng, dkey, cnt[dkey])
        elif pub or (debug_sync and eng in ENG_SEM):
            s = ENG_SEM[eng]
            cnt[s] += 1
            incs.append((sem_h[s], 1))
            tok = Tok(eng, s, cnt[s], tiny)
        ops.append({"eng": eng, "fn": fn, "waits": waits, "incs": incs})
        return tok

    last_dma = {}
    dbg_off = [0]
    dbg_names = []

    def dbg_cap(name, ap, deps, cols, parts=128):
        if not dbg:
            return
        o = dbg_off[0]
        dbg_names.append((name, o, cols, parts))
        dbg_off[0] += cols
        dma("gpsimd", "dbgk", dbg_h[0:parts, o:o + cols], ap, deps=list(deps),
            slow=True)
    build_program.dbg_names = dbg_names

    def dma(q, key, out, in_, deps=(), slow=False, chain=True):
        deps = list(deps)
        if chain and key in last_dma:
            deps.append(last_dma[key])
        t = emit(q, lambda e, o=out, i=in_, s=slow: e.dma_start(
            out=o, in_=i, allow_slow_non_contiguous=s), deps=deps, dkey=key)
        last_dma[key] = t
        return t

    def V(fn, deps=(), pub=False, tiny=False):
        return emit("vector", fn, deps, pub=pub, tiny=tiny)

    def S(fn, deps=(), pub=False, tiny=False):
        return emit("scalar", fn, deps, pub=pub, tiny=tiny)

    def TT(fn, deps=(), pub=False):
        return emit("tensor", fn, deps, pub=pub)

    l_convw = dma("sync", "ini0", convw[:, :], convw_h)
    l_convb = dma("sync", "ini1", convb[:, :], convb_h)
    l_inb = dma("sync", "ini2", inb_s[:, :], inb_h)
    l_xpb = dma("sync", "ini3", xpba_s[:, :], xpba_h)
    l_xpb2 = dma("sync", "ini1", xpbb_s[:, :], xpbb_h)
    l_idf = dma("sync", "ini0", id_f[:, :], ident_f_h)
    l_idb = dma("sync", "ini1", id_b[:, :], ident_b_h)
    l_egu = dma("sync", "ini2", egu_s[:, :], egu_h)
    l_ed = dma("sync", "ini3", ed_s[:, :], ed_h)
    l_outb = dma("sync", "ini0", outb_s[:, :], outb_h)
    l_oh = dma("sync", "ini1", onehot_s[:, :], onehot_h)
    l_hm = dma("sync", "ini2", hmask_s[:, :], hmask_h)
    l_ones1 = dma("sync", "ini3", ones1[:, :], ones1_h)
    l_c = dma("sync", "ini3", silc[:, :], c_h)
    l_wmod = dma("sync", "wmodk", wmodb[:, :].rearrange("p (m k q) -> p m k q", m=6, k=8),
                  wmod_h.rearrange("m k p q -> p m k q"))
    win_tok = [None, None]
    win_free = [None, None]
    for _pb in range(2):
        win_tok[_pb] = dma("gpsimd", f"win{_pb}",
                           win_bf[_pb][:, :].rearrange("p (k q) -> p k q", q=128),
                           winT_h[_pb], deps=[])
    l_xpw = dma("gpsimd", "gini0", xpw_bf[:, :].rearrange("p (k t) -> p k t", t=32),
                 xpwT_h.rearrange("k p t -> p k t"))
    l_outw = dma("gpsimd", "gini1", outw_bf[:, :].rearrange("p (m t) -> p m t", t=128),
                  outwT_h.rearrange("m p t -> p m t"))
    t_ones = V(lambda e: e.memset(ones_c[:, :], 1.0), pub=True)
    t_scl0 = V(lambda e: e.memset(scl[:, :], 0.0), pub=True)

    t_silc = S(lambda e: e.activation(silc[:, :], silc[:, :], Act.Silu),
               deps=[l_c], pub=True, tiny=True)

    mod_mm = []
    for m in range(6):
        for k in range(8):
            tk = TT(lambda e, m=m, k=k: e.matmul(
                ps_b[:, m:m + 1],
                wmodb[:, (m * 8 + k) * 128:(m * 8 + k) * 128 + 128],
                silc[:, k:k + 1], start=(k == 0), stop=(k == 7)),
                deps=[l_wmod, t_silc], pub=(k == 7))
        mod_mm.append(tk)
    mod_ev = []
    for m in range(6):
        mod_ev.append(V(lambda e, m=m: e.tensor_copy(
            modloc[:, m:m + 1], ps_b[:, m:m + 1]), deps=mod_mm, pub=True, tiny=True))
    t_modw = None
    for m in range(6):
        t_modw = dma("gpsimd", "ag0w", ag0_in.ap()[0:1, 128 * m:128 * m + 128]
                     .rearrange("one p -> p one"), modloc[:, m:m + 1],
                     deps=[mod_ev[m]])
    t_ag0 = emit("gpsimd", lambda e: e.collective_compute(
        "AllGather", Alu.bypass, replica_groups=[list(range(NCORES))],
        ins=[ag0_in.ap().opt()], outs=[ag0_out.ap().opt()]),
        deps=[t_modw], cc="cc0")
    flat0 = ag0_out.ap().rearrange("r v -> () (r v)")
    t_ad = []
    for col in range(48):
        vi, j = col // 8, col % 8
        g = vi * 8 + j
        t_ad.append(dma("gpsimd", f"adpu{col % 4}",
                        adp[:, col:col + 1],
                        flat0[0:1, 128 * g:128 * g + 128].rearrange(
                            "one p -> p one"), deps=[t_ag0]))
    t_ca1 = V(lambda e: e.tensor_scalar_add(adp[:, 8:16], adp[:, 8:16], 1.0),
              deps=t_ad, pub=True, tiny=True)
    t_cb1 = V(lambda e: e.tensor_scalar_add(adp[:, 32:40], adp[:, 32:40], 1.0),
              deps=t_ad, pub=True, tiny=True)
    gq_toks = []
    prev_gq = None
    for src0 in (16, 40):
        tsg = S(lambda e, s=src0: e.activation(adp[:, 48:56], adp[:, s:s + 8],
                                               Act.Sign),
                deps=t_ad + ([prev_gq] if prev_gq else []), pub=True, tiny=True)
        tab = S(lambda e, s=src0: e.activation(adp[:, 56:64], adp[:, s:s + 8],
                                               Act.Abs), deps=t_ad, pub=True, tiny=True)
        tmx = V(lambda e: e.tensor_scalar(gqs[:, :], adp[:, 56:64], 0.25,
                                          None, Alu.max),
                deps=[tab], pub=True, tiny=True)
        prev_gq = V(lambda e, s=src0: e.tensor_tensor(
            adp[:, s:s + 8], adp[:, 48:56], gqs[:, :], Alu.mult),
            deps=[tsg, tmx], pub=True, tiny=True)

        gq_toks.append(prev_gq)
    t_gaq, t_gbq = gq_toks

    dbg_cap("adp", adp[:, :], [t_gaq, t_gbq, t_ca1, t_cb1], 64)
    tr_guard = [None, None]

    def tr_slot(si):
        return (ps_tr if si % 2 == 0 else ps_tr2)[
            :, 128 * ((si // 2) % 4):128 * ((si // 2) % 4) + 128]

    h1_toks = []
    x_toks = [None] * 16
    lnprev = [None, None]
    tr_ctr = [0]
    xhat_guard = [None, None]

    def ln_tile(i, dst, ap_dst_tok, affine_c1, affine_c0, out_chunk_fn, idtile,
                extra_deps=()):
        par = i % 2
        so = 16 * par
        stats_ = stats[:, so:so + 16]
        sq_ = sq1 if par == 0 else sq1b
        t_sum = V(lambda e: e.tensor_reduce(stats_[:, 0:1], dst[:, :], AX,
                                            Alu.add),
                  deps=[ap_dst_tok, lnprev[par]] + list(extra_deps), pub=True,
                  tiny=True)
        t_sq0 = S(lambda e: e.activation(sq_[:, :], dst[:, :], Act.Square),
                  deps=[ap_dst_tok, lnprev[par]] + list(extra_deps), pub=True)
        t_sq = V(lambda e: e.tensor_reduce(stats_[:, 1:2], sq_[:, :], AX,
                                           Alu.add),
                 deps=[t_sq0], pub=True, tiny=True)
        t_mn = V(lambda e: e.tensor_scalar_mul(stats_[:, 2:3], stats_[:, 0:1],
                                               1.0 / D), deps=[t_sum], pub=True, tiny=True)
        t_ms = V(lambda e: e.tensor_scalar_mul(stats_[:, 3:4], stats_[:, 1:2],
                                               1.0 / D), deps=[t_sq], pub=True, tiny=True)
        t_m2 = V(lambda e: e.tensor_tensor(stats_[:, 4:5], stats_[:, 2:3],
                                           stats_[:, 2:3], Alu.mult),
                 deps=[t_mn], pub=True, tiny=True)
        t_vr = V(lambda e: e.tensor_tensor(stats_[:, 5:6], stats_[:, 3:4],
                                           stats_[:, 4:5], Alu.subtract),
                 deps=[t_ms, t_m2], pub=True, tiny=True)
        t_ve = V(lambda e: e.tensor_scalar_add(stats_[:, 6:7], stats_[:, 5:6],
                                               EPS),
                 deps=[t_vr], pub=True, tiny=True)
        t_rc = V(lambda e: e.reciprocal(stats_[:, 7:8], stats_[:, 6:7]),
                 deps=[t_ve], pub=True, tiny=True)
        t_rs = S(lambda e: e.activation(stats_[:, 8:9], stats_[:, 7:8],
                                        Act.Sqrt),
                 deps=[t_rc], pub=True, tiny=True)
        if i == 2:
            dbg_cap("stats2", stats[:, :], [t_rs, t_mn, t_m2], 16)
            dbg_cap("xin2", dst[:, :], [t_rs], 1024)
        xh = xhat[i % 2]
        t_xh = V(lambda e: e.tensor_scalar(xh[:, :], dst[:, :], stats_[:, 2:3],
                                           stats_[:, 8:9], Alu.subtract,
                                           Alu.mult),
                 deps=[t_rs, xhat_guard[i % 2]] + list(extra_deps), pub=True)
        if i == 2:
            dbg_cap("xhat2", xh[:, :], [t_xh], 1024)
        lnprev[par] = t_xh
        evs = []
        t_tr = None
        for j in range(8):
            si = tr_ctr[0] % 8
            tr_ctr[0] += 1
            t_tr = TT(lambda e, j=j, si=si: e.transpose(
                tr_slot(si), xh[:, 128 * j:128 * j + 128], id_b[:, :]),
                deps=[t_xh, l_idb, tr_guard[si % 2]], pub=True)
            ev = V(lambda e, j=j, si=si: e.tensor_scalar(
                out_chunk_fn(j), tr_slot(si), adp[:, affine_c1 + j:affine_c1 + j + 1],
                adp[:, affine_c0 + j:affine_c0 + j + 1], Alu.mult, Alu.add),
                deps=[t_tr, t_ca1, t_cb1], pub=True, tiny=True)
            tr_guard[si % 2] = ev
            evs.append(ev)
        xhat_guard[i % 2] = t_tr
        return evs

    agw_writes = []
    lp_guard = [None]
    lp_rm_guard = [None]

    def lpbq_slot(s):
        I = D if s < 6 else FF
        B = I // 32
        if s < 6:
            t_row = dma("sync", "lpr", wrow[:, 0:I], lpq_rows_gu_h[s],
                        deps=[lp_rm_guard[0]])
            t_lh = dma("sync", "lpl",
                       wlhs[:, 0:I].rearrange("p (k q) -> p k q", q=128),
                       lpq_lhsT_gu_h[s], deps=[lp_guard[0]])
        else:
            t_row = dma("sync", "lpr", wrow[:, 0:I], lpq_rows_d_h,
                        deps=[lp_rm_guard[0]])
            t_lh = dma("sync", "lpl",
                       wlhs[:, 0:I].rearrange("p (k q) -> p k q", q=128),
                       lpq_lhsT_d_h, deps=[lp_guard[0]])
        t_rm = V(lambda e, I=I, B=B: e.tensor_reduce(
            smax[:, 0:B].rearrange("p b -> p b ()"),
            wrow[:, 0:I].rearrange("p (b t) -> p b t", t=32),
            AX, Alu.max, apply_absolute_value=True), deps=[t_row], pub=True, tiny=True)
        lp_rm_guard[0] = t_rm
        t_s = V(lambda e, B=B: e.tensor_scalar(
            sval[:, 0:B], smax[:, 0:B], 1.0 / 7.0, 1e-12, Alu.mult, Alu.max),
            deps=[t_rm], pub=True, tiny=True)
        t_i = V(lambda e, B=B: e.reciprocal(sinv[:, 0:B], sval[:, 0:B]),
                deps=[t_s], pub=True, tiny=True)
        t_t1 = TT(lambda e, B=B: e.transpose(ps_mm[2][0:B, 0:128], sval[:, 0:B],
                                             id_f[:, :]),
                  deps=[t_s, l_idf, lp_guard[0]], pub=True)
        t_t2 = TT(lambda e, B=B: e.transpose(ps_mm[3][0:B, 0:128], sinv[:, 0:B],
                                             id_f[:, :]),
                  deps=[t_i, l_idf, lp_guard[0]], pub=True)
        B4 = B
        ev1 = V(lambda e, B=B: e.tensor_copy(sT_s[0:B, :], ps_mm[2][0:B, 0:128]),
                deps=[t_t1], pub=True, tiny=True)
        ev2 = V(lambda e, B=B: e.tensor_copy(invT_s[0:B, :], ps_mm[3][0:B, 0:128]),
                deps=[t_t2], pub=True, tiny=True)
        E_s = egu_s if s < 6 else ed_s
        l_E = l_egu if s < 6 else l_ed
        wq_toks = []
        piece_guard = [agw_writes[-1]] if agw_writes else []
        for p in range(I // 512):
            exps = []
            for q in range(4):
                kc = 4 * p + q
                exps.append(TT(lambda e, q=q, kc=kc, B=B, E_s=E_s: e.matmul(
                    ps_a[:, 128 * q:128 * q + 128],
                    E_s[0:B, 128 * kc:128 * kc + 128], sT_s[0:B, :],
                    start=True, stop=True),
                    deps=[ev1, ev2, l_E] + mod_ev + piece_guard, pub=(q == 3)))
                exps.append(TT(lambda e, q=q, kc=kc, B=B, E_s=E_s: e.matmul(
                    ps_b[:, 128 * q:128 * q + 128],
                    E_s[0:B, 128 * kc:128 * kc + 128], invT_s[0:B, :],
                    start=True, stop=True),
                    deps=[ev1, ev2, l_E] + mod_ev + piece_guard, pub=(q == 3)))
            c0 = 512 * p
            t_u = V(lambda e, c0=c0: e.tensor_tensor(
                hc[:, 0:512], wlhs[:, c0:c0 + 512], ps_b[:, 0:512], Alu.mult),
                deps=[t_lh, exps[-1], exps[-2]], pub=True)
            t_t = V(lambda e: e.tensor_scalar_add(hc[:, 512:1024], hc[:, 0:512],
                                                  MAGIC), deps=[t_u], pub=True)
            wq_toks.append(V(lambda e, c0=c0: e.scalar_tensor_tensor(
                wqcol[:, c0:c0 + 512], hc[:, 512:1024], MAGIC, ps_a[:, 0:512],
                Alu.subtract, Alu.mult), deps=[t_t], pub=True))
            piece_guard = [t_u, wq_toks[-1]]
        lp_guard[0] = wq_toks[-1]
        eo = SLOT_OFF[s] // 2
        agw_writes.append(dma(
            "gpsimd", "agww",
            agw_in.ap()[0:1, eo:eo + 128 * I].rearrange(
                "one (p q) -> (one p) q", p=128),
            wqcol[:, 0:I], deps=wq_toks))

    for i in range(17):
        dst = xstage if i == 0 else x_tm[i - 1]
        t_x = dma("sync", f"xl{i % 4}", dst[:, :], x_h[128 * i:128 * i + 128, :])
        if i >= 1:
            x_toks[i - 1] = t_x
        evs = ln_tile(i, dst, t_x, 8, 0,
                      lambda j, i=i: h1_bf[j][:, 128 * i:128 * i + 128], id_b)
        h1_toks.extend(evs)
        if i < 7:
            lpbq_slot(i)
    dbg_cap("h1_0", h1_bf[0][:, 0:512], h1_toks, 512)
    t_agw = emit("gpsimd", lambda e: e.collective_compute(
        "AllGather", Alu.bypass, replica_groups=[list(range(NCORES))],
        ins=[agw_in.ap().opt()], outs=[agw_out.ap().opt()]),
        deps=agw_writes, cc="ccw")

    NTILES_IN = [(0, 128), (128, 512), (640, 512), (1152, 512), (1664, 512)]
    bank_free = [None] * 4
    hprime = []
    for m in range(16):
        pb = m % 2
        if m >= 2:
            win_tok[pb] = dma("gpsimd", f"win{pb}",
                              win_bf[pb][:, :].rearrange("p (k q) -> p k q",
                                                         q=128),
                              winT_h[m], deps=[win_free[pb]])
        ev_toks = []
        for ni, (c0, w) in enumerate(NTILES_IN):
            bk = (m * 5 + ni) % 4
            last_tile = (c0 + w - 1) // 128
            h1_dep = h1_toks[:8 * (last_tile + 1)]
            mmt = None
            for k in range(8):
                mmt = TT(lambda e, pb=pb, bk=bk, k=k, c0=c0, w=w: e.matmul(
                    ps_mm[bk][:, 0:w], win_bf[pb][:, 128 * k:128 * k + 128],
                    h1_bf[k][:, c0:c0 + w], start=(k == 0), stop=(k == 7)),
                    deps=[win_tok[pb], bank_free[bk]] + (h1_dep if k == 0 else []),
                    pub=(k == 7))
            ev = S(lambda e, m=m, bk=bk, c0=c0, w=w: e.activation(
                h_bf[m][:, c0:c0 + w], ps_mm[bk][:, 0:w], Act.Identity,
                bias=inb_s[:, m:m + 1]),
                deps=[mmt, l_inb] + (agw_writes if m <= 7 else []), pub=True,
                tiny=True)
            bank_free[bk] = ev
            ev_toks.append(ev)
        win_free[pb] = ev_toks[-1]
        t_msk = V(lambda e, m=m: e.tensor_scalar(
            h_bf[m][:, HALO - 3:HALO], h_bf[m][:, HALO - 3:HALO],
            hmask_s[:, 0:1], None, Alu.mult), deps=ev_toks + [l_hm], pub=True)
        t_c = V(lambda e, m=m: e.tensor_scalar(
            hc[:, 0:T], h_bf[m][:, HALO - 3:HALO - 3 + T], convw[:, 4 * m:4 * m + 1],
            None, Alu.mult),
            deps=[t_msk, l_convw, hprime[-1] if hprime else None], pub=True)
        for j in (1, 2, 3):
            t_c = V(lambda e, m=m, j=j: e.scalar_tensor_tensor(
                hc[:, 0:T], h_bf[m][:, HALO - 3 + j:HALO - 3 + j + T],
                convw[:, 4 * m + j:4 * m + j + 1], hc[:, 0:T], Alu.mult, Alu.add),
                deps=[t_c], pub=True)
        hprime.append(S(lambda e, m=m: e.activation(
            h_bf[m][:, HALO:HALO + T], hc[:, 0:T], Act.Silu,
            bias=convb[:, m:m + 1]), deps=[t_c, l_convb], pub=True))

    dbg_cap("hprime0", h_bf[0][:, HALO:HALO + 512], hprime, 512)
    xp_toks = []
    for n in range(4):
        mmta = None
        mmtb = None
        for k in range(16):
            mmta = TT(lambda e, n=n, k=k: e.matmul(
                ps_a[0:16, 0:512], xpw_bf[:, 32 * k:32 * k + 16],
                h_bf[k][:, HALO + 512 * n:HALO + 512 * n + 512],
                start=(k == 0), stop=(k == 15)),
                deps=[l_xpw, hprime[k]] + (xp_toks[-2:] if xp_toks else []),
                pub=(k == 15))
            mmtb = TT(lambda e, n=n, k=k: e.matmul(
                ps_mm[0][0:16, 0:512], xpw_bf[:, 32 * k + 16:32 * k + 32],
                h_bf[k][:, HALO + 512 * n:HALO + 512 * n + 512],
                start=(k == 0), stop=(k == 15)),
                deps=[l_xpw, hprime[k], bank_free[0]]
                + (xp_toks[-2:] if xp_toks else []),
                pub=(k == 15))
        xp_toks.append(V(lambda e, n=n: e.tensor_scalar(
            scanA[0:16, 512 * n:512 * n + 512], ps_a[0:16, 0:512],
            xpba_s[0:16, 0:1], None, Alu.add), deps=[mmta, l_xpb], pub=True))
        xp_toks.append(V(lambda e, n=n: e.tensor_scalar(
            scanC[0:16, 512 * n:512 * n + 512], ps_mm[0][0:16, 0:512],
            xpbb_s[0:16, 0:1], None, Alu.add), deps=[mmtb, l_xpb2], pub=True))
    sum_toks = []
    for n in range(4):
        mmt = None
        for k in range(16):
            mmt = TT(lambda e, n=n, k=k: e.matmul(
                ps_b[0:1, 0:512], ones_c[:, :],
                h_bf[k][:, HALO + 512 * n:HALO + 512 * n + 512],
                start=(k == 0), stop=(k == 15)),
                deps=[t_ones, hprime[k]] + (sum_toks[-1:] if sum_toks else []),
                pub=(k == 15))
        t_se = V(lambda e: e.tensor_copy(sumstg[:, :], ps_b[0:1, 0:512]),
                 deps=[mmt, sum_toks[-1] if sum_toks else None], pub=True, tiny=True)
        sum_toks.append(dma("sync", "sumw", sumd.ap()[0:1, 512 * n:512 * n + 512],
                            sumstg[:, :], deps=[t_se]))
    t_e = S(lambda e: e.activation(scanB[0:16, :], scanA[0:16, :], Act.Exp),
            deps=xp_toks + sum_toks, pub=True)
    t_a = S(lambda e: e.activation(scanB[0:16, :], scanB[0:16, :], Act.Exp,
                                   scale=-1.0), deps=[t_e], pub=True)
    t_sl = S(lambda e: e.activation(scanC[:, :], scanC[:, :], Act.Silu),
             deps=xp_toks, pub=True)
    t_sb = dma("gpsimd", "sumb", scanD[:, :], sumd.ap().partition_broadcast(NS),
               deps=sum_toks)
    t_b = V(lambda e: e.tensor_tensor(scanC[:, :], scanC[:, :], scanD[:, :],
                                      Alu.mult), deps=[t_sl, t_sb], pub=True)
    t_y0 = V(lambda e: e.tensor_tensor_scan(scanD[:, :], scanB[0:16, :],
                                            scanC[:, :], 0.0, Alu.mult, Alu.add),
             deps=[t_a, t_b], pub=True)
    t_P = V(lambda e: e.tensor_tensor_scan(scanA[0:16, :], scanB[0:16, :],
                                           scanB[0:16, :], 1.0, Alu.mult,
                                           Alu.bypass), deps=[t_a], pub=True)
    t_c1 = dma("gpsimd", "cry", ag1_in.ap()[0:1, 0:16].rearrange("one n -> n one"),
               scanA[0:16, T - 1:T], deps=[t_P])
    t_c2 = dma("gpsimd", "cry", ag1_in.ap()[0:1, 16:32].rearrange("one n -> n one"),
               scanD[:, T - 1:T], deps=[t_y0])
    t_ag1 = emit("gpsimd", lambda e: e.collective_compute(
        "AllGather", Alu.bypass, replica_groups=[list(range(NCORES))],
        ins=[ag1_in.ap().opt()], outs=[ag1_out.ap().opt()]),
        deps=[t_c1, t_c2], cc="cc1")
    t_g1 = t_g2 = None
    for r in range(NCORES):
        t_g1 = dma("gpsimd", f"cryo{r % 2}", cgat[:, r:r + 1],
                   ag1_out.ap()[r:r + 1, 0:16].rearrange("one n -> n one"),
                   deps=[t_ag1])
        t_g2 = dma("gpsimd", f"cryp{r % 2}", cgee[:, r:r + 1],
                   ag1_out.ap()[r:r + 1, 16:32].rearrange("one n -> n one"),
                   deps=[t_ag1])
    t_cs = V(lambda e: e.tensor_tensor_scan(csc[:, :], cgat[:, :], cgee[:, :],
                                            0.0, Alu.mult, Alu.add),
             deps=[t_g1, t_g2, last_dma["cryo0"], last_dma["cryp0"]],
             pub=True, tiny=True)
    t_sel = V(lambda e: e.tensor_tensor(csc[:, :], csc[:, :], onehot_s[:, :],
                                        Alu.mult), deps=[t_cs, l_oh], pub=True, tiny=True)
    t_sin = V(lambda e: e.tensor_reduce(sin_s[:, 0:1], csc[:, :], AX, Alu.add),
              deps=[t_sel], pub=True, tiny=True)
    t_y = V(lambda e: e.tensor_tensor_scan(y_bf[0:16, :], scanB[0:16, :],
                                           scanC[:, :], sin_s[0:16, 0:1],
                                           Alu.mult, Alu.add),
            deps=[t_sin], pub=True)

    dbg_cap("a_", scanB[0:16, 0:512], [t_a], 512, parts=16)
    dbg_cap("b_", scanC[0:16, 0:512], [t_b, t_y0], 512, parts=16)
    dbg_cap("sin", sin_s[:, 0:1], [t_sin], 1, parts=16)
    dbg_cap("y", y_bf[0:16, 0:512], [t_y], 512, parts=16)
    t_gaob2 = V(lambda e: e.tensor_tensor(gaob[:, :], adp[:, 16:24],
                                          outb_s[:, :], Alu.mult),
                deps=[t_gaq, l_outb], pub=True, tiny=True)
    res_toks = []
    ctile_free = [None] * 4
    for n in range(4):
        for mo in range(8):
            bk = (n * 8 + mo) % 4
            mmt = TT(lambda e, bk=bk, mo=mo, n=n: e.matmul(
                ps_mm[bk][:, 0:512], outw_bf[0:16, 128 * mo:128 * mo + 128],
                y_bf[0:16, 512 * n:512 * n + 512], start=True, stop=True),
                deps=[l_outw, t_y, bank_free[bk]], pub=True)
            sl = (n * 8 + mo) % 4
            ev = S(lambda e, bk=bk, mo=mo, sl=sl: e.activation(
                ctile[sl][:, :], ps_mm[bk][:, 0:512], Act.Identity,
                scale=adp[:, 16 + mo:17 + mo], bias=gaob[:, mo:mo + 1]),
                deps=[mmt, t_gaob2, ctile_free[sl], t_c2, t_g1, t_g2],
                pub=True, tiny=True)
            bank_free[bk] = ev
            for q in range(4):
                si = tr_ctr[0] % 8
                tr_ctr[0] += 1
                t_tr = TT(lambda e, sl=sl, q=q, si=si: e.transpose(
                    tr_slot(si), ctile[sl][:, 128 * q:128 * q + 128], id_b[:, :]),
                    deps=[ev, tr_guard[si % 2]], pub=True)
                ra = V(lambda e, n=n, q=q, mo=mo, si=si: e.tensor_tensor(
                    x_tm[4 * n + q][:, 128 * mo:128 * mo + 128],
                    x_tm[4 * n + q][:, 128 * mo:128 * mo + 128],
                    tr_slot(si), Alu.add),
                    deps=[t_tr, x_toks[4 * n + q]], pub=True, tiny=True)
                tr_guard[si % 2] = ra
                res_toks.append(ra)
                ctile_free[sl] = t_tr

    dbg_cap("x2_0", x_tm[0][:, 0:512], res_toks, 512)
    h2_toks = []
    xsp_toks = []
    for i in range(16):
        evs = ln_tile(100 + i, x_tm[i], None, 32, 24,
                      lambda j, i=i: hx[j][:, 128 * i:128 * i + 128], id_b,
                      extra_deps=res_toks)
        h2_toks.extend(evs)
        xsp_toks.append(dma("sync", f"xsp{i % 4}", xspill[i], x_tm[i][:, :],
                            deps=evs))
    mx_h2 = []
    for j in range(8):
        mx_h2.append(V(lambda e, j=j: e.tensor_reduce(
            mxc[:, j:j + 1], hx[j][:, :], AX, Alu.max,
            apply_absolute_value=True), deps=h2_toks, pub=True, tiny=True))

    def global_max(src_col_range, dst_slot, deps):
        a, b = src_col_range
        t1 = V(lambda e: e.tensor_reduce(mxc[:, 126:127], mxc[:, a:b], AX,
                                         Alu.max), deps=deps, pub=True, tiny=True)
        return emit("gpsimd", lambda e, d=dst_slot: e.tensor_reduce(
            scl[0:1, d:d + 1], mxc[:, 126:127], mybir.AxisListType.C, Alu.max),
            deps=[t1], pub=True, tiny=True)

    def allreduce_pair(src_slot, deps):
        t1 = dma("sync", "arb", ar_in.ap(), scl[0:1, src_slot:src_slot + 2],
                 deps=deps)
        arid = ar_ctr[0]
        ar_ctr[0] += 1
        t2 = emit("gpsimd", lambda e: e.collective_compute(
            "AllReduce", Alu.max, replica_groups=[list(range(NCORES))],
            ins=[ar_in.ap().opt()], outs=[ar_out.ap().opt()]),
            deps=[t1], cc=f"ccr{arid}")
        return dma("sync", "arb", scl[0:1, src_slot:src_slot + 2], ar_out.ap(),
                   deps=[t2])

    ar_ctr = [0]
    bc_guard = [None]

    def bcast_scl(deps):
        tm = TT(lambda e: e.matmul(ps_b[:, 0:32], ones1[:, :], scl[0:1, :],
                                   start=True, stop=True),
                deps=list(deps) + [l_ones1, bc_guard[0], sum_toks[-1] if sum_toks else None],
                pub=True)
        tv = V(lambda e: e.tensor_copy(sclb[:, :], ps_b[:, 0:32]),
               deps=[tm], pub=True, tiny=True)
        bc_guard[0] = tv
        return tv

    dbg_cap("h2_0", hx[0][:, 0:512], h2_toks, 512)
    t_mx1 = global_max((0, 8), 0, mx_h2 + [t_scl0])
    t_ar1 = allreduce_pair(0, [t_mx1])
    t_s1 = V(lambda e: e.tensor_scalar(scl[0:1, 1:2], scl[0:1, 0:1], INV_QMAX,
                                       1e-12, Alu.mult, Alu.max),
             deps=[t_ar1], pub=True, tiny=True)
    t_i1 = V(lambda e: e.reciprocal(scl[0:1, 2:3], scl[0:1, 1:2]),
             deps=[t_s1], pub=True, tiny=True)
    t_bc1 = bcast_scl([t_i1])

    dbg_cap("scl1", scl[:, :], [t_i1, t_bc1], 32, parts=1)
    dbg_cap("sclb1", sclb[:, :], [t_bc1], 32)
    xq_toks = [None] * 32
    for j in range(8):
        for qq in range(4):
            rs = rtring[qq]
            c0 = 512 * qq
            t_r = V(lambda e, j=j, c0=c0, rs=rs: e.tensor_scalar(
                rs[:, :], hx[j][:, c0:c0 + 512], sclb[:, 2:3], MAGIC,
                Alu.mult, Alu.add),
                deps=[t_bc1, xq_toks[4 * j + qq - 4] if 4 * j + qq >= 4 else None]
                + h2_toks, pub=True)
            xq_toks[4 * j + qq] = V(lambda e, j=j, c0=c0, rs=rs: e.tensor_scalar(
                hx[j][:, c0:c0 + 512], rs[:, :], MAGIC, None, Alu.subtract),
                deps=[t_r], pub=True)

    scan_dead = [t_y, t_c1, t_c2, hprime[-1], xp_toks[-1], sum_toks[-1]]

    def ffn_col_load(j, pb, deps):
        r, eo = col_addr(j)
        I = D if j < 48 else FF
        return dma("sync", f"wff{pb}", wffcol[pb][:, 0:I],
                   agw_out.ap()[r:r + 1, eo:eo + 128 * I].rearrange(
                       "one (p q) -> (one p) q", p=128),
                   deps=deps + [t_agw] + scan_dead)

    wff_tok = [None, None]
    wff_free = [None, None]
    dbg_cap("xq_0", hx[0][:, 0:512], xq_toks, 512)
    gate_ev = []
    for m in range(24):
        pb = m % 2
        wff_tok[pb] = ffn_col_load(m, pb, [wff_free[pb]])
        for n in range(4):
            bk = (m * 4 + n) % 4
            mmt = None
            for k in range(8):
                mmt = TT(lambda e, pb=pb, bk=bk, k=k, n=n: e.matmul(
                    ps_mm[bk][:, :], wffcol[pb][:, 128 * k:128 * k + 128],
                    hx[k][:, 512 * n:512 * n + 512],
                    start=(k == 0), stop=(k == 7)),
                    deps=[wff_tok[pb], bank_free[bk], xq_toks[4 * k + n]],
                    pub=(k == 7))
            ev = S(lambda e, m=m, n=n, bk=bk: e.activation(
                gP(m)[:, 512 * n:512 * n + 512], ps_mm[bk][:, :], Act.Copy,
                scale=sclb[:, 1:2]),
                deps=[mmt, t_bc1] + ([xsp_toks[m]] if m < 16 else []), pub=True,
                tiny=True)
            bank_free[bk] = ev
            gate_ev.append(ev)
            gate_ev.append(V(lambda e, m=m, n=n: e.tensor_reduce(
                mxc[:, 4 * m + n:4 * m + n + 1], gP(m)[:, 512 * n:512 * n + 512],
                AX, Alu.max, apply_absolute_value=True), deps=[ev], pub=True, tiny=True))
        wff_free[pb] = gate_ev[-2]
    dbg_cap("gate_0", gP(0)[:, 0:512], gate_ev, 512)
    t_mxg = global_max((0, 96), 3, gate_ev)
    t_ar2 = allreduce_pair(3, [t_mxg])
    t_sg = V(lambda e: e.tensor_scalar(scl[0:1, 4:5], scl[0:1, 3:4], INV_QMAX,
                                       1e-12, Alu.mult, Alu.max),
             deps=[t_ar2], pub=True, tiny=True)
    t_ig = V(lambda e: e.reciprocal(scl[0:1, 5:6], scl[0:1, 4:5]),
             deps=[t_sg], pub=True, tiny=True)
    t_bc2 = bcast_scl([t_ig])

    up_ev = []
    up_spill_toks = {}
    upr_free = [None] * 8
    up_last_mxe = [None] * 8
    w1_toks = []
    w1_rt_guard = [None] * 4

    def window1_chunk(m):
        for hh in range(4):
            c0 = 512 * hh
            rs = rtring[hh]
            xh = xhat[hh % 2][:, 512 * (hh // 2):512 * (hh // 2) + 512]
            t1 = S(lambda e, m=m, c0=c0, rs=rs: e.activation(
                rs[:, :], gP(m)[:, c0:c0 + 512], Act.Copy, bias=MAGIC,
                scale=sclb[:, 5:6]),
                deps=[t_bc2, w1_rt_guard[hh]] + gate_ev, pub=True)
            t2 = V(lambda e, m=m, c0=c0, rs=rs: e.tensor_scalar(
                gP(m)[:, c0:c0 + 512], rs[:, :], MAGIC, None, Alu.subtract),
                deps=[t1], pub=True)
            t3 = S(lambda e, m=m, c0=c0, xh=xh: e.activation(
                xh[:, :], gP(m)[:, c0:c0 + 512], Act.Sigmoid,
                scale=sclb[:, 4:5]), deps=[t2], pub=True)
            t4 = S(lambda e, xh=xh, rs=rs: e.activation(
                rs[:, :], xh[:, :], Act.Copy, bias=MAGIC, scale=65536.0),
                deps=[t3], pub=True)
            t5 = V(lambda e, rs=rs: e.tensor_scalar(
                rs[:, :], rs[:, :], MAGIC, 65535.0, Alu.subtract, Alu.min),
                deps=[t4], pub=True)
            t6 = V(lambda e, m=m, c0=c0, rs=rs: e.tensor_tensor(
                gP(m)[:, c0:c0 + 512], rs[:, :], gP(m)[:, c0:c0 + 512],
                Alu.mult), deps=[t5], pub=True)
            w1_rt_guard[hh] = t6
            w1_toks.append(V(lambda e, m=m, c0=c0, hh=hh: e.tensor_reduce(
                mxc2[:, (4 * m + hh):(4 * m + hh) + 1],
                gP(m)[:, c0:c0 + 512], AX, Alu.max,
                apply_absolute_value=True), deps=[t6], pub=True, tiny=True))

    for m in range(24):
        pb = m % 2
        wff_tok[pb] = ffn_col_load(24 + m, pb, [wff_free[pb]])
        for n in range(4):
            bk = (m * 4 + n) % 4
            sl = (m * 4 + n) % 8
            mmt = None
            for k in range(8):
                mmt = TT(lambda e, pb=pb, bk=bk, k=k, n=n: e.matmul(
                    ps_mm[bk][:, :], wffcol[pb][:, 128 * k:128 * k + 128],
                    hx[k][:, 512 * n:512 * n + 512],
                    start=(k == 0), stop=(k == 7)),
                    deps=[wff_tok[pb], bank_free[bk]], pub=(k == 7))
            ev = S(lambda e, sl=sl, bk=bk: e.activation(
                uprng[sl][:, :], ps_mm[bk][:, :], Act.Copy, scale=sclb[:, 1:2]),
                deps=[mmt, upr_free[sl], t_bc2], pub=True, tiny=True)
            bank_free[bk] = ev
            mxe = V(lambda e, m=m, n=n, sl=sl: e.tensor_reduce(
                mxc[:, 4 * m + n:4 * m + n + 1], uprng[sl][:, :], AX, Alu.max,
                apply_absolute_value=True), deps=[ev], pub=True, tiny=True)
            up_ev.append(mxe)
            up_last_mxe[sl] = mxe
            sp = dma("sync", f"upr{sl}", upspill.ap()[m, n], uprng[sl][:, :],
                     deps=[ev])
            up_spill_toks[(m, n)] = sp
            upr_free[sl] = sp
        window1_chunk(m)

    dbg_cap("scl2", scl[:, :], [t_bc2], 32, parts=1)
    dbg_cap("P_0", gP(0)[:, 0:512], w1_toks, 512)
    t_mxu = global_max((0, 96), 6, up_ev)
    t_mxP1 = V(lambda e: e.tensor_reduce(mxc2[:, 95:96]
                                         if False else mxc[:, 125:126],
                                         mxc2[:, 0:96], AX, Alu.max),
               deps=w1_toks, pub=True, tiny=True)
    t_mxP = emit("gpsimd", lambda e: e.tensor_reduce(
        scl[0:1, 7:8], mxc[:, 125:126], mybir.AxisListType.C, Alu.max),
        deps=[t_mxP1, t_mxu], pub=True, tiny=True)
    t_ar3 = allreduce_pair(6, [t_mxu, t_mxP])
    t_sup = V(lambda e: e.tensor_scalar(scl[0:1, 8:9], scl[0:1, 6:7], INV_QMAX,
                                        1e-12, Alu.mult, Alu.max),
              deps=[t_ar3], pub=True, tiny=True)
    t_iup = V(lambda e: e.reciprocal(scl[0:1, 9:10], scl[0:1, 8:9]),
              deps=[t_sup], pub=True, tiny=True)
    t_cg = V(lambda e: e.tensor_scalar_mul(scl[0:1, 12:13], scl[0:1, 4:5],
                                           1.0 / 65536.0), deps=[t_sg], pub=True, tiny=True)
    t_mg2 = V(lambda e: e.tensor_tensor(scl[0:1, 20:21], scl[0:1, 7:8],
                                        scl[0:1, 12:13], Alu.mult),
              deps=[t_ar3, t_cg], pub=True, tiny=True)
    t_s4 = V(lambda e: e.tensor_scalar(scl[0:1, 10:11], scl[0:1, 20:21],
                                       INV_QMAX, 1e-12, Alu.mult, Alu.max),
             deps=[t_mg2], pub=True, tiny=True)
    t_i4 = V(lambda e: e.reciprocal(scl[0:1, 11:12], scl[0:1, 10:11]),
             deps=[t_s4], pub=True, tiny=True)
    t_c4 = V(lambda e: e.tensor_tensor(scl[0:1, 13:14], scl[0:1, 12:13],
                                       scl[0:1, 11:12], Alu.mult),
             deps=[t_i4], pub=True, tiny=True)
    t_s4u = V(lambda e: e.tensor_tensor(scl[0:1, 14:15], scl[0:1, 10:11],
                                        scl[0:1, 8:9], Alu.mult),
              deps=[t_s4, t_sup], pub=True, tiny=True)
    t_bc3 = bcast_scl([t_iup, t_c4, t_s4u])

    dbg_cap("scl3", scl[:, :], [t_bc3, t_s4u, t_c4], 32, parts=1)
    o_toks = []
    upr_free2 = list(up_last_mxe)
    w2_gu = [None] * 2
    w2_gg = [None] * 2
    for m in range(24):
        for n in range(4):
            i = m * 4 + n
            sl = i % 8
            ru = rtring[i % 2]
            rg = rtring[2 + i % 2]
            rd = dma("sync", f"upr{sl}", uprng[sl][:, :], upspill.ap()[m, n],
                     deps=[up_spill_toks[(m, n)], upr_free2[sl]])
            t1 = S(lambda e, sl=sl, ru=ru: e.activation(
                ru[:, :], uprng[sl][:, :], Act.Copy, bias=MAGIC,
                scale=sclb[:, 9:10]),
                deps=[rd, t_bc3, w2_gu[i % 2]], pub=True)
            t2 = V(lambda e, sl=sl, ru=ru: e.tensor_scalar(
                uprng[sl][:, :], ru[:, :], MAGIC, None, Alu.subtract),
                deps=[t1], pub=True)
            w2_gu[i % 2] = t2
            t3 = S(lambda e, m=m, n=n, rg=rg: e.activation(
                rg[:, :], gP(m)[:, 512 * n:512 * n + 512], Act.Copy,
                bias=MAGIC, scale=sclb[:, 13:14]),
                deps=[t_bc3, w2_gg[i % 2]] + w1_toks, pub=True)
            t4 = V(lambda e, m=m, n=n, sl=sl, rg=rg: e.scalar_tensor_tensor(
                gP(m)[:, 512 * n:512 * n + 512], rg[:, :], MAGIC,
                uprng[sl][:, :], Alu.subtract, Alu.mult),
                deps=[t2, t3], pub=True)
            w2_gg[i % 2] = t4
            upr_free2[sl] = t4
            o_toks.append(V(lambda e, m=m, n=n: e.tensor_reduce(
                mxc[:, 4 * m + n:4 * m + n + 1], gP(m)[:, 512 * n:512 * n + 512],
                AX, Alu.max, apply_absolute_value=True), deps=[t4], pub=True, tiny=True))
    dbg_cap("o_0", gP(0)[:, 0:512], o_toks, 512)
    t_mxo = global_max((0, 96), 15, o_toks)
    t_ar4 = allreduce_pair(15, [t_mxo])
    t_mo = V(lambda e: e.tensor_tensor(scl[0:1, 16:17], scl[0:1, 15:16],
                                       scl[0:1, 14:15], Alu.mult),
             deps=[t_ar4, t_s4u], pub=True, tiny=True)
    t_s5 = V(lambda e: e.tensor_scalar(scl[0:1, 17:18], scl[0:1, 16:17],
                                       INV_QMAX, 1e-12, Alu.mult, Alu.max),
             deps=[t_mo], pub=True, tiny=True)
    t_i5 = V(lambda e: e.reciprocal(scl[0:1, 18:19], scl[0:1, 17:18]),
             deps=[t_s5], pub=True, tiny=True)
    t_t5 = V(lambda e: e.tensor_tensor(scl[0:1, 19:20], scl[0:1, 14:15],
                                       scl[0:1, 18:19], Alu.mult),
             deps=[t_i5], pub=True, tiny=True)
    t_bc4 = bcast_scl([t_t5])
    t_gb5 = V(lambda e: e.tensor_scalar(adp[:, 48:56], adp[:, 40:48],
                                        sclb[:, 17:18], None, Alu.mult),
              deps=[t_bc4, t_gbq], pub=True, tiny=True)

    dbg_cap("scl4", scl[:, :], [t_bc4, t_gb5], 32, parts=1)
    dbg_cap("gb5", adp[:, 48:56], [t_gb5], 8)
    oq_toks = {}
    w3_rt_guard = [None] * 4
    for m in range(24):
        for n in range(4):
            hh = (m * 4 + n) % 4
            rs = rtring[hh]
            t1 = S(lambda e, m=m, n=n, rs=rs: e.activation(
                rs[:, 0:512], gP(m)[:, 512 * n:512 * n + 512], Act.Copy,
                bias=MAGIC, scale=sclb[:, 19:20]),
                deps=[t_bc4, w3_rt_guard[hh]] + o_toks, pub=True)
            oq_toks[(m, n)] = V(lambda e, m=m, n=n, rs=rs: e.tensor_scalar(
                gP(m)[:, 512 * n:512 * n + 512], rs[:, 0:512], MAGIC, None,
                Alu.subtract), deps=[t1], pub=True)
            w3_rt_guard[hh] = oq_toks[(m, n)]

    dbg_cap("oq_0", gP(0)[:, 0:512], [oq_toks[(0, 0)], oq_toks[(0, 1)]], 512)
    st_toks = []
    xr_toks = [None] * 8
    up_mm_done = up_ev[-1]
    for half in range(2):
        for q8 in range(8):
            n = 2 * half + q8 // 4
            q = q8 % 4
            xr_toks[q8] = dma("sync", f"xrr{q8}", xrr[q8][:, :],
                              xspill.ap()[4 * n + q],
                              deps=xsp_toks + (st_toks[-8:] if half else [])
                              + [xq_toks[-1], up_mm_done])
        for mo in range(8):
            pb = mo % 2
            wff_tok[pb] = ffn_col_load(48 + mo, pb, [wff_free[pb]])
            for nn in range(2):
                n = 2 * half + nn
                bk = (mo * 2 + nn) % 4
                mmt = None
                for k in range(24):
                    mmt = TT(lambda e, pb=pb, bk=bk, k=k, n=n: e.matmul(
                        ps_mm[bk][:, :], wffcol[pb][:, 128 * k:128 * k + 128],
                        gP(k)[:, 512 * n:512 * n + 512],
                        start=(k == 0), stop=(k == 23)),
                        deps=[wff_tok[pb], bank_free[bk], oq_toks[(k, n)]],
                        pub=(k == 23))
                sl = (mo * 2 + nn) % 4
                ev = S(lambda e, bk=bk, mo=mo, sl=sl: e.activation(
                    ctile[sl][:, :], ps_mm[bk][:, :], Act.Copy,
                    scale=adp[:, 48 + mo:49 + mo]),
                    deps=[mmt, t_gb5, ctile_free[sl]], pub=True, tiny=True)
                bank_free[bk] = ev
                wff_free[pb] = mmt
                for q in range(4):
                    si = tr_ctr[0] % 8
                    tr_ctr[0] += 1
                    t_tr = TT(lambda e, sl=sl, q=q, si=si: e.transpose(
                        tr_slot(si), ctile[sl][:, 128 * q:128 * q + 128],
                        id_b[:, :]),
                        deps=[ev, tr_guard[si % 2]], pub=True)
                    s8 = 4 * nn + q
                    ra = V(lambda e, s8=s8, mo=mo, si=si: e.tensor_tensor(
                        xrr[s8][:, 128 * mo:128 * mo + 128],
                        xrr[s8][:, 128 * mo:128 * mo + 128], tr_slot(si),
                        Alu.add),
                        deps=[t_tr, xr_toks[s8]], pub=True)
                    tr_guard[si % 2] = ra
                    xr_toks[s8] = ra
                    ctile_free[sl] = t_tr
        for q8 in range(8):
            n = 2 * half + q8 // 4
            q = q8 % 4
            st_toks.append(dma("sync", f"ost{q8 % 4}",
                               out_h[512 * n + 128 * q:512 * n + 128 * q + 128, :],
                               xrr[q8][:, :], deps=[xr_toks[q8]]))

    wait_list = st_toks[-4:]
    if dbg:
        wait_list = wait_list + [last_dma.get("dbgk")]
    emit("sync", lambda e: None, deps=wait_list)

    build_program.n_ops = len(ops)
    build_program.ops_dbg = [(o["eng"], o["fn"].__code__.co_firstlineno,
                              o["waits"], o["incs"]) for o in ops]
    if trunc is not None:
        ops[:] = ops[:trunc]
    with nc.Block() as block:
        def mk(eng_name):
            def run(e):
                for op in ops:
                    if op["eng"] != eng_name:
                        continue
                    for s, v in op["waits"]:
                        e.wait_ge(s, v)
                    ins = op["fn"](e)
                    for s, nn in op["incs"]:
                        assert ins is not None, "inc on op returning no instruction"
                        ins = ins.then_inc(s, nn)
            return run

        block.sync(mk("sync"))
        block.gpsimd(mk("gpsimd"))
        block.vector(mk("vector"))
        block.scalar(mk("scalar"))
        block.tensor(mk("tensor"))
    return nc


_NC_CACHE = None


def _get_nc():
    global _NC_CACHE
    if _NC_CACHE is None:
        _NC_CACHE = build_program(debug_sync=DEBUG_SYNC)
    return _NC_CACHE


def make_inmaps(L0_attn_x, L0_attn_in_w, L0_attn_in_b, L0_attn_conv_w,
                L0_attn_conv_b, L0_attn_xp_w, L0_attn_xp_b, L0_attn_out_w,
                L0_attn_out_b, L0_ffn_w_gate, L0_ffn_w_up, L0_ffn_w_down,
                L0_c, L0_wmod):
    f = np.float32
    x = np.ascontiguousarray(L0_attn_x, dtype=f).reshape(16384, D)
    in_w = np.ascontiguousarray(L0_attn_in_w, dtype=f)
    winT = np.ascontiguousarray(
        in_w.reshape(16, 128, 8, 128).transpose(0, 3, 2, 1))
    inb = np.ascontiguousarray(
        np.asarray(L0_attn_in_b, dtype=f).reshape(16, 128).T)
    convw = np.ascontiguousarray(
        np.asarray(L0_attn_conv_w, dtype=f).reshape(16, 128, 4)
        .transpose(1, 0, 2).reshape(128, 64))
    convb = np.ascontiguousarray(
        np.asarray(L0_attn_conv_b, dtype=f).reshape(16, 128).T)
    xpwT = np.ascontiguousarray(
        np.ascontiguousarray(L0_attn_xp_w, dtype=f).T.reshape(16, 128, 32))
    xpba = np.ascontiguousarray(np.asarray(L0_attn_xp_b, dtype=f)[:16].reshape(16, 1))
    xpbb = np.ascontiguousarray(np.asarray(L0_attn_xp_b, dtype=f)[16:].reshape(16, 1))
    outwT = np.ascontiguousarray(
        np.asarray(L0_attn_out_w, dtype=f).T.reshape(16, 8, 128).transpose(1, 0, 2))
    outb = np.ascontiguousarray(
        np.asarray(L0_attn_out_b, dtype=f).reshape(8, 128).T)
    wg = np.ascontiguousarray(L0_ffn_w_gate, dtype=f)
    wu = np.ascontiguousarray(L0_ffn_w_up, dtype=f)
    wd = np.ascontiguousarray(L0_ffn_w_down, dtype=f)
    cvec = np.ascontiguousarray(np.asarray(L0_c, dtype=f).reshape(8, 128).T)
    wmod = np.ascontiguousarray(L0_wmod, dtype=f)

    identf = np.eye(128, dtype=f)
    identb = np.eye(128, dtype=ml_dtypes.bfloat16)
    egu = np.zeros((32, 8 * 128), dtype=f)
    for kc in range(8):
        for kp in range(128):
            egu[4 * kc + kp // 32, 128 * kc + kp] = 1.0
    ed = np.zeros((96, 24 * 128), dtype=f)
    for kc in range(24):
        for kp in range(128):
            ed[4 * kc + kp // 32, 128 * kc + kp] = 1.0

    in_maps = []
    for r in range(NCORES):
        xs = np.empty((TH, D), dtype=f)
        t0 = r * T
        if r == 0:
            xs[:HALO] = 0.0
        else:
            xs[:HALO] = x[t0 - HALO:t0]
        xs[HALO:] = x[t0:t0 + T]
        rows_gu = np.empty((6, 128, D), dtype=f)
        lhsT_gu = np.empty((6, 128, 8, 128), dtype=f)
        for s in range(6):
            j = r + 8 * s
            if j < 24:
                W, m = wg, j
            else:
                W, m = wu, j - 24
            blk = W[128 * m:128 * m + 128, :]
            rows_gu[s] = blk
            lhsT_gu[s] = blk.reshape(128, 8, 128).transpose(2, 1, 0)
        md = (r + 48) - 48
        blkd = wd[128 * md:128 * md + 128, :]
        rows_d = np.ascontiguousarray(blkd)
        lhsT_d = np.ascontiguousarray(blkd.reshape(128, 24, 128).transpose(2, 1, 0))
        wms = wmod[:, 768 * r:768 * r + 768]
        wmod_t = np.ascontiguousarray(
            wms.reshape(8, 128, 6, 128).transpose(2, 0, 1, 3))
        onehot = np.zeros((NS, 8), dtype=f)
        if r > 0:
            onehot[:, r - 1] = 1.0
        hmask = np.full((128, 1), 0.0 if r == 0 else 1.0, dtype=f)
        in_maps.append({
            "x": np.ascontiguousarray(xs),
            "winT": winT, "inb": inb, "convw": convw, "convb": convb,
            "xpwT": xpwT, "xpba": xpba, "xpbb": xpbb, "outwT": outwT, "outb": outb,
            "lpq_rows_gu": np.ascontiguousarray(rows_gu),
            "lpq_lhsT_gu": np.ascontiguousarray(lhsT_gu),
            "lpq_rows_d": rows_d, "lpq_lhsT_d": lhsT_d,
            "cvec": cvec, "wmod": wmod_t,
            "identf": identf, "identb": identb, "egu": egu, "ed": ed,
            "onehot": onehot, "hmask": hmask,
            "ones1": np.ones((1, 128), dtype=f),
        })

    return in_maps


def kernel(**inputs):
    in_maps = make_inmaps(**inputs)
    nc = _get_nc()
    res = run_bass_kernel_spmd(nc, in_maps, core_ids=list(range(NCORES)))
    out = np.concatenate([np.asarray(r["out"], dtype=np.float32)
                          for r in res.results], axis=0)
    return out.reshape(1, 16384, D)


# version 10
# speedup vs baseline: 1.1722x; 1.0521x over previous
import numpy as np
import ml_dtypes
import concourse.bass as bass
import concourse.mybir as mybir
from concourse.bass_utils import run_bass_kernel_spmd

F32 = mybir.dt.float32
BF16 = mybir.dt.bfloat16
Alu = mybir.AluOpType
Act = mybir.ActivationFunctionType
AX = mybir.AxisListType.X
MAGIC = float(np.float32(1.5 * 2.0 ** 23))
INV_QMAX = float(np.float32(1.0 / 32767.0))

NCORES = 8
T = 2048
HALO = 128
TH = T + HALO
D = 1024
DI = 2048
FF = 3072
NS = 16
EPS = 1e-5

SLOT_OFF = [0, 262144, 524288, 786432, 1048576, 1310720, 1572864]
RANK_BYTES = 2359296


def col_addr(j):
    return j % NCORES, SLOT_OFF[j // NCORES] // 2


DEBUG_SYNC = False


def build_program(debug_sync=False, trunc=None, dbg=False):
    nc = bass.Bass("TRN2", target_bir_lowering=False,
                   detect_race_conditions=debug_sync)

    def din(name, shape, dt=F32):
        return nc.dram_tensor(name, shape, dt, kind="ExternalInput").ap()

    x_h = din("x", [TH, D])
    winT_h = din("winT", [16, 128, 8, 128])
    inb_h = din("inb", [128, 16])
    convw_h = din("convw", [128, 64])
    convb_h = din("convb", [128, 16])
    xpwT_h = din("xpwT", [16, 128, 32])
    xpba_h = din("xpba", [16, 1])
    xpbb_h = din("xpbb", [16, 1])
    outwT_h = din("outwT", [8, 16, 128])
    outb_h = din("outb", [128, 8])
    lpq_rows_gu_h = din("lpq_rows_gu", [6, 128, D])
    lpq_lhsT_gu_h = din("lpq_lhsT_gu", [6, 128, 8, 128])
    lpq_rows_d_h = din("lpq_rows_d", [128, FF])
    lpq_lhsT_d_h = din("lpq_lhsT_d", [128, 24, 128])
    c_h = din("cvec", [128, 8])
    wmod_h = din("wmod", [6, 8, 128, 128])
    ident_f_h = din("identf", [128, 128])
    ident_b_h = din("identb", [128, 128], BF16)
    egu_h = din("egu", [32, 8 * 128])
    ed_h = din("ed", [96, 24 * 128])
    onehot_h = din("onehot", [NS, 8])
    hmask_h = din("hmask", [128, 1])
    ones1_h = din("ones1", [1, 128])

    out_h = nc.dram_tensor("out", [T, D], F32, kind="ExternalOutput").ap()
    dbg_h = nc.dram_tensor("dbg", [128, 16384], F32,
                           kind="ExternalOutput").ap() if dbg else None

    ag0_in = nc.dram_tensor("ag0_in", [1, 768], F32)
    ag0_out = nc.dram_tensor("ag0_out", [NCORES, 768], F32)
    agw_in = nc.dram_tensor("agw_in", [1, RANK_BYTES // 2], BF16)
    agw_out = nc.dram_tensor("agw_out", [NCORES, RANK_BYTES // 2], BF16)
    ag1_in = nc.dram_tensor("ag1_in", [1, 32], F32)
    ag1_out = nc.dram_tensor("ag1_out", [NCORES, 32], F32)
    ar_in = nc.dram_tensor("ar_in", [1, 2], F32)
    ar_out = nc.dram_tensor("ar_out", [1, 2], F32)
    sumd = nc.dram_tensor("sumd", [1, T], F32)
    mxd = nc.dram_tensor("mxd", [128, 1], F32)
    scld = nc.dram_tensor("scld", [1, 32], F32)
    xspill = nc.dram_tensor("xspill", [16, 128, D], F32)
    upspill = nc.dram_tensor("upspill", [24, 4, 128, 512], BF16)

    SBUF_SZ = nc.SBUF_PARTITION_SIZE_BYTES
    AR_S = SBUF_SZ - 34816 - 32
    AR_H = AR_S - 69632
    AR_X = AR_H - 65536

    def at(name, shape, dt, off):
        return nc.alloc_sbuf_tensor_at(name, shape, dt, offset=off)

    x_tm = [at(f"x{i}", [128, D], F32, AR_X + 4096 * i) for i in range(16)]
    gP_lo = [at(f"gPl{m}", [128, T], BF16, AR_X + 4096 * m) for m in range(16)]
    wrow = at("wrow", [128, FF], F32, AR_H + 0)
    wlhs = at("wlhs", [128, FF], F32, AR_H + 12288)
    wqcol = at("wqcol", [128, FF], BF16, AR_H + 24576)
    wmodb = at("wmodb", [128, 6 * 1024], F32, AR_H + 30720)
    ed_s = at("ed_s", [96, 24 * 128], F32, AR_H + 55296)
    smax = at("smax", [128, 96], F32, AR_H + 67584)
    sval = at("sval", [128, 96], F32, AR_H + 67968)
    sinv = at("sinv", [128, 96], F32, AR_H + 68352)
    h_bf = [at(f"h{m}", [128, TH], BF16, AR_H + 4352 * m) for m in range(16)]
    gP_hi = [at(f"gPh{m}", [128, T], BF16, AR_H + 4096 * m) for m in range(8)]
    hx = [at(f"hx{j}", [128, T], BF16, AR_H + 32768 + 4096 * j) for j in range(8)]
    xrr = [at(f"xrr{q}", [128, D], F32, AR_H + 32768 + 4096 * q) for q in range(8)]
    h1_bf = [at(f"h1_{j}", [128, TH], BF16, AR_S + 4352 * j) for j in range(8)]
    scanA = at("scanA", [32, T], F32, AR_S + 0)
    scanB = at("scanB", [16, T], F32, AR_S + 8192)
    scanC = at("scanC", [16, T], F32, AR_S + 16384)
    scanD = at("scanD", [16, T], F32, AR_S + 24576)
    y_bf = at("ybf", [16, T], BF16, AR_S + 24576)
    wffcol = [at(f"wff{p}", [128, FF], BF16, AR_S + 6144 * p) for p in range(2)]
    uprng = [at(f"upr{p}", [128, 512], BF16, AR_S + 12288 + 1024 * p) for p in range(8)]
    rtring = [at(f"rt{p}", [128, 512], F32, AR_S + 20480 + 2048 * p) for p in range(4)]
    ctile = [at(f"ct{p}", [128, 512], BF16, AR_S + 28672 + 1024 * p) for p in range(4)]

    def gP(m):
        return gP_lo[m] if m < 16 else gP_hi[m - 16]

    def psk(bk):
        return (ps_mm + [ps_a])[bk]

    A = nc.alloc_sbuf_tensor
    win_bf = [A(f"winb{p}", [128, 1024], BF16) for p in range(2)]
    hc = A("hc", [128, T], F32)
    sq1 = A("sq1", [128, D], F32)
    xstage = A("xstage", [128, D], F32)
    xhat = [A(f"xhat{p}", [128, D], BF16) for p in range(2)]

    convw = A("convw_s", [128, 64], F32)
    convb = A("convb_s", [128, 16], F32)
    xpw_bf = A("xpwb", [128, 512], BF16)
    xpba_s = A("xpba_s", [16, 1], F32)
    xpbb_s = A("xpbb_s", [16, 1], F32)
    outw_bf = A("outwb", [16, 1024], BF16)
    outb_s = A("outb_s", [128, 8], F32)
    gaob = A("gaob", [128, 8], F32)
    ones1 = A("ones1_s", [1, 128], F32)
    adp = A("adp", [128, 64], F32)
    silc = A("silc", [128, 8], F32)
    inb_s = A("inb_s", [128, 16], F32)
    id_f = A("id_f", [128, 128], F32)
    id_b = A("id_b", [128, 128], BF16)
    egu_s = A("egu_s", [32, 8 * 128], F32)
    sT_s = A("sT_s", [96, 128], F32)
    invT_s = A("invT_s", [96, 128], F32)
    ones_c = A("ones_c", [128, 1], BF16)
    mxc = A("mxc", [128, 100], F32)
    mxc2 = A("mxc2", [128, 96], F32)
    sumstg = A("sumstg", [1, 512], F32)
    lnsum = A("lnsum", [128, 20], F32)
    lnsq = A("lnsq", [128, 20], F32)
    lnmean = A("lnmean", [128, 20], F32)
    lnms = A("lnms", [128, 20], F32)
    lnrstd = A("lnrstd", [128, 20], F32)
    lnm2, lnvar, lnrc = lnsum, lnsq, lnms
    gqs = A("gqs", [128, 8], F32)
    sq1b = A("sq1b", [128, D], F32)
    scl = A("scl", [1, 32], F32)
    sclb = A("sclb", [128, 32], F32)
    onehot_s = A("onehot_s", [NS, 8], F32)
    sin_s = A("sin_s", [NS, 1], F32)
    cgat = A("cgat", [NS, 8], F32)
    cgee = A("cgee", [NS, 8], F32)
    csc = A("csc", [NS, 8], F32)
    hmask_s = A("hmask_s", [128, 1], F32)
    modloc = A("modloc", [128, 6], F32)



    canary = A("canary", [1, 8], F32)
    can_addr = nc.lookup_mloc(canary).addr
    assert can_addr + 32 <= AR_X, (
        f"auto SBUF allocations overflow into manual arena: {can_addr} > {AR_X}")
    ps_mm = [nc.alloc_psum_tensor(f"psmm{i}", [128, 512], F32) for i in range(4)]
    ps_tr = nc.alloc_psum_tensor("pstr", [128, 1024], BF16)
    ps_tr2 = nc.alloc_psum_tensor("pstr2", [128, 1024], BF16)
    ps_a = nc.alloc_psum_tensor("psa", [128, 512], F32)
    ps_b = nc.alloc_psum_tensor("psb", [128, 512], F32)

    sem_h = {n: nc.alloc_semaphore(n) for n in ["sT", "sV", "sS", "sG"]}
    cnt = {n: 0 for n in sem_h}
    ENG_SEM = {"vector": "sV", "scalar": "sS", "tensor": "sT", "gpsimd": "sG"}
    ops = []
    watermark = {e: {} for e in ["sync", "gpsimd", "vector", "scalar", "tensor"]}

    def sem_for(key):
        if key not in sem_h:
            sem_h[key] = nc.alloc_semaphore(key)
            cnt[key] = 0
        return sem_h[key]

    class Tok:
        __slots__ = ("eng", "sem", "val", "tiny")

        def __init__(self, eng, sem, val, tiny=False):
            self.eng, self.sem, self.val, self.tiny = eng, sem, val, tiny

    def _flat(deps):
        for d in deps:
            if d is None:
                continue
            if isinstance(d, (list, tuple)):
                yield from _flat(d)
            else:
                yield d

    def emit(eng, fn, deps=(), pub=False, dkey=None, cc=None, tiny=False):
        need = {}
        for d in _flat(deps):
            if (d.sem in ("sT", "sV", "sS", "sG") and d.eng == eng
                    and not debug_sync and not d.tiny):
                continue
            if need.get(d.sem, -1) < d.val:
                need[d.sem] = d.val
        waits = []
        wm = watermark[eng]
        for s, v in need.items():
            if wm.get(s, -1) >= v:
                continue
            wm[s] = v
            waits.append((sem_h[s], v))
        incs = []
        tok = None
        if cc is not None:
            sem_for(cc)
            cnt[cc] += 1
            incs.append((sem_h[cc], 1))
            tok = Tok(eng, cc, cnt[cc])
        elif dkey is not None:
            sem_for(dkey)
            cnt[dkey] += 16
            incs.append((sem_h[dkey], 16))
            tok = Tok(eng, dkey, cnt[dkey])
        elif pub or (debug_sync and eng in ENG_SEM):
            s = ENG_SEM[eng]
            cnt[s] += 1
            incs.append((sem_h[s], 1))
            tok = Tok(eng, s, cnt[s], tiny)
        ops.append({"eng": eng, "fn": fn, "waits": waits, "incs": incs})
        return tok

    last_dma = {}
    dbg_off = [0]
    dbg_names = []

    def dbg_cap(name, ap, deps, cols, parts=128):
        if not dbg:
            return
        o = dbg_off[0]
        dbg_names.append((name, o, cols, parts))
        dbg_off[0] += cols
        dma("gpsimd", "dbgk", dbg_h[0:parts, o:o + cols], ap, deps=list(deps),
            slow=True)
    build_program.dbg_names = dbg_names

    def dma(q, key, out, in_, deps=(), slow=False, chain=True):
        deps = list(deps)
        if chain and key in last_dma:
            deps.append(last_dma[key])
        t = emit(q, lambda e, o=out, i=in_, s=slow: e.dma_start(
            out=o, in_=i, allow_slow_non_contiguous=s), deps=deps, dkey=key)
        last_dma[key] = t
        return t

    def V(fn, deps=(), pub=False, tiny=False):
        return emit("vector", fn, deps, pub=pub, tiny=tiny)

    def S(fn, deps=(), pub=False, tiny=False):
        return emit("scalar", fn, deps, pub=pub, tiny=tiny)

    def TT(fn, deps=(), pub=False):
        return emit("tensor", fn, deps, pub=pub)

    l_convw = dma("sync", "ini0", convw[:, :], convw_h)
    l_convb = dma("sync", "ini1", convb[:, :], convb_h)
    l_inb = dma("sync", "ini2", inb_s[:, :], inb_h)
    l_xpb = dma("sync", "ini3", xpba_s[:, :], xpba_h)
    l_xpb2 = dma("sync", "ini1", xpbb_s[:, :], xpbb_h)
    l_idf = dma("sync", "ini0", id_f[:, :], ident_f_h)
    l_idb = dma("sync", "ini1", id_b[:, :], ident_b_h)
    l_egu = dma("sync", "ini2", egu_s[:, :], egu_h)
    l_ed = dma("sync", "ini3", ed_s[:, :], ed_h)
    l_outb = dma("sync", "ini0", outb_s[:, :], outb_h)
    l_oh = dma("sync", "ini1", onehot_s[:, :], onehot_h)
    l_hm = dma("sync", "ini2", hmask_s[:, :], hmask_h)
    l_ones1 = dma("sync", "ini3", ones1[:, :], ones1_h)
    l_c = dma("sync", "ini3", silc[:, :], c_h)
    l_wmod = dma("sync", "wmodk", wmodb[:, :].rearrange("p (m k q) -> p m k q", m=6, k=8),
                  wmod_h.rearrange("m k p q -> p m k q"))
    win_tok = [None, None]
    win_free = [None, None]
    for _pb in range(2):
        win_tok[_pb] = dma("gpsimd", f"win{_pb}",
                           win_bf[_pb][:, :].rearrange("p (k q) -> p k q", q=128),
                           winT_h[_pb], deps=[])
    l_xpw = dma("gpsimd", "gini0", xpw_bf[:, :].rearrange("p (k t) -> p k t", t=32),
                 xpwT_h.rearrange("k p t -> p k t"))
    l_outw = dma("gpsimd", "gini1", outw_bf[:, :].rearrange("p (m t) -> p m t", t=128),
                  outwT_h.rearrange("m p t -> p m t"))
    t_ones = V(lambda e: e.memset(ones_c[:, :], 1.0), pub=True)
    t_scl0 = V(lambda e: e.memset(scl[:, :], 0.0), pub=True)

    t_silc = S(lambda e: e.activation(silc[:, :], silc[:, :], Act.Silu),
               deps=[l_c], pub=True, tiny=True)

    mod_mm = []
    for m in range(6):
        for k in range(8):
            tk = TT(lambda e, m=m, k=k: e.matmul(
                ps_b[:, m:m + 1],
                wmodb[:, (m * 8 + k) * 128:(m * 8 + k) * 128 + 128],
                silc[:, k:k + 1], start=(k == 0), stop=(k == 7)),
                deps=[l_wmod, t_silc], pub=(k == 7))
        mod_mm.append(tk)
    mod_ev = []
    for m in range(6):
        mod_ev.append(V(lambda e, m=m: e.tensor_copy(
            modloc[:, m:m + 1], ps_b[:, m:m + 1]), deps=mod_mm, pub=True, tiny=True))
    t_modw = None
    for m in range(6):
        t_modw = dma("gpsimd", "ag0w", ag0_in.ap()[0:1, 128 * m:128 * m + 128]
                     .rearrange("one p -> p one"), modloc[:, m:m + 1],
                     deps=[mod_ev[m]])
    t_ag0 = emit("gpsimd", lambda e: e.collective_compute(
        "AllGather", Alu.bypass, replica_groups=[list(range(NCORES))],
        ins=[ag0_in.ap().opt()], outs=[ag0_out.ap().opt()]),
        deps=[t_modw], cc="cc0")
    flat0 = ag0_out.ap().rearrange("r v -> () (r v)")
    t_ad = []
    for col in range(48):
        vi, j = col // 8, col % 8
        g = vi * 8 + j
        t_ad.append(dma("gpsimd", f"adpu{col % 4}",
                        adp[:, col:col + 1],
                        flat0[0:1, 128 * g:128 * g + 128].rearrange(
                            "one p -> p one"), deps=[t_ag0]))
    t_ca1 = V(lambda e: e.tensor_scalar_add(adp[:, 8:16], adp[:, 8:16], 1.0),
              deps=t_ad, pub=True, tiny=True)
    t_cb1 = V(lambda e: e.tensor_scalar_add(adp[:, 32:40], adp[:, 32:40], 1.0),
              deps=t_ad, pub=True, tiny=True)
    gq_toks = []
    prev_gq = None
    for src0 in (16, 40):
        tsg = S(lambda e, s=src0: e.activation(adp[:, 48:56], adp[:, s:s + 8],
                                               Act.Sign),
                deps=t_ad + ([prev_gq] if prev_gq else []), pub=True, tiny=True)
        tab = S(lambda e, s=src0: e.activation(adp[:, 56:64], adp[:, s:s + 8],
                                               Act.Abs), deps=t_ad, pub=True, tiny=True)
        tmx = V(lambda e: e.tensor_scalar(gqs[:, :], adp[:, 56:64], 0.25,
                                          None, Alu.max),
                deps=[tab], pub=True, tiny=True)
        prev_gq = V(lambda e, s=src0: e.tensor_tensor(
            adp[:, s:s + 8], adp[:, 48:56], gqs[:, :], Alu.mult),
            deps=[tsg, tmx], pub=True, tiny=True)

        gq_toks.append(prev_gq)
    t_gaq, t_gbq = gq_toks

    dbg_cap("adp", adp[:, :], [t_gaq, t_gbq, t_ca1, t_cb1], 64)
    tr_guard = [None, None]

    def tr_slot(si):
        return (ps_tr if si % 2 == 0 else ps_tr2)[
            :, 128 * ((si // 2) % 4):128 * ((si // 2) % 4) + 128]

    h1_toks = []
    x_toks = [None] * 16
    lnprev = [None, None]
    tr_ctr = [0]
    xhat_guard = [None, None]

    def ln_sums(i, col, dst, ap_dst_tok, extra_deps=()):
        par = i % 2
        sq_ = sq1 if par == 0 else sq1b
        t_sum = V(lambda e, col=col: e.tensor_reduce(
            lnsum[:, col:col + 1], dst[:, :], AX, Alu.add),
            deps=[ap_dst_tok] + list(extra_deps), pub=True, tiny=True)
        t_sq0 = S(lambda e, sq_=sq_: e.activation(sq_[:, :], dst[:, :],
                                                  Act.Square),
                  deps=[ap_dst_tok, lnprev[par]] + list(extra_deps), pub=True)
        t_sq = V(lambda e, col=col, sq_=sq_: e.tensor_reduce(
            lnsq[:, col:col + 1], sq_[:, :], AX, Alu.add),
            deps=[t_sq0], pub=True, tiny=True)
        lnprev[par] = t_sq
        return [t_sum, t_sq]

    def ln_finish(ncols, deps):
        c = ncols
        t1 = V(lambda e: e.tensor_scalar_mul(lnmean[:, 0:c], lnsum[:, 0:c],
                                             1.0 / D), deps=deps, pub=True,
               tiny=True)
        t2 = V(lambda e: e.tensor_scalar_mul(lnms[:, 0:c], lnsq[:, 0:c],
                                             1.0 / D), deps=deps, pub=True,
               tiny=True)
        t3 = V(lambda e: e.tensor_tensor(lnm2[:, 0:c], lnmean[:, 0:c],
                                         lnmean[:, 0:c], Alu.mult),
               deps=[t1], pub=True, tiny=True)
        t4 = V(lambda e: e.tensor_tensor(lnvar[:, 0:c], lnms[:, 0:c],
                                         lnm2[:, 0:c], Alu.subtract),
               deps=[t2, t3], pub=True, tiny=True)
        t5 = V(lambda e: e.tensor_scalar_add(lnrc[:, 0:c], lnvar[:, 0:c], EPS),
               deps=[t4], pub=True, tiny=True)
        t6 = V(lambda e: e.reciprocal(lnsum[:, 0:c], lnrc[:, 0:c]),
               deps=[t5], pub=True, tiny=True)
        return S(lambda e: e.activation(lnrstd[:, 0:c], lnsum[:, 0:c],
                                        Act.Sqrt),
                 deps=[t6], pub=True, tiny=True)

    def ln_apply(i, col, dst, t_stats, affine_c1, affine_c0, out_chunk_fn,
                 extra_deps=()):
        xh = xhat[i % 2]
        t_xh = V(lambda e, col=col: e.tensor_scalar(
            xh[:, :], dst[:, :], lnmean[:, col:col + 1],
            lnrstd[:, col:col + 1], Alu.subtract, Alu.mult),
            deps=[t_stats, xhat_guard[i % 2]] + list(extra_deps), pub=True)
        evs = []
        t_tr = None
        for j in range(8):
            si = tr_ctr[0] % 8
            tr_ctr[0] += 1
            t_tr = TT(lambda e, j=j, si=si, xh=xh: e.transpose(
                tr_slot(si), xh[:, 128 * j:128 * j + 128], id_b[:, :]),
                deps=[t_xh, l_idb, tr_guard[si % 2]], pub=True)
            ev = V(lambda e, j=j, si=si: e.tensor_scalar(
                out_chunk_fn(j), tr_slot(si),
                adp[:, affine_c1 + j:affine_c1 + j + 1],
                adp[:, affine_c0 + j:affine_c0 + j + 1], Alu.mult, Alu.add),
                deps=[t_tr, t_ca1, t_cb1], pub=True, tiny=True)
            tr_guard[si % 2] = ev
            evs.append(ev)
        xhat_guard[i % 2] = t_tr
        return evs

    agw_writes = []
    lp_guard = [None]
    lp_rm_guard = [None]

    def lpbq_slot(s):
        I = D if s < 6 else FF
        B = I // 32
        if s < 6:
            t_row = dma("sync", "lpr", wrow[:, 0:I], lpq_rows_gu_h[s],
                        deps=[lp_rm_guard[0]])
            t_lh = dma("sync", "lpl",
                       wlhs[:, 0:I].rearrange("p (k q) -> p k q", q=128),
                       lpq_lhsT_gu_h[s], deps=[lp_guard[0]])
        else:
            t_row = dma("sync", "lpr", wrow[:, 0:I], lpq_rows_d_h,
                        deps=[lp_rm_guard[0]])
            t_lh = dma("sync", "lpl",
                       wlhs[:, 0:I].rearrange("p (k q) -> p k q", q=128),
                       lpq_lhsT_d_h, deps=[lp_guard[0]])
        t_rm = V(lambda e, I=I, B=B: e.tensor_reduce(
            smax[:, 0:B].rearrange("p b -> p b ()"),
            wrow[:, 0:I].rearrange("p (b t) -> p b t", t=32),
            AX, Alu.max, apply_absolute_value=True), deps=[t_row], pub=True, tiny=True)
        lp_rm_guard[0] = t_rm
        t_s = V(lambda e, B=B: e.tensor_scalar(
            sval[:, 0:B], smax[:, 0:B], 1.0 / 7.0, 1e-12, Alu.mult, Alu.max),
            deps=[t_rm], pub=True, tiny=True)
        t_i = V(lambda e, B=B: e.reciprocal(sinv[:, 0:B], sval[:, 0:B]),
                deps=[t_s], pub=True, tiny=True)
        t_t1 = TT(lambda e, B=B: e.transpose(ps_mm[2][0:B, 0:128], sval[:, 0:B],
                                             id_f[:, :]),
                  deps=[t_s, l_idf, lp_guard[0]], pub=True)
        t_t2 = TT(lambda e, B=B: e.transpose(ps_mm[3][0:B, 0:128], sinv[:, 0:B],
                                             id_f[:, :]),
                  deps=[t_i, l_idf, lp_guard[0]], pub=True)
        B4 = B
        ev1 = V(lambda e, B=B: e.tensor_copy(sT_s[0:B, :], ps_mm[2][0:B, 0:128]),
                deps=[t_t1], pub=True, tiny=True)
        ev2 = V(lambda e, B=B: e.tensor_copy(invT_s[0:B, :], ps_mm[3][0:B, 0:128]),
                deps=[t_t2], pub=True, tiny=True)
        E_s = egu_s if s < 6 else ed_s
        l_E = l_egu if s < 6 else l_ed
        wq_toks = []
        piece_guard = [agw_writes[-1]] if agw_writes else []
        for p in range(I // 512):
            exps = []
            for q in range(4):
                kc = 4 * p + q
                exps.append(TT(lambda e, q=q, kc=kc, B=B, E_s=E_s: e.matmul(
                    ps_a[:, 128 * q:128 * q + 128],
                    E_s[0:B, 128 * kc:128 * kc + 128], sT_s[0:B, :],
                    start=True, stop=True),
                    deps=[ev1, ev2, l_E] + mod_ev + piece_guard, pub=(q == 3)))
                exps.append(TT(lambda e, q=q, kc=kc, B=B, E_s=E_s: e.matmul(
                    ps_b[:, 128 * q:128 * q + 128],
                    E_s[0:B, 128 * kc:128 * kc + 128], invT_s[0:B, :],
                    start=True, stop=True),
                    deps=[ev1, ev2, l_E] + mod_ev + piece_guard, pub=(q == 3)))
            c0 = 512 * p
            t_u = V(lambda e, c0=c0: e.tensor_tensor(
                hc[:, 0:512], wlhs[:, c0:c0 + 512], ps_b[:, 0:512], Alu.mult),
                deps=[t_lh, exps[-1], exps[-2]], pub=True)
            t_t = V(lambda e: e.tensor_scalar_add(hc[:, 512:1024], hc[:, 0:512],
                                                  MAGIC), deps=[t_u], pub=True)
            wq_toks.append(V(lambda e, c0=c0: e.scalar_tensor_tensor(
                wqcol[:, c0:c0 + 512], hc[:, 512:1024], MAGIC, ps_a[:, 0:512],
                Alu.subtract, Alu.mult), deps=[t_t], pub=True))
            piece_guard = [t_u, wq_toks[-1]]
        lp_guard[0] = wq_toks[-1]
        eo = SLOT_OFF[s] // 2
        agw_writes.append(dma(
            "gpsimd", "agww",
            agw_in.ap()[0:1, eo:eo + 128 * I].rearrange(
                "one (p q) -> (one p) q", p=128),
            wqcol[:, 0:I], deps=wq_toks))

    x_dst = []
    sum_deps = []
    for i in range(17):
        dst = xstage if i == 0 else x_tm[i - 1]
        x_dst.append(dst)
        t_x = dma("sync", f"xl{i % 4}", dst[:, :], x_h[128 * i:128 * i + 128, :])
        if i >= 1:
            x_toks[i - 1] = t_x
        sum_deps += ln_sums(i, i, dst, t_x)
        if i < 7:
            lpbq_slot(i)
    t_ln1 = ln_finish(17, sum_deps)
    for i in range(17):
        evs = ln_apply(i, i, x_dst[i], t_ln1, 8, 0,
                       lambda j, i=i: h1_bf[j][:, 128 * i:128 * i + 128])
        h1_toks.extend(evs)
    dbg_cap("h1_0", h1_bf[0][:, 0:512], h1_toks, 512)
    t_agw = emit("gpsimd", lambda e: e.collective_compute(
        "AllGather", Alu.bypass, replica_groups=[list(range(NCORES))],
        ins=[agw_in.ap().opt()], outs=[agw_out.ap().opt()]),
        deps=agw_writes, cc="ccw")

    NTILES_IN = [(0, 128), (128, 512), (640, 512), (1152, 512), (1664, 512)]
    bank_free = [None] * 4
    hprime = []
    for m in range(16):
        pb = m % 2
        if m >= 2:
            win_tok[pb] = dma("gpsimd", f"win{pb}",
                              win_bf[pb][:, :].rearrange("p (k q) -> p k q",
                                                         q=128),
                              winT_h[m], deps=[win_free[pb]])
        ev_toks = []
        for ni, (c0, w) in enumerate(NTILES_IN):
            bk = (m * 5 + ni) % 4
            last_tile = (c0 + w - 1) // 128
            h1_dep = h1_toks[:8 * (last_tile + 1)]
            mmt = None
            for k in range(8):
                mmt = TT(lambda e, pb=pb, bk=bk, k=k, c0=c0, w=w: e.matmul(
                    ps_mm[bk][:, 0:w], win_bf[pb][:, 128 * k:128 * k + 128],
                    h1_bf[k][:, c0:c0 + w], start=(k == 0), stop=(k == 7)),
                    deps=[win_tok[pb], bank_free[bk]] + (h1_dep if k == 0 else []),
                    pub=(k == 7))
            ev = S(lambda e, m=m, bk=bk, c0=c0, w=w: e.activation(
                h_bf[m][:, c0:c0 + w], ps_mm[bk][:, 0:w], Act.Identity,
                bias=inb_s[:, m:m + 1]),
                deps=[mmt, l_inb] + (agw_writes if m <= 7 else []), pub=True,
                tiny=True)
            bank_free[bk] = ev
            ev_toks.append(ev)
        win_free[pb] = ev_toks[-1]
        t_msk = V(lambda e, m=m: e.tensor_scalar(
            h_bf[m][:, HALO - 3:HALO], h_bf[m][:, HALO - 3:HALO],
            hmask_s[:, 0:1], None, Alu.mult), deps=ev_toks + [l_hm], pub=True)
        t_c = V(lambda e, m=m: e.tensor_scalar(
            hc[:, 0:T], h_bf[m][:, HALO - 3:HALO - 3 + T], convw[:, 4 * m:4 * m + 1],
            None, Alu.mult),
            deps=[t_msk, l_convw, hprime[-1] if hprime else None], pub=True)
        for j in (1, 2, 3):
            t_c = V(lambda e, m=m, j=j: e.scalar_tensor_tensor(
                hc[:, 0:T], h_bf[m][:, HALO - 3 + j:HALO - 3 + j + T],
                convw[:, 4 * m + j:4 * m + j + 1], hc[:, 0:T], Alu.mult, Alu.add),
                deps=[t_c], pub=True)
        hprime.append(S(lambda e, m=m: e.activation(
            h_bf[m][:, HALO:HALO + T], hc[:, 0:T], Act.Silu,
            bias=convb[:, m:m + 1]), deps=[t_c, l_convb], pub=True))

    dbg_cap("hprime0", h_bf[0][:, HALO:HALO + 512], hprime, 512)
    xp_toks = []
    for n in range(4):
        mmta = None
        mmtb = None
        for k in range(16):
            mmta = TT(lambda e, n=n, k=k: e.matmul(
                ps_a[0:16, 0:512], xpw_bf[:, 32 * k:32 * k + 16],
                h_bf[k][:, HALO + 512 * n:HALO + 512 * n + 512],
                start=(k == 0), stop=(k == 15)),
                deps=[l_xpw, hprime[k]] + (xp_toks[-2:] if xp_toks else []),
                pub=(k == 15))
            mmtb = TT(lambda e, n=n, k=k: e.matmul(
                ps_mm[0][0:16, 0:512], xpw_bf[:, 32 * k + 16:32 * k + 32],
                h_bf[k][:, HALO + 512 * n:HALO + 512 * n + 512],
                start=(k == 0), stop=(k == 15)),
                deps=[l_xpw, hprime[k], bank_free[0]]
                + (xp_toks[-2:] if xp_toks else []),
                pub=(k == 15))
        xp_toks.append(V(lambda e, n=n: e.tensor_scalar(
            scanA[0:16, 512 * n:512 * n + 512], ps_a[0:16, 0:512],
            xpba_s[0:16, 0:1], None, Alu.add), deps=[mmta, l_xpb], pub=True))
        xp_toks.append(V(lambda e, n=n: e.tensor_scalar(
            scanC[0:16, 512 * n:512 * n + 512], ps_mm[0][0:16, 0:512],
            xpbb_s[0:16, 0:1], None, Alu.add), deps=[mmtb, l_xpb2], pub=True))
    sum_toks = []
    for n in range(4):
        mmt = None
        for k in range(16):
            mmt = TT(lambda e, n=n, k=k: e.matmul(
                ps_b[0:1, 0:512], ones_c[:, :],
                h_bf[k][:, HALO + 512 * n:HALO + 512 * n + 512],
                start=(k == 0), stop=(k == 15)),
                deps=[t_ones, hprime[k]] + (sum_toks[-1:] if sum_toks else []),
                pub=(k == 15))
        t_se = V(lambda e: e.tensor_copy(sumstg[:, :], ps_b[0:1, 0:512]),
                 deps=[mmt, sum_toks[-1] if sum_toks else None], pub=True, tiny=True)
        sum_toks.append(dma("sync", "sumw", sumd.ap()[0:1, 512 * n:512 * n + 512],
                            sumstg[:, :], deps=[t_se]))
    t_e = S(lambda e: e.activation(scanB[0:16, :], scanA[0:16, :], Act.Exp),
            deps=xp_toks + sum_toks, pub=True)
    t_a = S(lambda e: e.activation(scanB[0:16, :], scanB[0:16, :], Act.Exp,
                                   scale=-1.0), deps=[t_e], pub=True)
    t_sl = S(lambda e: e.activation(scanC[:, :], scanC[:, :], Act.Silu),
             deps=xp_toks, pub=True)
    t_sb = dma("gpsimd", "sumb", scanD[:, :], sumd.ap().partition_broadcast(NS),
               deps=sum_toks)
    t_b = V(lambda e: e.tensor_tensor(scanC[:, :], scanC[:, :], scanD[:, :],
                                      Alu.mult), deps=[t_sl, t_sb], pub=True)
    t_y0 = V(lambda e: e.tensor_tensor_scan(scanD[:, :], scanB[0:16, :],
                                            scanC[:, :], 0.0, Alu.mult, Alu.add),
             deps=[t_a, t_b], pub=True)
    t_P = V(lambda e: e.tensor_tensor_scan(scanA[0:16, :], scanB[0:16, :],
                                           scanB[0:16, :], 1.0, Alu.mult,
                                           Alu.bypass), deps=[t_a], pub=True)
    t_c1 = dma("gpsimd", "cry", ag1_in.ap()[0:1, 0:16].rearrange("one n -> n one"),
               scanA[0:16, T - 1:T], deps=[t_P])
    t_c2 = dma("gpsimd", "cry", ag1_in.ap()[0:1, 16:32].rearrange("one n -> n one"),
               scanD[:, T - 1:T], deps=[t_y0])
    t_ag1 = emit("gpsimd", lambda e: e.collective_compute(
        "AllGather", Alu.bypass, replica_groups=[list(range(NCORES))],
        ins=[ag1_in.ap().opt()], outs=[ag1_out.ap().opt()]),
        deps=[t_c1, t_c2], cc="cc1")
    t_g1 = t_g2 = None
    for r in range(NCORES):
        t_g1 = dma("gpsimd", f"cryo{r % 2}", cgat[:, r:r + 1],
                   ag1_out.ap()[r:r + 1, 0:16].rearrange("one n -> n one"),
                   deps=[t_ag1])
        t_g2 = dma("gpsimd", f"cryp{r % 2}", cgee[:, r:r + 1],
                   ag1_out.ap()[r:r + 1, 16:32].rearrange("one n -> n one"),
                   deps=[t_ag1])
    t_cs = V(lambda e: e.tensor_tensor_scan(csc[:, :], cgat[:, :], cgee[:, :],
                                            0.0, Alu.mult, Alu.add),
             deps=[t_g1, t_g2, last_dma["cryo0"], last_dma["cryp0"]],
             pub=True, tiny=True)
    t_sel = V(lambda e: e.tensor_tensor(csc[:, :], csc[:, :], onehot_s[:, :],
                                        Alu.mult), deps=[t_cs, l_oh], pub=True, tiny=True)
    t_sin = V(lambda e: e.tensor_reduce(sin_s[:, 0:1], csc[:, :], AX, Alu.add),
              deps=[t_sel], pub=True, tiny=True)
    t_y = V(lambda e: e.tensor_tensor_scan(y_bf[0:16, :], scanB[0:16, :],
                                           scanC[:, :], sin_s[0:16, 0:1],
                                           Alu.mult, Alu.add),
            deps=[t_sin], pub=True)

    dbg_cap("a_", scanB[0:16, 0:512], [t_a], 512, parts=16)
    dbg_cap("b_", scanC[0:16, 0:512], [t_b, t_y0], 512, parts=16)
    dbg_cap("sin", sin_s[:, 0:1], [t_sin], 1, parts=16)
    dbg_cap("y", y_bf[0:16, 0:512], [t_y], 512, parts=16)
    t_gaob2 = V(lambda e: e.tensor_tensor(gaob[:, :], adp[:, 16:24],
                                          outb_s[:, :], Alu.mult),
                deps=[t_gaq, l_outb], pub=True, tiny=True)
    res_toks = []
    ctile_free = [None] * 4
    pend = [None]

    def flush_pend(pn, pmo, pev, psl):
        for q in range(4):
            si = tr_ctr[0] % 8
            tr_ctr[0] += 1
            t_tr = TT(lambda e, psl=psl, q=q, si=si: e.transpose(
                tr_slot(si), ctile[psl][:, 128 * q:128 * q + 128], id_b[:, :]),
                deps=[pev, tr_guard[si % 2]], pub=True)
            ra = V(lambda e, pn=pn, q=q, pmo=pmo, si=si: e.tensor_tensor(
                x_tm[4 * pn + q][:, 128 * pmo:128 * pmo + 128],
                x_tm[4 * pn + q][:, 128 * pmo:128 * pmo + 128],
                tr_slot(si), Alu.add),
                deps=[t_tr, x_toks[4 * pn + q]], pub=True, tiny=True)
            tr_guard[si % 2] = ra
            res_toks.append(ra)
            ctile_free[psl] = t_tr

    for n in range(4):
        for mo in range(8):
            bk = (n * 8 + mo) % 4
            mmt = TT(lambda e, bk=bk, mo=mo, n=n: e.matmul(
                ps_mm[bk][:, 0:512], outw_bf[0:16, 128 * mo:128 * mo + 128],
                y_bf[0:16, 512 * n:512 * n + 512], start=True, stop=True),
                deps=[l_outw, t_y, bank_free[bk]], pub=True)
            sl = (n * 8 + mo) % 4
            ev = S(lambda e, bk=bk, mo=mo, sl=sl: e.activation(
                ctile[sl][:, :], ps_mm[bk][:, 0:512], Act.Identity,
                scale=adp[:, 16 + mo:17 + mo], bias=gaob[:, mo:mo + 1]),
                deps=[mmt, t_gaob2, ctile_free[sl], t_c2, t_g1, t_g2],
                pub=True, tiny=True)
            bank_free[bk] = ev
            if pend[0] is not None:
                flush_pend(*pend[0])
            pend[0] = (n, mo, ev, sl)
    flush_pend(*pend[0])
    pend[0] = None

    dbg_cap("x2_0", x_tm[0][:, 0:512], res_toks, 512)
    h2_toks = []
    xsp_toks = []
    sum2_deps = []
    for i in range(16):
        sum2_deps += ln_sums(100 + i, i, x_tm[i], None, extra_deps=res_toks)
    t_ln2 = ln_finish(16, sum2_deps)
    for i in range(16):
        evs = ln_apply(100 + i, i, x_tm[i], t_ln2, 32, 24,
                       lambda j, i=i: hx[j][:, 128 * i:128 * i + 128],
                       extra_deps=res_toks)
        h2_toks.extend(evs)
        xsp_toks.append(dma("sync", f"xsp{i % 4}", xspill[i], x_tm[i][:, :],
                            deps=evs))
    mx_h2 = []
    for j in range(8):
        mx_h2.append(V(lambda e, j=j: e.tensor_reduce(
            mxc[:, j:j + 1], hx[j][:, :], AX, Alu.max,
            apply_absolute_value=True), deps=h2_toks, pub=True, tiny=True))

    def global_max(src_col_range, dst_slot, deps):
        a, b = src_col_range
        t1 = V(lambda e: e.tensor_reduce(mxc[:, 96:97], mxc[:, a:b], AX,
                                         Alu.max), deps=deps, pub=True, tiny=True)
        return emit("gpsimd", lambda e, d=dst_slot: e.tensor_reduce(
            scl[0:1, d:d + 1], mxc[:, 96:97], mybir.AxisListType.C, Alu.max),
            deps=[t1], pub=True, tiny=True)

    def allreduce_pair(src_slot, deps):
        t1 = dma("sync", "arb", ar_in.ap(), scl[0:1, src_slot:src_slot + 2],
                 deps=deps)
        arid = ar_ctr[0]
        ar_ctr[0] += 1
        t2 = emit("gpsimd", lambda e: e.collective_compute(
            "AllReduce", Alu.max, replica_groups=[list(range(NCORES))],
            ins=[ar_in.ap().opt()], outs=[ar_out.ap().opt()]),
            deps=[t1], cc=f"ccr{arid}")
        return dma("sync", "arb", scl[0:1, src_slot:src_slot + 2], ar_out.ap(),
                   deps=[t2])

    ar_ctr = [0]
    bc_guard = [None]

    def bcast_scl(deps):
        tm = TT(lambda e: e.matmul(ps_b[:, 0:32], ones1[:, :], scl[0:1, :],
                                   start=True, stop=True),
                deps=list(deps) + [l_ones1, bc_guard[0], sum_toks[-1] if sum_toks else None],
                pub=True)
        tv = V(lambda e: e.tensor_copy(sclb[:, :], ps_b[:, 0:32]),
               deps=[tm], pub=True, tiny=True)
        bc_guard[0] = tv
        return tv

    dbg_cap("h2_0", hx[0][:, 0:512], h2_toks, 512)
    t_mx1 = global_max((0, 8), 0, mx_h2 + [t_scl0])
    t_ar1 = allreduce_pair(0, [t_mx1])
    t_s1 = V(lambda e: e.tensor_scalar(scl[0:1, 1:2], scl[0:1, 0:1], INV_QMAX,
                                       1e-12, Alu.mult, Alu.max),
             deps=[t_ar1], pub=True, tiny=True)
    t_i1 = V(lambda e: e.reciprocal(scl[0:1, 2:3], scl[0:1, 1:2]),
             deps=[t_s1], pub=True, tiny=True)
    t_bc1 = bcast_scl([t_i1])

    dbg_cap("scl1", scl[:, :], [t_i1, t_bc1], 32, parts=1)
    dbg_cap("sclb1", sclb[:, :], [t_bc1], 32)
    xq_toks = [None] * 32
    for j in range(8):
        for qq in range(4):
            rs = rtring[qq]
            c0 = 512 * qq
            t_r = V(lambda e, j=j, c0=c0, rs=rs: e.tensor_scalar(
                rs[:, :], hx[j][:, c0:c0 + 512], sclb[:, 2:3], MAGIC,
                Alu.mult, Alu.add),
                deps=[t_bc1, xq_toks[4 * j + qq - 4] if 4 * j + qq >= 4 else None]
                + h2_toks, pub=True)
            xq_toks[4 * j + qq] = V(lambda e, j=j, c0=c0, rs=rs: e.tensor_scalar(
                hx[j][:, c0:c0 + 512], rs[:, :], MAGIC, None, Alu.subtract),
                deps=[t_r], pub=True)

    scan_dead = [t_y, t_c1, t_c2, hprime[-1], xp_toks[-1], sum_toks[-1]]
    exp_dead = xp_toks[-1]

    def ffn_col_load(j, pb, deps):
        r, eo = col_addr(j)
        I = D if j < 48 else FF
        return dma("sync", f"wff{pb}", wffcol[pb][:, 0:I],
                   agw_out.ap()[r:r + 1, eo:eo + 128 * I].rearrange(
                       "one (p q) -> (one p) q", p=128),
                   deps=deps + [t_agw] + scan_dead)

    wff_tok = [None, None]
    wff_free = [None, None]
    dbg_cap("xq_0", hx[0][:, 0:512], xq_toks, 512)
    gate_ev = []
    bank_free = [None] * 5
    for m in range(24):
        pb = m % 2
        wff_tok[pb] = ffn_col_load(m, pb, [wff_free[pb]])
        for n in range(4):
            bk = (m * 4 + n) % 5
            mmt = None
            for k in range(8):
                mmt = TT(lambda e, pb=pb, bk=bk, k=k, n=n: e.matmul(
                    psk(bk)[:, :], wffcol[pb][:, 128 * k:128 * k + 128],
                    hx[k][:, 512 * n:512 * n + 512],
                    start=(k == 0), stop=(k == 7)),
                    deps=[wff_tok[pb], bank_free[bk], xq_toks[4 * k + n],
                          exp_dead], pub=(k == 7))
            ev = V(lambda e, m=m, n=n, bk=bk: e.tensor_scalar(
                gP(m)[:, 512 * n:512 * n + 512], psk(bk)[:, :], sclb[:, 1:2],
                None, Alu.mult),
                deps=[mmt, t_bc1] + ([xsp_toks[m]] if m < 16 else []), pub=True,
                tiny=True)
            bank_free[bk] = ev
            gate_ev.append(ev)
            gate_ev.append(V(lambda e, m=m, n=n: e.tensor_reduce(
                mxc[:, 4 * m + n:4 * m + n + 1], gP(m)[:, 512 * n:512 * n + 512],
                AX, Alu.max, apply_absolute_value=True), deps=[ev], pub=True, tiny=True))
        wff_free[pb] = gate_ev[-2]
    dbg_cap("gate_0", gP(0)[:, 0:512], gate_ev, 512)
    t_mxg = global_max((0, 96), 3, gate_ev)
    t_ar2 = allreduce_pair(3, [t_mxg])
    t_sg = V(lambda e: e.tensor_scalar(scl[0:1, 4:5], scl[0:1, 3:4], INV_QMAX,
                                       1e-12, Alu.mult, Alu.max),
             deps=[t_ar2], pub=True, tiny=True)
    t_ig = V(lambda e: e.reciprocal(scl[0:1, 5:6], scl[0:1, 4:5]),
             deps=[t_sg], pub=True, tiny=True)
    t_bc2 = bcast_scl([t_ig])

    up_ev = []
    up_spill_toks = {}
    upr_free = [None] * 8
    up_last_mxe = [None] * 8
    w1_toks = []
    w1_rt_guard = [None] * 4

    def window1_chunk(m):
        for hh in range(4):
            c0 = 512 * hh
            rs = rtring[hh]
            xh = xhat[hh % 2][:, 512 * (hh // 2):512 * (hh // 2) + 512]
            t1 = S(lambda e, m=m, c0=c0, rs=rs: e.activation(
                rs[:, :], gP(m)[:, c0:c0 + 512], Act.Copy, bias=MAGIC,
                scale=sclb[:, 5:6]),
                deps=[t_bc2, w1_rt_guard[hh]] + gate_ev, pub=True)
            t2 = V(lambda e, m=m, c0=c0, rs=rs: e.tensor_scalar(
                gP(m)[:, c0:c0 + 512], rs[:, :], MAGIC, None, Alu.subtract),
                deps=[t1], pub=True)
            t3 = S(lambda e, m=m, c0=c0, xh=xh: e.activation(
                xh[:, :], gP(m)[:, c0:c0 + 512], Act.Sigmoid,
                scale=sclb[:, 4:5]), deps=[t2], pub=True)
            t4 = S(lambda e, xh=xh, rs=rs: e.activation(
                rs[:, :], xh[:, :], Act.Copy, bias=MAGIC, scale=65536.0),
                deps=[t3], pub=True)
            t5 = V(lambda e, rs=rs: e.tensor_scalar(
                rs[:, :], rs[:, :], MAGIC, 65535.0, Alu.subtract, Alu.min),
                deps=[t4], pub=True)
            t6 = V(lambda e, m=m, c0=c0, rs=rs: e.tensor_tensor(
                gP(m)[:, c0:c0 + 512], rs[:, :], gP(m)[:, c0:c0 + 512],
                Alu.mult), deps=[t5], pub=True)
            w1_rt_guard[hh] = t6
            w1_toks.append(V(lambda e, m=m, c0=c0, hh=hh: e.tensor_reduce(
                mxc2[:, (4 * m + hh):(4 * m + hh) + 1],
                gP(m)[:, c0:c0 + 512], AX, Alu.max,
                apply_absolute_value=True), deps=[t6], pub=True, tiny=True))

    for m in range(24):
        pb = m % 2
        wff_tok[pb] = ffn_col_load(24 + m, pb, [wff_free[pb]])
        for n in range(4):
            bk = (m * 4 + n) % 5
            sl = (m * 4 + n) % 8
            mmt = None
            for k in range(8):
                mmt = TT(lambda e, pb=pb, bk=bk, k=k, n=n: e.matmul(
                    psk(bk)[:, :], wffcol[pb][:, 128 * k:128 * k + 128],
                    hx[k][:, 512 * n:512 * n + 512],
                    start=(k == 0), stop=(k == 7)),
                    deps=[wff_tok[pb], bank_free[bk]], pub=(k == 7))
            ev = S(lambda e, sl=sl, bk=bk: e.activation(
                uprng[sl][:, :], psk(bk)[:, :], Act.Copy, scale=sclb[:, 1:2]),
                deps=[mmt, upr_free[sl], t_bc2], pub=True, tiny=True)
            bank_free[bk] = ev
            mxe = V(lambda e, m=m, n=n, sl=sl: e.tensor_reduce(
                mxc[:, 4 * m + n:4 * m + n + 1], uprng[sl][:, :], AX, Alu.max,
                apply_absolute_value=True), deps=[ev], pub=True, tiny=True)
            up_ev.append(mxe)
            up_last_mxe[sl] = mxe
            sp = dma("sync", f"upr{sl}", upspill.ap()[m, n], uprng[sl][:, :],
                     deps=[ev])
            up_spill_toks[(m, n)] = sp
            upr_free[sl] = sp
        window1_chunk(m)

    dbg_cap("scl2", scl[:, :], [t_bc2], 32, parts=1)
    dbg_cap("P_0", gP(0)[:, 0:512], w1_toks, 512)
    t_mxu = global_max((0, 96), 6, up_ev)
    t_mxP1 = V(lambda e: e.tensor_reduce(mxc2[:, 95:96]
                                         if False else mxc[:, 97:98],
                                         mxc2[:, 0:96], AX, Alu.max),
               deps=w1_toks, pub=True, tiny=True)
    t_mxP = emit("gpsimd", lambda e: e.tensor_reduce(
        scl[0:1, 7:8], mxc[:, 97:98], mybir.AxisListType.C, Alu.max),
        deps=[t_mxP1, t_mxu], pub=True, tiny=True)
    t_ar3 = allreduce_pair(6, [t_mxu, t_mxP])
    t_sup = V(lambda e: e.tensor_scalar(scl[0:1, 8:9], scl[0:1, 6:7], INV_QMAX,
                                        1e-12, Alu.mult, Alu.max),
              deps=[t_ar3], pub=True, tiny=True)
    t_iup = V(lambda e: e.reciprocal(scl[0:1, 9:10], scl[0:1, 8:9]),
              deps=[t_sup], pub=True, tiny=True)
    t_cg = V(lambda e: e.tensor_scalar_mul(scl[0:1, 12:13], scl[0:1, 4:5],
                                           1.0 / 65536.0), deps=[t_sg], pub=True, tiny=True)
    t_mg2 = V(lambda e: e.tensor_tensor(scl[0:1, 20:21], scl[0:1, 7:8],
                                        scl[0:1, 12:13], Alu.mult),
              deps=[t_ar3, t_cg], pub=True, tiny=True)
    t_s4 = V(lambda e: e.tensor_scalar(scl[0:1, 10:11], scl[0:1, 20:21],
                                       INV_QMAX, 1e-12, Alu.mult, Alu.max),
             deps=[t_mg2], pub=True, tiny=True)
    t_i4 = V(lambda e: e.reciprocal(scl[0:1, 11:12], scl[0:1, 10:11]),
             deps=[t_s4], pub=True, tiny=True)
    t_c4 = V(lambda e: e.tensor_tensor(scl[0:1, 13:14], scl[0:1, 12:13],
                                       scl[0:1, 11:12], Alu.mult),
             deps=[t_i4], pub=True, tiny=True)
    t_s4u = V(lambda e: e.tensor_tensor(scl[0:1, 14:15], scl[0:1, 10:11],
                                        scl[0:1, 8:9], Alu.mult),
              deps=[t_s4, t_sup], pub=True, tiny=True)
    t_bc3 = bcast_scl([t_iup, t_c4, t_s4u])

    dbg_cap("scl3", scl[:, :], [t_bc3, t_s4u, t_c4], 32, parts=1)
    o_toks = []
    upr_free2 = list(up_last_mxe)
    w2_gu = [None] * 2
    w2_gg = [None] * 2
    for m in range(24):
        for n in range(4):
            i = m * 4 + n
            sl = i % 8
            ru = rtring[i % 2]
            rg = rtring[2 + i % 2]
            rd = dma("sync", f"upr{sl}", uprng[sl][:, :], upspill.ap()[m, n],
                     deps=[up_spill_toks[(m, n)], upr_free2[sl]])
            t1 = S(lambda e, sl=sl, ru=ru: e.activation(
                ru[:, :], uprng[sl][:, :], Act.Copy, bias=MAGIC,
                scale=sclb[:, 9:10]),
                deps=[rd, t_bc3, w2_gu[i % 2]], pub=True)
            t2 = V(lambda e, sl=sl, ru=ru: e.tensor_scalar(
                uprng[sl][:, :], ru[:, :], MAGIC, None, Alu.subtract),
                deps=[t1], pub=True)
            w2_gu[i % 2] = t2
            t3 = S(lambda e, m=m, n=n, rg=rg: e.activation(
                rg[:, :], gP(m)[:, 512 * n:512 * n + 512], Act.Copy,
                bias=MAGIC, scale=sclb[:, 13:14]),
                deps=[t_bc3, w2_gg[i % 2]] + w1_toks, pub=True)
            t4 = V(lambda e, m=m, n=n, sl=sl, rg=rg: e.scalar_tensor_tensor(
                gP(m)[:, 512 * n:512 * n + 512], rg[:, :], MAGIC,
                uprng[sl][:, :], Alu.subtract, Alu.mult),
                deps=[t2, t3], pub=True)
            w2_gg[i % 2] = t4
            upr_free2[sl] = t4
            o_toks.append(V(lambda e, m=m, n=n: e.tensor_reduce(
                mxc[:, 4 * m + n:4 * m + n + 1], gP(m)[:, 512 * n:512 * n + 512],
                AX, Alu.max, apply_absolute_value=True), deps=[t4], pub=True, tiny=True))
    dbg_cap("o_0", gP(0)[:, 0:512], o_toks, 512)
    t_mxo = global_max((0, 96), 15, o_toks)
    t_ar4 = allreduce_pair(15, [t_mxo])
    t_mo = V(lambda e: e.tensor_tensor(scl[0:1, 16:17], scl[0:1, 15:16],
                                       scl[0:1, 14:15], Alu.mult),
             deps=[t_ar4, t_s4u], pub=True, tiny=True)
    t_s5 = V(lambda e: e.tensor_scalar(scl[0:1, 17:18], scl[0:1, 16:17],
                                       INV_QMAX, 1e-12, Alu.mult, Alu.max),
             deps=[t_mo], pub=True, tiny=True)
    t_i5 = V(lambda e: e.reciprocal(scl[0:1, 18:19], scl[0:1, 17:18]),
             deps=[t_s5], pub=True, tiny=True)
    t_t5 = V(lambda e: e.tensor_tensor(scl[0:1, 19:20], scl[0:1, 14:15],
                                       scl[0:1, 18:19], Alu.mult),
             deps=[t_i5], pub=True, tiny=True)
    t_bc4 = bcast_scl([t_t5])
    t_gb5 = V(lambda e: e.tensor_scalar(adp[:, 48:56], adp[:, 40:48],
                                        sclb[:, 17:18], None, Alu.mult),
              deps=[t_bc4, t_gbq], pub=True, tiny=True)

    dbg_cap("scl4", scl[:, :], [t_bc4, t_gb5], 32, parts=1)
    dbg_cap("gb5", adp[:, 48:56], [t_gb5], 8)
    oq_toks = {}
    w3_rt_guard = [None] * 4
    for m in range(24):
        for n in range(4):
            hh = (m * 4 + n) % 4
            rs = rtring[hh]
            t1 = S(lambda e, m=m, n=n, rs=rs: e.activation(
                rs[:, 0:512], gP(m)[:, 512 * n:512 * n + 512], Act.Copy,
                bias=MAGIC, scale=sclb[:, 19:20]),
                deps=[t_bc4, w3_rt_guard[hh]] + o_toks, pub=True)
            oq_toks[(m, n)] = V(lambda e, m=m, n=n, rs=rs: e.tensor_scalar(
                gP(m)[:, 512 * n:512 * n + 512], rs[:, 0:512], MAGIC, None,
                Alu.subtract), deps=[t1], pub=True)
            w3_rt_guard[hh] = oq_toks[(m, n)]

    dbg_cap("oq_0", gP(0)[:, 0:512], [oq_toks[(0, 0)], oq_toks[(0, 1)]], 512)
    st_toks = []
    xr_toks = [None] * 8
    up_mm_done = up_ev[-1]
    dpend = [None]

    def flush_dpend(pnn, pmo, pev, psl):
        for q in range(4):
            si = tr_ctr[0] % 8
            tr_ctr[0] += 1
            t_tr = TT(lambda e, psl=psl, q=q, si=si: e.transpose(
                tr_slot(si), ctile[psl][:, 128 * q:128 * q + 128], id_b[:, :]),
                deps=[pev, tr_guard[si % 2]], pub=True)
            s8 = 4 * pnn + q
            ra = V(lambda e, s8=s8, pmo=pmo, si=si: e.tensor_tensor(
                xrr[s8][:, 128 * pmo:128 * pmo + 128],
                xrr[s8][:, 128 * pmo:128 * pmo + 128], tr_slot(si), Alu.add),
                deps=[t_tr, xr_toks[s8]], pub=True, tiny=True)
            tr_guard[si % 2] = ra
            xr_toks[s8] = ra
            ctile_free[psl] = t_tr

    for half in range(2):
        for q8 in range(8):
            n = 2 * half + q8 // 4
            q = q8 % 4
            xr_toks[q8] = dma("sync", f"xrr{q8}", xrr[q8][:, :],
                              xspill.ap()[4 * n + q],
                              deps=xsp_toks + (st_toks[-8:] if half else [])
                              + [xq_toks[-1], up_mm_done])
        for mo in range(8):
            pb = mo % 2
            wff_tok[pb] = ffn_col_load(48 + mo, pb, [wff_free[pb]])
            for nn in range(2):
                n = 2 * half + nn
                bk = (mo * 2 + nn) % 5
                mmt = None
                for k in range(24):
                    mmt = TT(lambda e, pb=pb, bk=bk, k=k, n=n: e.matmul(
                        psk(bk)[:, :], wffcol[pb][:, 128 * k:128 * k + 128],
                        gP(k)[:, 512 * n:512 * n + 512],
                        start=(k == 0), stop=(k == 23)),
                        deps=[wff_tok[pb], bank_free[bk], oq_toks[(k, n)]],
                        pub=(k == 23))
                sl = (mo * 2 + nn) % 4
                ev = S(lambda e, bk=bk, mo=mo, sl=sl: e.activation(
                    ctile[sl][:, :], psk(bk)[:, :], Act.Copy,
                    scale=adp[:, 48 + mo:49 + mo]),
                    deps=[mmt, t_gb5, ctile_free[sl]], pub=True, tiny=True)
                bank_free[bk] = ev
                wff_free[pb] = mmt
                if dpend[0] is not None:
                    flush_dpend(*dpend[0])
                dpend[0] = (nn, mo, ev, sl)
        if dpend[0] is not None:
            flush_dpend(*dpend[0])
            dpend[0] = None
        for q8 in range(8):
            n = 2 * half + q8 // 4
            q = q8 % 4
            st_toks.append(dma("sync", f"ost{q8 % 4}",
                               out_h[512 * n + 128 * q:512 * n + 128 * q + 128, :],
                               xrr[q8][:, :], deps=[xr_toks[q8]]))

    wait_list = st_toks[-4:]
    if dbg:
        wait_list = wait_list + [last_dma.get("dbgk")]
    emit("sync", lambda e: None, deps=wait_list)

    build_program.n_ops = len(ops)
    build_program.ops_dbg = [(o["eng"], o["fn"].__code__.co_firstlineno,
                              o["waits"], o["incs"]) for o in ops]
    if trunc is not None:
        ops[:] = ops[:trunc]
    with nc.Block() as block:
        def mk(eng_name):
            def run(e):
                for op in ops:
                    if op["eng"] != eng_name:
                        continue
                    for s, v in op["waits"]:
                        e.wait_ge(s, v)
                    ins = op["fn"](e)
                    for s, nn in op["incs"]:
                        assert ins is not None, "inc on op returning no instruction"
                        ins = ins.then_inc(s, nn)
            return run

        block.sync(mk("sync"))
        block.gpsimd(mk("gpsimd"))
        block.vector(mk("vector"))
        block.scalar(mk("scalar"))
        block.tensor(mk("tensor"))
    return nc


_NC_CACHE = None


def _get_nc():
    global _NC_CACHE
    if _NC_CACHE is None:
        _NC_CACHE = build_program(debug_sync=DEBUG_SYNC)
    return _NC_CACHE


def make_inmaps(L0_attn_x, L0_attn_in_w, L0_attn_in_b, L0_attn_conv_w,
                L0_attn_conv_b, L0_attn_xp_w, L0_attn_xp_b, L0_attn_out_w,
                L0_attn_out_b, L0_ffn_w_gate, L0_ffn_w_up, L0_ffn_w_down,
                L0_c, L0_wmod):
    f = np.float32
    x = np.ascontiguousarray(L0_attn_x, dtype=f).reshape(16384, D)
    in_w = np.ascontiguousarray(L0_attn_in_w, dtype=f)
    winT = np.ascontiguousarray(
        in_w.reshape(16, 128, 8, 128).transpose(0, 3, 2, 1))
    inb = np.ascontiguousarray(
        np.asarray(L0_attn_in_b, dtype=f).reshape(16, 128).T)
    convw = np.ascontiguousarray(
        np.asarray(L0_attn_conv_w, dtype=f).reshape(16, 128, 4)
        .transpose(1, 0, 2).reshape(128, 64))
    convb = np.ascontiguousarray(
        np.asarray(L0_attn_conv_b, dtype=f).reshape(16, 128).T)
    xpwT = np.ascontiguousarray(
        np.ascontiguousarray(L0_attn_xp_w, dtype=f).T.reshape(16, 128, 32))
    xpba = np.ascontiguousarray(np.asarray(L0_attn_xp_b, dtype=f)[:16].reshape(16, 1))
    xpbb = np.ascontiguousarray(np.asarray(L0_attn_xp_b, dtype=f)[16:].reshape(16, 1))
    outwT = np.ascontiguousarray(
        np.asarray(L0_attn_out_w, dtype=f).T.reshape(16, 8, 128).transpose(1, 0, 2))
    outb = np.ascontiguousarray(
        np.asarray(L0_attn_out_b, dtype=f).reshape(8, 128).T)
    wg = np.ascontiguousarray(L0_ffn_w_gate, dtype=f)
    wu = np.ascontiguousarray(L0_ffn_w_up, dtype=f)
    wd = np.ascontiguousarray(L0_ffn_w_down, dtype=f)
    cvec = np.ascontiguousarray(np.asarray(L0_c, dtype=f).reshape(8, 128).T)
    wmod = np.ascontiguousarray(L0_wmod, dtype=f)

    identf = np.eye(128, dtype=f)
    identb = np.eye(128, dtype=ml_dtypes.bfloat16)
    egu = np.zeros((32, 8 * 128), dtype=f)
    for kc in range(8):
        for kp in range(128):
            egu[4 * kc + kp // 32, 128 * kc + kp] = 1.0
    ed = np.zeros((96, 24 * 128), dtype=f)
    for kc in range(24):
        for kp in range(128):
            ed[4 * kc + kp // 32, 128 * kc + kp] = 1.0

    in_maps = []
    for r in range(NCORES):
        xs = np.empty((TH, D), dtype=f)
        t0 = r * T
        if r == 0:
            xs[:HALO] = 0.0
        else:
            xs[:HALO] = x[t0 - HALO:t0]
        xs[HALO:] = x[t0:t0 + T]
        rows_gu = np.empty((6, 128, D), dtype=f)
        lhsT_gu = np.empty((6, 128, 8, 128), dtype=f)
        for s in range(6):
            j = r + 8 * s
            if j < 24:
                W, m = wg, j
            else:
                W, m = wu, j - 24
            blk = W[128 * m:128 * m + 128, :]
            rows_gu[s] = blk
            lhsT_gu[s] = blk.reshape(128, 8, 128).transpose(2, 1, 0)
        md = (r + 48) - 48
        blkd = wd[128 * md:128 * md + 128, :]
        rows_d = np.ascontiguousarray(blkd)
        lhsT_d = np.ascontiguousarray(blkd.reshape(128, 24, 128).transpose(2, 1, 0))
        wms = wmod[:, 768 * r:768 * r + 768]
        wmod_t = np.ascontiguousarray(
            wms.reshape(8, 128, 6, 128).transpose(2, 0, 1, 3))
        onehot = np.zeros((NS, 8), dtype=f)
        if r > 0:
            onehot[:, r - 1] = 1.0
        hmask = np.full((128, 1), 0.0 if r == 0 else 1.0, dtype=f)
        in_maps.append({
            "x": np.ascontiguousarray(xs),
            "winT": winT, "inb": inb, "convw": convw, "convb": convb,
            "xpwT": xpwT, "xpba": xpba, "xpbb": xpbb, "outwT": outwT, "outb": outb,
            "lpq_rows_gu": np.ascontiguousarray(rows_gu),
            "lpq_lhsT_gu": np.ascontiguousarray(lhsT_gu),
            "lpq_rows_d": rows_d, "lpq_lhsT_d": lhsT_d,
            "cvec": cvec, "wmod": wmod_t,
            "identf": identf, "identb": identb, "egu": egu, "ed": ed,
            "onehot": onehot, "hmask": hmask,
            "ones1": np.ones((1, 128), dtype=f),
        })

    return in_maps


def kernel(**inputs):
    in_maps = make_inmaps(**inputs)
    nc = _get_nc()
    res = run_bass_kernel_spmd(nc, in_maps, core_ids=list(range(NCORES)))
    out = np.concatenate([np.asarray(r["out"], dtype=np.float32)
                          for r in res.results], axis=0)
    return out.reshape(1, 16384, D)


# version 11
# speedup vs baseline: 1.1956x; 1.0200x over previous
import numpy as np
import ml_dtypes
import concourse.bass as bass
import concourse.mybir as mybir
from concourse.bass_utils import run_bass_kernel_spmd

F32 = mybir.dt.float32
BF16 = mybir.dt.bfloat16
Alu = mybir.AluOpType
Act = mybir.ActivationFunctionType
AX = mybir.AxisListType.X
MAGIC = float(np.float32(1.5 * 2.0 ** 23))
INV_QMAX = float(np.float32(1.0 / 32767.0))

NCORES = 8
T = 2048
HALO = 128
TH = T + HALO
D = 1024
DI = 2048
FF = 3072
NS = 16
EPS = 1e-5

SLOT_OFF = [0, 262144, 524288, 786432, 1048576, 1310720, 1572864]
RANK_BYTES = 2359296


def col_addr(j):
    return j % NCORES, SLOT_OFF[j // NCORES] // 2


DEBUG_SYNC = False


def build_program(debug_sync=False, trunc=None, dbg=False):
    nc = bass.Bass("TRN2", target_bir_lowering=False,
                   detect_race_conditions=debug_sync)

    def din(name, shape, dt=F32):
        return nc.dram_tensor(name, shape, dt, kind="ExternalInput").ap()

    x_h = din("x", [TH, D])
    winT_h = din("winT", [16, 128, 8, 128])
    inb_h = din("inb", [128, 16])
    convw_h = din("convw", [128, 64])
    convb_h = din("convb", [128, 16])
    xpwT_h = din("xpwT", [16, 128, 32])
    xpba_h = din("xpba", [16, 1])
    xpbb_h = din("xpbb", [16, 1])
    outwT_h = din("outwT", [8, 16, 128])
    outb_h = din("outb", [128, 8])
    lpq_rows_gu_h = din("lpq_rows_gu", [6, 128, D])
    lpq_lhsT_gu_h = din("lpq_lhsT_gu", [6, 128, 8, 128])
    lpq_rows_d_h = din("lpq_rows_d", [128, FF])
    lpq_lhsT_d_h = din("lpq_lhsT_d", [128, 24, 128])
    c_h = din("cvec", [128, 8])
    wmod_h = din("wmod", [6, 8, 128, 128])
    ident_f_h = din("identf", [128, 128])
    ident_b_h = din("identb", [128, 128], BF16)
    egu_h = din("egu", [32, 8 * 128])
    ed_h = din("ed", [96, 24 * 128])
    onehot_h = din("onehot", [NS, 8])
    hmask_h = din("hmask", [128, 1])
    ones1_h = din("ones1", [1, 128])

    out_h = nc.dram_tensor("out", [T, D], F32, kind="ExternalOutput").ap()
    dbg_h = nc.dram_tensor("dbg", [128, 16384], F32,
                           kind="ExternalOutput").ap() if dbg else None

    ag0_in = nc.dram_tensor("ag0_in", [1, 768], F32)
    ag0_out = nc.dram_tensor("ag0_out", [NCORES, 768], F32)
    agw_in = nc.dram_tensor("agw_in", [1, RANK_BYTES // 2], BF16)
    agw_out = nc.dram_tensor("agw_out", [NCORES, RANK_BYTES // 2], BF16)
    ag1_in = nc.dram_tensor("ag1_in", [1, 32], F32)
    ag1_out = nc.dram_tensor("ag1_out", [NCORES, 32], F32)
    ar_in = nc.dram_tensor("ar_in", [1, 2], F32)
    ar_out = nc.dram_tensor("ar_out", [1, 2], F32)
    sumd = nc.dram_tensor("sumd", [1, T], F32)
    mxd = nc.dram_tensor("mxd", [128, 1], F32)
    scld = nc.dram_tensor("scld", [1, 32], F32)
    xspill = nc.dram_tensor("xspill", [16, 128, D], F32)
    upspill = nc.dram_tensor("upspill", [24, 4, 128, 512], BF16)

    SBUF_SZ = nc.SBUF_PARTITION_SIZE_BYTES
    AR_S = SBUF_SZ - 34816 - 32
    AR_H = AR_S - 69632
    AR_X = AR_H - 65536

    def at(name, shape, dt, off):
        return nc.alloc_sbuf_tensor_at(name, shape, dt, offset=off)

    x_tm = [at(f"x{i}", [128, D], F32, AR_X + 4096 * i) for i in range(16)]
    gP_lo = [at(f"gPl{m}", [128, T], BF16, AR_X + 4096 * m) for m in range(16)]
    wrow = at("wrow", [128, FF], F32, AR_H + 0)
    wlhs = at("wlhs", [128, FF], F32, AR_H + 12288)
    wqcol = at("wqcol", [128, FF], BF16, AR_H + 24576)
    wmodb = at("wmodb", [128, 6 * 1024], F32, AR_H + 30720)
    ed_s = at("ed_s", [96, 24 * 128], F32, AR_H + 55296)
    smax = at("smax", [128, 96], F32, AR_H + 67584)
    sval = at("sval", [128, 96], F32, AR_H + 67968)
    sinv = at("sinv", [128, 96], F32, AR_H + 68352)
    h_bf = [at(f"h{m}", [128, TH], BF16, AR_H + 4352 * m) for m in range(16)]
    gP_hi = [at(f"gPh{m}", [128, T], BF16, AR_H + 4096 * m) for m in range(8)]
    hx = [at(f"hx{j}", [128, T], BF16, AR_H + 32768 + 4096 * j) for j in range(8)]
    xrr = [at(f"xrr{q}", [128, D], F32, AR_H + 32768 + 4096 * q) for q in range(8)]
    h1_bf = [at(f"h1_{j}", [128, TH], BF16, AR_S + 4352 * j) for j in range(8)]
    scanA = at("scanA", [32, T], F32, AR_S + 0)
    scanB = at("scanB", [16, T], F32, AR_S + 8192)
    scanC = at("scanC", [16, T], F32, AR_S + 16384)
    scanD = at("scanD", [16, T], F32, AR_S + 24576)
    y_bf = at("ybf", [16, T], BF16, AR_S + 24576)
    wffcol = [at(f"wff{p}", [128, FF], BF16, AR_S + 6144 * p) for p in range(2)]
    uprng = [at(f"upr{p}", [128, 512], BF16, AR_S + 12288 + 1024 * p) for p in range(8)]
    rtring = [at(f"rt{p}", [128, 512], F32, AR_S + 20480 + 2048 * p) for p in range(4)]
    ctile = [at(f"ct{p}", [128, 512], BF16, AR_S + 28672 + 1024 * p) for p in range(4)]

    def gP(m):
        return gP_lo[m] if m < 16 else gP_hi[m - 16]

    def psk(bk):
        return (ps_mm + [ps_a])[bk]

    A = nc.alloc_sbuf_tensor
    win_bf = [A(f"winb{p}", [128, 1024], BF16) for p in range(2)]
    hc = A("hc", [128, T], F32)
    sq1 = A("sq1", [128, D], F32)
    xstage = A("xstage", [128, D], F32)
    xhat = [A(f"xhat{p}", [128, D], BF16) for p in range(2)]

    convw = A("convw_s", [128, 64], F32)
    convb = A("convb_s", [128, 16], F32)
    xpw_bf = A("xpwb", [128, 512], BF16)
    xpba_s = A("xpba_s", [16, 1], F32)
    xpbb_s = A("xpbb_s", [16, 1], F32)
    outw_bf = A("outwb", [16, 1024], BF16)
    outb_s = A("outb_s", [128, 8], F32)
    gaob = A("gaob", [128, 8], F32)
    ones1 = A("ones1_s", [1, 128], F32)
    adp = A("adp", [128, 64], F32)
    silc = A("silc", [128, 8], F32)
    inb_s = A("inb_s", [128, 16], F32)
    id_f = A("id_f", [128, 128], F32)
    id_b = A("id_b", [128, 128], BF16)
    egu_s = A("egu_s", [32, 8 * 128], F32)
    sT_s = A("sT_s", [96, 128], F32)
    invT_s = A("invT_s", [96, 128], F32)
    ones_c = A("ones_c", [128, 1], BF16)
    mxc = A("mxc", [128, 100], F32)
    mxc2 = A("mxc2", [128, 96], F32)
    sumstg = A("sumstg", [1, 512], F32)
    lnsum = A("lnsum", [128, 20], F32)
    lnsq = A("lnsq", [128, 20], F32)
    lnmean = A("lnmean", [128, 20], F32)
    lnms = A("lnms", [128, 20], F32)
    lnrstd = A("lnrstd", [128, 20], F32)
    lnm2, lnvar, lnrc = lnsum, lnsq, lnms
    gqs = A("gqs", [128, 8], F32)
    sq1b = A("sq1b", [128, D], F32)
    scl = A("scl", [1, 32], F32)
    sclb = A("sclb", [128, 32], F32)
    onehot_s = A("onehot_s", [NS, 8], F32)
    sin_s = A("sin_s", [NS, 1], F32)
    cgat = A("cgat", [NS, 8], F32)
    cgee = A("cgee", [NS, 8], F32)
    csc = A("csc", [NS, 8], F32)
    hmask_s = A("hmask_s", [128, 1], F32)
    modloc = A("modloc", [128, 6], F32)



    canary = A("canary", [1, 8], F32)
    can_addr = nc.lookup_mloc(canary).addr
    assert can_addr + 32 <= AR_X, (
        f"auto SBUF allocations overflow into manual arena: {can_addr} > {AR_X}")
    ps_mm = [nc.alloc_psum_tensor(f"psmm{i}", [128, 512], F32) for i in range(4)]
    ps_tr = nc.alloc_psum_tensor("pstr", [128, 1024], BF16)
    ps_tr2 = nc.alloc_psum_tensor("pstr2", [128, 1024], BF16)
    ps_a = nc.alloc_psum_tensor("psa", [128, 512], F32)
    ps_b = nc.alloc_psum_tensor("psb", [128, 512], F32)

    sem_h = {n: nc.alloc_semaphore(n) for n in ["sT", "sV", "sS", "sG"]}
    cnt = {n: 0 for n in sem_h}
    ENG_SEM = {"vector": "sV", "scalar": "sS", "tensor": "sT", "gpsimd": "sG"}
    ops = []
    watermark = {e: {} for e in ["sync", "gpsimd", "vector", "scalar", "tensor"]}

    def sem_for(key):
        if key not in sem_h:
            sem_h[key] = nc.alloc_semaphore(key)
            cnt[key] = 0
        return sem_h[key]

    class Tok:
        __slots__ = ("eng", "sem", "val", "tiny")

        def __init__(self, eng, sem, val, tiny=False):
            self.eng, self.sem, self.val, self.tiny = eng, sem, val, tiny

    def _flat(deps):
        for d in deps:
            if d is None:
                continue
            if isinstance(d, (list, tuple)):
                yield from _flat(d)
            else:
                yield d

    def emit(eng, fn, deps=(), pub=False, dkey=None, cc=None, tiny=False):
        need = {}
        for d in _flat(deps):
            if (d.sem in ("sT", "sV", "sS", "sG") and d.eng == eng
                    and not debug_sync and not d.tiny):
                continue
            if need.get(d.sem, -1) < d.val:
                need[d.sem] = d.val
        waits = []
        wm = watermark[eng]
        for s, v in need.items():
            if wm.get(s, -1) >= v:
                continue
            wm[s] = v
            waits.append((sem_h[s], v))
        incs = []
        tok = None
        if cc is not None:
            sem_for(cc)
            cnt[cc] += 1
            incs.append((sem_h[cc], 1))
            tok = Tok(eng, cc, cnt[cc])
        elif dkey is not None:
            sem_for(dkey)
            cnt[dkey] += 16
            incs.append((sem_h[dkey], 16))
            tok = Tok(eng, dkey, cnt[dkey])
        elif pub or (debug_sync and eng in ENG_SEM):
            s = ENG_SEM[eng]
            cnt[s] += 1
            incs.append((sem_h[s], 1))
            tok = Tok(eng, s, cnt[s], tiny)
        ops.append({"eng": eng, "fn": fn, "waits": waits, "incs": incs})
        return tok

    last_dma = {}
    dbg_off = [0]
    dbg_names = []

    def dbg_cap(name, ap, deps, cols, parts=128):
        if not dbg:
            return
        o = dbg_off[0]
        dbg_names.append((name, o, cols, parts))
        dbg_off[0] += cols
        dma("gpsimd", "dbgk", dbg_h[0:parts, o:o + cols], ap, deps=list(deps),
            slow=True)
    build_program.dbg_names = dbg_names

    def dma(q, key, out, in_, deps=(), slow=False, chain=True):
        deps = list(deps)
        if chain and key in last_dma:
            deps.append(last_dma[key])
        t = emit(q, lambda e, o=out, i=in_, s=slow: e.dma_start(
            out=o, in_=i, allow_slow_non_contiguous=s), deps=deps, dkey=key)
        last_dma[key] = t
        return t

    def V(fn, deps=(), pub=False, tiny=False):
        return emit("vector", fn, deps, pub=pub, tiny=tiny)

    def S(fn, deps=(), pub=False, tiny=False):
        return emit("scalar", fn, deps, pub=pub, tiny=tiny)

    def TT(fn, deps=(), pub=False):
        return emit("tensor", fn, deps, pub=pub)

    l_convw = dma("sync", "ini0", convw[:, :], convw_h)
    l_convb = dma("sync", "ini1", convb[:, :], convb_h)
    l_inb = dma("sync", "ini2", inb_s[:, :], inb_h)
    l_xpb = dma("sync", "ini3", xpba_s[:, :], xpba_h)
    l_xpb2 = dma("sync", "ini1", xpbb_s[:, :], xpbb_h)
    l_idf = dma("sync", "ini0", id_f[:, :], ident_f_h)
    l_idb = dma("sync", "ini1", id_b[:, :], ident_b_h)
    l_egu = dma("sync", "ini2", egu_s[:, :], egu_h)
    l_ed = dma("sync", "ini3", ed_s[:, :], ed_h)
    l_outb = dma("sync", "ini0", outb_s[:, :], outb_h)
    l_oh = dma("sync", "ini1", onehot_s[:, :], onehot_h)
    l_hm = dma("sync", "ini2", hmask_s[:, :], hmask_h)
    l_ones1 = dma("sync", "ini3", ones1[:, :], ones1_h)
    l_c = dma("sync", "ini3", silc[:, :], c_h)
    l_wmod = dma("sync", "wmodk", wmodb[:, :].rearrange("p (m k q) -> p m k q", m=6, k=8),
                  wmod_h.rearrange("m k p q -> p m k q"))
    win_tok = [None, None]
    win_free = [None, None]
    for _pb in range(2):
        win_tok[_pb] = dma("gpsimd", f"win{_pb}",
                           win_bf[_pb][:, :].rearrange("p (k q) -> p k q", q=128),
                           winT_h[_pb], deps=[])
    l_xpw = dma("gpsimd", "gini0", xpw_bf[:, :].rearrange("p (k t) -> p k t", t=32),
                 xpwT_h.rearrange("k p t -> p k t"))
    l_outw = dma("gpsimd", "gini1", outw_bf[:, :].rearrange("p (m t) -> p m t", t=128),
                  outwT_h.rearrange("m p t -> p m t"))
    t_ones = V(lambda e: e.memset(ones_c[:, :], 1.0), pub=True)
    t_scl0 = V(lambda e: e.memset(scl[:, :], 0.0), pub=True)

    t_silc = S(lambda e: e.activation(silc[:, :], silc[:, :], Act.Silu),
               deps=[l_c], pub=True, tiny=True)

    mod_mm = []
    for m in range(6):
        for k in range(8):
            tk = TT(lambda e, m=m, k=k: e.matmul(
                ps_b[:, m:m + 1],
                wmodb[:, (m * 8 + k) * 128:(m * 8 + k) * 128 + 128],
                silc[:, k:k + 1], start=(k == 0), stop=(k == 7)),
                deps=[l_wmod, t_silc], pub=(k == 7))
        mod_mm.append(tk)
    mod_ev = []
    for m in range(6):
        mod_ev.append(V(lambda e, m=m: e.tensor_copy(
            modloc[:, m:m + 1], ps_b[:, m:m + 1]), deps=mod_mm, pub=True, tiny=True))
    t_modw = None
    for m in range(6):
        t_modw = dma("gpsimd", "ag0w", ag0_in.ap()[0:1, 128 * m:128 * m + 128]
                     .rearrange("one p -> p one"), modloc[:, m:m + 1],
                     deps=[mod_ev[m]])
    t_ag0 = emit("gpsimd", lambda e: e.collective_compute(
        "AllGather", Alu.bypass, replica_groups=[list(range(NCORES))],
        ins=[ag0_in.ap().opt()], outs=[ag0_out.ap().opt()]),
        deps=[t_modw], cc="cc0")
    flat0 = ag0_out.ap().rearrange("r v -> () (r v)")
    tr_guard = [None, None]

    def tr_slot(si):
        return (ps_tr if si % 2 == 0 else ps_tr2)[
            :, 128 * ((si // 2) % 4):128 * ((si // 2) % 4) + 128]

    h1_toks = []
    x_toks = [None] * 16
    lnprev = [None, None]
    tr_ctr = [0]
    xhat_guard = [None, None]

    def ln_sums(i, col, dst, ap_dst_tok, extra_deps=()):
        par = i % 2
        sq_ = sq1 if par == 0 else sq1b
        t_sum = V(lambda e, col=col: e.tensor_reduce(
            lnsum[:, col:col + 1], dst[:, :], AX, Alu.add),
            deps=[ap_dst_tok] + list(extra_deps), pub=True, tiny=True)
        t_sq0 = S(lambda e, sq_=sq_: e.activation(sq_[:, :], dst[:, :],
                                                  Act.Square),
                  deps=[ap_dst_tok, lnprev[par]] + list(extra_deps), pub=True)
        t_sq = V(lambda e, col=col, sq_=sq_: e.tensor_reduce(
            lnsq[:, col:col + 1], sq_[:, :], AX, Alu.add),
            deps=[t_sq0], pub=True, tiny=True)
        lnprev[par] = t_sq
        return [t_sum, t_sq]

    def ln_finish(ncols, deps):
        c = ncols
        t1 = V(lambda e: e.tensor_scalar_mul(lnmean[:, 0:c], lnsum[:, 0:c],
                                             1.0 / D), deps=deps, pub=True,
               tiny=True)
        t2 = V(lambda e: e.tensor_scalar_mul(lnms[:, 0:c], lnsq[:, 0:c],
                                             1.0 / D), deps=deps, pub=True,
               tiny=True)
        t3 = V(lambda e: e.tensor_tensor(lnm2[:, 0:c], lnmean[:, 0:c],
                                         lnmean[:, 0:c], Alu.mult),
               deps=[t1], pub=True, tiny=True)
        t4 = V(lambda e: e.tensor_tensor(lnvar[:, 0:c], lnms[:, 0:c],
                                         lnm2[:, 0:c], Alu.subtract),
               deps=[t2, t3], pub=True, tiny=True)
        t5 = V(lambda e: e.tensor_scalar_add(lnrc[:, 0:c], lnvar[:, 0:c], EPS),
               deps=[t4], pub=True, tiny=True)
        t6 = V(lambda e: e.reciprocal(lnsum[:, 0:c], lnrc[:, 0:c]),
               deps=[t5], pub=True, tiny=True)
        return S(lambda e: e.activation(lnrstd[:, 0:c], lnsum[:, 0:c],
                                        Act.Sqrt),
                 deps=[t6], pub=True, tiny=True)

    def ln_apply(i, col, dst, t_stats, affine_c1, affine_c0, out_chunk_fn,
                 extra_deps=()):
        xh = xhat[i % 2]
        t_xh = V(lambda e, col=col: e.tensor_scalar(
            xh[:, :], dst[:, :], lnmean[:, col:col + 1],
            lnrstd[:, col:col + 1], Alu.subtract, Alu.mult),
            deps=[t_stats, xhat_guard[i % 2]] + list(extra_deps), pub=True)
        evs = []
        t_tr = None
        for j in range(8):
            si = tr_ctr[0] % 8
            tr_ctr[0] += 1
            t_tr = TT(lambda e, j=j, si=si, xh=xh: e.transpose(
                tr_slot(si), xh[:, 128 * j:128 * j + 128], id_b[:, :]),
                deps=[t_xh, l_idb, tr_guard[si % 2]], pub=True)
            ev = V(lambda e, j=j, si=si: e.tensor_scalar(
                out_chunk_fn(j), tr_slot(si),
                adp[:, affine_c1 + j:affine_c1 + j + 1],
                adp[:, affine_c0 + j:affine_c0 + j + 1], Alu.mult, Alu.add),
                deps=[t_tr, t_ca1, t_cb1], pub=True, tiny=True)
            tr_guard[si % 2] = ev
            evs.append(ev)
        xhat_guard[i % 2] = t_tr
        return evs

    agw_writes = []
    lp_guard = [None]
    lp_rm_guard = [None]

    def lpbq_slot(s):
        I = D if s < 6 else FF
        B = I // 32
        if s < 6:
            t_row = dma("sync", "lpr", wrow[:, 0:I], lpq_rows_gu_h[s],
                        deps=[lp_rm_guard[0]])
            t_lh = dma("sync", "lpl",
                       wlhs[:, 0:I].rearrange("p (k q) -> p k q", q=128),
                       lpq_lhsT_gu_h[s], deps=[lp_guard[0]])
        else:
            t_row = dma("sync", "lpr", wrow[:, 0:I], lpq_rows_d_h,
                        deps=[lp_rm_guard[0]])
            t_lh = dma("sync", "lpl",
                       wlhs[:, 0:I].rearrange("p (k q) -> p k q", q=128),
                       lpq_lhsT_d_h, deps=[lp_guard[0]])
        t_rm = V(lambda e, I=I, B=B: e.tensor_reduce(
            smax[:, 0:B].rearrange("p b -> p b ()"),
            wrow[:, 0:I].rearrange("p (b t) -> p b t", t=32),
            AX, Alu.max, apply_absolute_value=True), deps=[t_row], pub=True, tiny=True)
        lp_rm_guard[0] = t_rm
        t_s = V(lambda e, B=B: e.tensor_scalar(
            sval[:, 0:B], smax[:, 0:B], 1.0 / 7.0, 1e-12, Alu.mult, Alu.max),
            deps=[t_rm], pub=True, tiny=True)
        t_i = V(lambda e, B=B: e.reciprocal(sinv[:, 0:B], sval[:, 0:B]),
                deps=[t_s], pub=True, tiny=True)
        t_t1 = TT(lambda e, B=B: e.transpose(ps_mm[2][0:B, 0:128], sval[:, 0:B],
                                             id_f[:, :]),
                  deps=[t_s, l_idf, lp_guard[0]], pub=True)
        t_t2 = TT(lambda e, B=B: e.transpose(ps_mm[3][0:B, 0:128], sinv[:, 0:B],
                                             id_f[:, :]),
                  deps=[t_i, l_idf, lp_guard[0]], pub=True)
        B4 = B
        ev1 = V(lambda e, B=B: e.tensor_copy(sT_s[0:B, :], ps_mm[2][0:B, 0:128]),
                deps=[t_t1], pub=True, tiny=True)
        ev2 = V(lambda e, B=B: e.tensor_copy(invT_s[0:B, :], ps_mm[3][0:B, 0:128]),
                deps=[t_t2], pub=True, tiny=True)
        E_s = egu_s if s < 6 else ed_s
        l_E = l_egu if s < 6 else l_ed
        wq_toks = []
        piece_guard = [agw_writes[-1]] if agw_writes else []
        for p in range(I // 512):
            exps = []
            for q in range(4):
                kc = 4 * p + q
                exps.append(TT(lambda e, q=q, kc=kc, B=B, E_s=E_s: e.matmul(
                    ps_a[:, 128 * q:128 * q + 128],
                    E_s[0:B, 128 * kc:128 * kc + 128], sT_s[0:B, :],
                    start=True, stop=True),
                    deps=[ev1, ev2, l_E] + mod_ev + piece_guard, pub=(q == 3)))
                exps.append(TT(lambda e, q=q, kc=kc, B=B, E_s=E_s: e.matmul(
                    ps_b[:, 128 * q:128 * q + 128],
                    E_s[0:B, 128 * kc:128 * kc + 128], invT_s[0:B, :],
                    start=True, stop=True),
                    deps=[ev1, ev2, l_E] + mod_ev + piece_guard, pub=(q == 3)))
            c0 = 512 * p
            t_u = V(lambda e, c0=c0: e.tensor_tensor(
                hc[:, 0:512], wlhs[:, c0:c0 + 512], ps_b[:, 0:512], Alu.mult),
                deps=[t_lh, exps[-1], exps[-2]], pub=True)
            t_t = V(lambda e: e.tensor_scalar_add(hc[:, 512:1024], hc[:, 0:512],
                                                  MAGIC), deps=[t_u], pub=True)
            wq_toks.append(V(lambda e, c0=c0: e.scalar_tensor_tensor(
                wqcol[:, c0:c0 + 512], hc[:, 512:1024], MAGIC, ps_a[:, 0:512],
                Alu.subtract, Alu.mult), deps=[t_t], pub=True))
            piece_guard = [t_u, wq_toks[-1]]
        lp_guard[0] = wq_toks[-1]
        eo = SLOT_OFF[s] // 2
        agw_writes.append(dma(
            "gpsimd", "agww",
            agw_in.ap()[0:1, eo:eo + 128 * I].rearrange(
                "one (p q) -> (one p) q", p=128),
            wqcol[:, 0:I], deps=wq_toks))

    x_dst = []
    sum_deps = []
    for i in range(17):
        dst = xstage if i == 0 else x_tm[i - 1]
        x_dst.append(dst)
        t_x = dma("sync", f"xl{i % 4}", dst[:, :], x_h[128 * i:128 * i + 128, :])
        if i >= 1:
            x_toks[i - 1] = t_x
        sum_deps += ln_sums(i, i, dst, t_x)
        if i < 7:
            lpbq_slot(i)
    t_ad = []
    for col in range(48):
        vi, j = col // 8, col % 8
        g = vi * 8 + j
        t_ad.append(dma("gpsimd", f"adpu{col % 4}",
                        adp[:, col:col + 1],
                        flat0[0:1, 128 * g:128 * g + 128].rearrange(
                            "one p -> p one"), deps=[t_ag0]))
    t_ca1 = V(lambda e: e.tensor_scalar_add(adp[:, 8:16], adp[:, 8:16], 1.0),
              deps=t_ad, pub=True, tiny=True)
    t_cb1 = V(lambda e: e.tensor_scalar_add(adp[:, 32:40], adp[:, 32:40], 1.0),
              deps=t_ad, pub=True, tiny=True)
    gq_toks = []
    prev_gq = None
    for src0 in (16, 40):
        tsg = S(lambda e, s=src0: e.activation(adp[:, 48:56], adp[:, s:s + 8],
                                               Act.Sign),
                deps=t_ad + ([prev_gq] if prev_gq else []), pub=True, tiny=True)
        tab = S(lambda e, s=src0: e.activation(adp[:, 56:64], adp[:, s:s + 8],
                                               Act.Abs), deps=t_ad, pub=True, tiny=True)
        tmx = V(lambda e: e.tensor_scalar(gqs[:, :], adp[:, 56:64], 0.25,
                                          None, Alu.max),
                deps=[tab], pub=True, tiny=True)
        prev_gq = V(lambda e, s=src0: e.tensor_tensor(
            adp[:, s:s + 8], adp[:, 48:56], gqs[:, :], Alu.mult),
            deps=[tsg, tmx], pub=True, tiny=True)

        gq_toks.append(prev_gq)
    t_gaq, t_gbq = gq_toks

    dbg_cap("adp", adp[:, :], [t_gaq, t_gbq, t_ca1, t_cb1], 64)
    t_ln1 = ln_finish(17, sum_deps)
    for i in range(17):
        evs = ln_apply(i, i, x_dst[i], t_ln1, 8, 0,
                       lambda j, i=i: h1_bf[j][:, 128 * i:128 * i + 128])
        h1_toks.extend(evs)
    dbg_cap("h1_0", h1_bf[0][:, 0:512], h1_toks, 512)
    t_agw = emit("gpsimd", lambda e: e.collective_compute(
        "AllGather", Alu.bypass, replica_groups=[list(range(NCORES))],
        ins=[agw_in.ap().opt()], outs=[agw_out.ap().opt()]),
        deps=agw_writes, cc="ccw")

    NTILES_IN = [(0, 128), (128, 512), (640, 512), (1152, 512), (1664, 512)]
    bank_free = [None] * 4
    hprime = []
    for m in range(16):
        pb = m % 2
        if m >= 2:
            win_tok[pb] = dma("gpsimd", f"win{pb}",
                              win_bf[pb][:, :].rearrange("p (k q) -> p k q",
                                                         q=128),
                              winT_h[m], deps=[win_free[pb]])
        ev_toks = []
        for ni, (c0, w) in enumerate(NTILES_IN):
            bk = (m * 5 + ni) % 4
            last_tile = (c0 + w - 1) // 128
            h1_dep = h1_toks[:8 * (last_tile + 1)]
            mmt = None
            for k in range(8):
                mmt = TT(lambda e, pb=pb, bk=bk, k=k, c0=c0, w=w: e.matmul(
                    ps_mm[bk][:, 0:w], win_bf[pb][:, 128 * k:128 * k + 128],
                    h1_bf[k][:, c0:c0 + w], start=(k == 0), stop=(k == 7)),
                    deps=[win_tok[pb], bank_free[bk]] + (h1_dep if k == 0 else []),
                    pub=(k == 7))
            ev = S(lambda e, m=m, bk=bk, c0=c0, w=w: e.activation(
                h_bf[m][:, c0:c0 + w], ps_mm[bk][:, 0:w], Act.Identity,
                bias=inb_s[:, m:m + 1]),
                deps=[mmt, l_inb] + (agw_writes if m <= 7 else []), pub=True,
                tiny=True)
            bank_free[bk] = ev
            ev_toks.append(ev)
        win_free[pb] = ev_toks[-1]
        t_msk = V(lambda e, m=m: e.tensor_scalar(
            h_bf[m][:, HALO - 3:HALO], h_bf[m][:, HALO - 3:HALO],
            hmask_s[:, 0:1], None, Alu.mult), deps=ev_toks + [l_hm], pub=True)
        t_c = V(lambda e, m=m: e.tensor_scalar(
            hc[:, 0:T], h_bf[m][:, HALO - 3:HALO - 3 + T], convw[:, 4 * m:4 * m + 1],
            None, Alu.mult),
            deps=[t_msk, l_convw, hprime[-1] if hprime else None], pub=True)
        for j in (1, 2, 3):
            t_c = V(lambda e, m=m, j=j: e.scalar_tensor_tensor(
                hc[:, 0:T], h_bf[m][:, HALO - 3 + j:HALO - 3 + j + T],
                convw[:, 4 * m + j:4 * m + j + 1], hc[:, 0:T], Alu.mult, Alu.add),
                deps=[t_c], pub=True)
        hprime.append(S(lambda e, m=m: e.activation(
            h_bf[m][:, HALO:HALO + T], hc[:, 0:T], Act.Silu,
            bias=convb[:, m:m + 1]), deps=[t_c, l_convb], pub=True))

    dbg_cap("hprime0", h_bf[0][:, HALO:HALO + 512], hprime, 512)
    xp_toks = []
    for n in range(4):
        mmta = None
        mmtb = None
        for k in range(16):
            mmta = TT(lambda e, n=n, k=k: e.matmul(
                ps_a[0:16, 0:512], xpw_bf[:, 32 * k:32 * k + 16],
                h_bf[k][:, HALO + 512 * n:HALO + 512 * n + 512],
                start=(k == 0), stop=(k == 15)),
                deps=[l_xpw, hprime[k]] + (xp_toks[-2:] if xp_toks else []),
                pub=(k == 15))
            mmtb = TT(lambda e, n=n, k=k: e.matmul(
                ps_mm[0][0:16, 0:512], xpw_bf[:, 32 * k + 16:32 * k + 32],
                h_bf[k][:, HALO + 512 * n:HALO + 512 * n + 512],
                start=(k == 0), stop=(k == 15)),
                deps=[l_xpw, hprime[k], bank_free[0]]
                + (xp_toks[-2:] if xp_toks else []),
                pub=(k == 15))
        xp_toks.append(V(lambda e, n=n: e.tensor_scalar(
            scanA[0:16, 512 * n:512 * n + 512], ps_a[0:16, 0:512],
            xpba_s[0:16, 0:1], None, Alu.add), deps=[mmta, l_xpb], pub=True))
        xp_toks.append(V(lambda e, n=n: e.tensor_scalar(
            scanC[0:16, 512 * n:512 * n + 512], ps_mm[0][0:16, 0:512],
            xpbb_s[0:16, 0:1], None, Alu.add), deps=[mmtb, l_xpb2], pub=True))
    sum_toks = []
    for n in range(4):
        mmt = None
        for k in range(16):
            mmt = TT(lambda e, n=n, k=k: e.matmul(
                ps_b[0:1, 0:512], ones_c[:, :],
                h_bf[k][:, HALO + 512 * n:HALO + 512 * n + 512],
                start=(k == 0), stop=(k == 15)),
                deps=[t_ones, hprime[k]] + (sum_toks[-1:] if sum_toks else []),
                pub=(k == 15))
        t_se = V(lambda e: e.tensor_copy(sumstg[:, :], ps_b[0:1, 0:512]),
                 deps=[mmt, sum_toks[-1] if sum_toks else None], pub=True, tiny=True)
        sum_toks.append(dma("sync", "sumw", sumd.ap()[0:1, 512 * n:512 * n + 512],
                            sumstg[:, :], deps=[t_se]))
    t_e = S(lambda e: e.activation(scanB[0:16, :], scanA[0:16, :], Act.Exp),
            deps=xp_toks + sum_toks, pub=True)
    t_a = S(lambda e: e.activation(scanB[0:16, :], scanB[0:16, :], Act.Exp,
                                   scale=-1.0), deps=[t_e], pub=True)
    t_sl = S(lambda e: e.activation(scanC[:, :], scanC[:, :], Act.Silu),
             deps=xp_toks, pub=True)
    t_sb = dma("gpsimd", "sumb", scanD[:, :], sumd.ap().partition_broadcast(NS),
               deps=sum_toks)
    t_b = V(lambda e: e.tensor_tensor(scanC[:, :], scanC[:, :], scanD[:, :],
                                      Alu.mult), deps=[t_sl, t_sb], pub=True)
    t_y0 = V(lambda e: e.tensor_tensor_scan(scanD[:, :], scanB[0:16, :],
                                            scanC[:, :], 0.0, Alu.mult, Alu.add),
             deps=[t_a, t_b], pub=True)
    t_P = V(lambda e: e.tensor_tensor_scan(scanA[0:16, :], scanB[0:16, :],
                                           scanB[0:16, :], 1.0, Alu.mult,
                                           Alu.bypass), deps=[t_a], pub=True)
    t_c1 = dma("gpsimd", "cry", ag1_in.ap()[0:1, 0:16].rearrange("one n -> n one"),
               scanA[0:16, T - 1:T], deps=[t_P])
    t_c2 = dma("gpsimd", "cry", ag1_in.ap()[0:1, 16:32].rearrange("one n -> n one"),
               scanD[:, T - 1:T], deps=[t_y0])
    t_ag1 = emit("gpsimd", lambda e: e.collective_compute(
        "AllGather", Alu.bypass, replica_groups=[list(range(NCORES))],
        ins=[ag1_in.ap().opt()], outs=[ag1_out.ap().opt()]),
        deps=[t_c1, t_c2], cc="cc1")
    t_g1 = t_g2 = None
    for r in range(NCORES):
        t_g1 = dma("gpsimd", f"cryo{r % 2}", cgat[:, r:r + 1],
                   ag1_out.ap()[r:r + 1, 0:16].rearrange("one n -> n one"),
                   deps=[t_ag1])
        t_g2 = dma("gpsimd", f"cryp{r % 2}", cgee[:, r:r + 1],
                   ag1_out.ap()[r:r + 1, 16:32].rearrange("one n -> n one"),
                   deps=[t_ag1])
    t_cs = V(lambda e: e.tensor_tensor_scan(csc[:, :], cgat[:, :], cgee[:, :],
                                            0.0, Alu.mult, Alu.add),
             deps=[t_g1, t_g2, last_dma["cryo0"], last_dma["cryp0"]],
             pub=True, tiny=True)
    t_sel = V(lambda e: e.tensor_tensor(csc[:, :], csc[:, :], onehot_s[:, :],
                                        Alu.mult), deps=[t_cs, l_oh], pub=True, tiny=True)
    t_sin = V(lambda e: e.tensor_reduce(sin_s[:, 0:1], csc[:, :], AX, Alu.add),
              deps=[t_sel], pub=True, tiny=True)
    t_y = V(lambda e: e.tensor_tensor_scan(y_bf[0:16, :], scanB[0:16, :],
                                           scanC[:, :], sin_s[0:16, 0:1],
                                           Alu.mult, Alu.add),
            deps=[t_sin], pub=True)

    dbg_cap("a_", scanB[0:16, 0:512], [t_a], 512, parts=16)
    dbg_cap("b_", scanC[0:16, 0:512], [t_b, t_y0], 512, parts=16)
    dbg_cap("sin", sin_s[:, 0:1], [t_sin], 1, parts=16)
    dbg_cap("y", y_bf[0:16, 0:512], [t_y], 512, parts=16)
    t_gaob2 = V(lambda e: e.tensor_tensor(gaob[:, :], adp[:, 16:24],
                                          outb_s[:, :], Alu.mult),
                deps=[t_gaq, l_outb], pub=True, tiny=True)
    res_toks = []
    res_by = [[] for _ in range(16)]
    ctile_free = [None] * 4
    pend = [None]

    def flush_pend(pn, pmo, pev, psl):
        for q in range(4):
            si = tr_ctr[0] % 8
            tr_ctr[0] += 1
            t_tr = TT(lambda e, psl=psl, q=q, si=si: e.transpose(
                tr_slot(si), ctile[psl][:, 128 * q:128 * q + 128], id_b[:, :]),
                deps=[pev, tr_guard[si % 2]], pub=True)
            ra = V(lambda e, pn=pn, q=q, pmo=pmo, si=si: e.tensor_tensor(
                x_tm[4 * pn + q][:, 128 * pmo:128 * pmo + 128],
                x_tm[4 * pn + q][:, 128 * pmo:128 * pmo + 128],
                tr_slot(si), Alu.add),
                deps=[t_tr, x_toks[4 * pn + q]], pub=True, tiny=True)
            tr_guard[si % 2] = ra
            res_toks.append(ra)
            res_by[4 * pn + q].append(ra)
            ctile_free[psl] = t_tr

    for n in range(4):
        for mo in range(8):
            bk = (n * 8 + mo) % 4
            mmt = TT(lambda e, bk=bk, mo=mo, n=n: e.matmul(
                ps_mm[bk][:, 0:512], outw_bf[0:16, 128 * mo:128 * mo + 128],
                y_bf[0:16, 512 * n:512 * n + 512], start=True, stop=True),
                deps=[l_outw, t_y, bank_free[bk]], pub=True)
            sl = (n * 8 + mo) % 4
            ev = S(lambda e, bk=bk, mo=mo, sl=sl: e.activation(
                ctile[sl][:, :], ps_mm[bk][:, 0:512], Act.Identity,
                scale=adp[:, 16 + mo:17 + mo], bias=gaob[:, mo:mo + 1]),
                deps=[mmt, t_gaob2, ctile_free[sl], t_c2, t_g1, t_g2],
                pub=True, tiny=True)
            bank_free[bk] = ev
            if pend[0] is not None:
                flush_pend(*pend[0])
            pend[0] = (n, mo, ev, sl)
    flush_pend(*pend[0])
    pend[0] = None

    dbg_cap("x2_0", x_tm[0][:, 0:512], res_toks, 512)
    h2_toks = []
    xsp_toks = []
    sum2_deps = []
    for i in range(16):
        sum2_deps += ln_sums(100 + i, i, x_tm[i], None, extra_deps=res_by[i])
    t_ln2 = ln_finish(16, sum2_deps)
    for i in range(16):
        evs = ln_apply(100 + i, i, x_tm[i], t_ln2, 32, 24,
                       lambda j, i=i: hx[j][:, 128 * i:128 * i + 128],
                       extra_deps=res_by[i])
        h2_toks.extend(evs)
        xsp_toks.append(dma("sync", f"xsp{i % 4}", xspill[i], x_tm[i][:, :],
                            deps=evs))
    mx_h2 = []
    for j in range(8):
        mx_h2.append(V(lambda e, j=j: e.tensor_reduce(
            mxc[:, j:j + 1], hx[j][:, :], AX, Alu.max,
            apply_absolute_value=True), deps=h2_toks, pub=True, tiny=True))

    def global_max(src_col_range, dst_slot, deps):
        a, b = src_col_range
        t1 = V(lambda e: e.tensor_reduce(mxc[:, 96:97], mxc[:, a:b], AX,
                                         Alu.max), deps=deps, pub=True, tiny=True)
        return emit("gpsimd", lambda e, d=dst_slot: e.tensor_reduce(
            scl[0:1, d:d + 1], mxc[:, 96:97], mybir.AxisListType.C, Alu.max),
            deps=[t1], pub=True, tiny=True)

    def allreduce_pair(src_slot, deps):
        t1 = dma("sync", "arb", ar_in.ap(), scl[0:1, src_slot:src_slot + 2],
                 deps=deps)
        arid = ar_ctr[0]
        ar_ctr[0] += 1
        t2 = emit("gpsimd", lambda e: e.collective_compute(
            "AllReduce", Alu.max, replica_groups=[list(range(NCORES))],
            ins=[ar_in.ap().opt()], outs=[ar_out.ap().opt()]),
            deps=[t1], cc=f"ccr{arid}")
        return dma("sync", "arb", scl[0:1, src_slot:src_slot + 2], ar_out.ap(),
                   deps=[t2])

    ar_ctr = [0]
    bc_guard = [None]

    def bcast_scl(deps):
        tm = TT(lambda e: e.matmul(ps_b[:, 0:32], ones1[:, :], scl[0:1, :],
                                   start=True, stop=True),
                deps=list(deps) + [l_ones1, bc_guard[0], sum_toks[-1] if sum_toks else None],
                pub=True)
        tv = V(lambda e: e.tensor_copy(sclb[:, :], ps_b[:, 0:32]),
               deps=[tm], pub=True, tiny=True)
        bc_guard[0] = tv
        return tv

    dbg_cap("h2_0", hx[0][:, 0:512], h2_toks, 512)
    t_mx1 = global_max((0, 8), 0, mx_h2 + [t_scl0])
    t_ar1 = allreduce_pair(0, [t_mx1])
    t_s1 = V(lambda e: e.tensor_scalar(scl[0:1, 1:2], scl[0:1, 0:1], INV_QMAX,
                                       1e-12, Alu.mult, Alu.max),
             deps=[t_ar1], pub=True, tiny=True)
    t_i1 = V(lambda e: e.reciprocal(scl[0:1, 2:3], scl[0:1, 1:2]),
             deps=[t_s1], pub=True, tiny=True)
    t_bc1 = bcast_scl([t_i1])

    dbg_cap("scl1", scl[:, :], [t_i1, t_bc1], 32, parts=1)
    dbg_cap("sclb1", sclb[:, :], [t_bc1], 32)
    xq_toks = [None] * 32
    for j in range(8):
        for qq in range(4):
            rs = rtring[qq]
            c0 = 512 * qq
            t_r = V(lambda e, j=j, c0=c0, rs=rs: e.tensor_scalar(
                rs[:, :], hx[j][:, c0:c0 + 512], sclb[:, 2:3], MAGIC,
                Alu.mult, Alu.add),
                deps=[t_bc1, xq_toks[4 * j + qq - 4] if 4 * j + qq >= 4 else None]
                + h2_toks, pub=True)
            xq_toks[4 * j + qq] = V(lambda e, j=j, c0=c0, rs=rs: e.tensor_scalar(
                hx[j][:, c0:c0 + 512], rs[:, :], MAGIC, None, Alu.subtract),
                deps=[t_r], pub=True)

    scan_dead = [t_y, t_c1, t_c2, hprime[-1], xp_toks[-1], sum_toks[-1]]
    exp_dead = xp_toks[-1]

    def ffn_col_load(j, pb, deps):
        r, eo = col_addr(j)
        I = D if j < 48 else FF
        return dma("sync", f"wff{pb}", wffcol[pb][:, 0:I],
                   agw_out.ap()[r:r + 1, eo:eo + 128 * I].rearrange(
                       "one (p q) -> (one p) q", p=128),
                   deps=deps + [t_agw] + scan_dead)

    wff_tok = [None, None]
    wff_free = [None, None]
    dbg_cap("xq_0", hx[0][:, 0:512], xq_toks, 512)
    gate_ev = []
    bank_free = [None] * 5
    for m in range(24):
        pb = m % 2
        wff_tok[pb] = ffn_col_load(m, pb, [wff_free[pb]])
        for n in range(4):
            bk = (m * 4 + n) % 5
            mmt = None
            for k in range(8):
                mmt = TT(lambda e, pb=pb, bk=bk, k=k, n=n: e.matmul(
                    psk(bk)[:, :], wffcol[pb][:, 128 * k:128 * k + 128],
                    hx[k][:, 512 * n:512 * n + 512],
                    start=(k == 0), stop=(k == 7)),
                    deps=[wff_tok[pb], bank_free[bk], xq_toks[4 * k + n],
                          exp_dead], pub=(k == 7))
            ev = V(lambda e, m=m, n=n, bk=bk: e.tensor_scalar(
                gP(m)[:, 512 * n:512 * n + 512], psk(bk)[:, :], sclb[:, 1:2],
                None, Alu.mult),
                deps=[mmt, t_bc1] + ([xsp_toks[m]] if m < 16 else []), pub=True,
                tiny=True)
            bank_free[bk] = ev
            gate_ev.append(ev)
            gate_ev.append(V(lambda e, m=m, n=n: e.tensor_reduce(
                mxc[:, 4 * m + n:4 * m + n + 1], gP(m)[:, 512 * n:512 * n + 512],
                AX, Alu.max, apply_absolute_value=True), deps=[ev], pub=True, tiny=True))
        wff_free[pb] = gate_ev[-2]
    dbg_cap("gate_0", gP(0)[:, 0:512], gate_ev, 512)
    t_mxg = global_max((0, 96), 3, gate_ev)
    t_ar2 = allreduce_pair(3, [t_mxg])
    t_sg = V(lambda e: e.tensor_scalar(scl[0:1, 4:5], scl[0:1, 3:4], INV_QMAX,
                                       1e-12, Alu.mult, Alu.max),
             deps=[t_ar2], pub=True, tiny=True)
    t_ig = V(lambda e: e.reciprocal(scl[0:1, 5:6], scl[0:1, 4:5]),
             deps=[t_sg], pub=True, tiny=True)
    t_bc2 = bcast_scl([t_ig])

    up_ev = []
    up_spill_toks = {}
    upr_free = [None] * 8
    up_last_mxe = [None] * 8
    w1_toks = []
    w1_rt_guard = [None] * 4

    def window1_chunk(m):
        for hh in range(4):
            c0 = 512 * hh
            rs = rtring[hh]
            xh = xhat[hh % 2][:, 512 * (hh // 2):512 * (hh // 2) + 512]
            t1 = S(lambda e, m=m, c0=c0, rs=rs: e.activation(
                rs[:, :], gP(m)[:, c0:c0 + 512], Act.Copy, bias=MAGIC,
                scale=sclb[:, 5:6]),
                deps=[t_bc2, w1_rt_guard[hh]] + gate_ev, pub=True)
            t2 = V(lambda e, m=m, c0=c0, rs=rs: e.tensor_scalar(
                gP(m)[:, c0:c0 + 512], rs[:, :], MAGIC, None, Alu.subtract),
                deps=[t1], pub=True)
            t3 = S(lambda e, m=m, c0=c0, xh=xh: e.activation(
                xh[:, :], gP(m)[:, c0:c0 + 512], Act.Sigmoid,
                scale=sclb[:, 4:5]), deps=[t2], pub=True)
            t4 = S(lambda e, xh=xh, rs=rs: e.activation(
                rs[:, :], xh[:, :], Act.Copy, bias=MAGIC, scale=65536.0),
                deps=[t3], pub=True)
            t5 = V(lambda e, rs=rs: e.tensor_scalar(
                rs[:, :], rs[:, :], MAGIC, 65535.0, Alu.subtract, Alu.min),
                deps=[t4], pub=True)
            t6 = V(lambda e, m=m, c0=c0, rs=rs: e.tensor_tensor(
                gP(m)[:, c0:c0 + 512], rs[:, :], gP(m)[:, c0:c0 + 512],
                Alu.mult), deps=[t5], pub=True)
            w1_rt_guard[hh] = t6
            w1_toks.append(V(lambda e, m=m, c0=c0, hh=hh: e.tensor_reduce(
                mxc2[:, (4 * m + hh):(4 * m + hh) + 1],
                gP(m)[:, c0:c0 + 512], AX, Alu.max,
                apply_absolute_value=True), deps=[t6], pub=True, tiny=True))

    for m in range(24):
        pb = m % 2
        wff_tok[pb] = ffn_col_load(24 + m, pb, [wff_free[pb]])
        for n in range(4):
            bk = (m * 4 + n) % 5
            sl = (m * 4 + n) % 8
            mmt = None
            for k in range(8):
                mmt = TT(lambda e, pb=pb, bk=bk, k=k, n=n: e.matmul(
                    psk(bk)[:, :], wffcol[pb][:, 128 * k:128 * k + 128],
                    hx[k][:, 512 * n:512 * n + 512],
                    start=(k == 0), stop=(k == 7)),
                    deps=[wff_tok[pb], bank_free[bk]], pub=(k == 7))
            ev = S(lambda e, sl=sl, bk=bk: e.activation(
                uprng[sl][:, :], psk(bk)[:, :], Act.Copy, scale=sclb[:, 1:2]),
                deps=[mmt, upr_free[sl], t_bc2], pub=True, tiny=True)
            bank_free[bk] = ev
            mxe = V(lambda e, m=m, n=n, sl=sl: e.tensor_reduce(
                mxc[:, 4 * m + n:4 * m + n + 1], uprng[sl][:, :], AX, Alu.max,
                apply_absolute_value=True), deps=[ev], pub=True, tiny=True)
            up_ev.append(mxe)
            up_last_mxe[sl] = mxe
            sp = dma("sync", f"upr{sl}", upspill.ap()[m, n], uprng[sl][:, :],
                     deps=[ev])
            up_spill_toks[(m, n)] = sp
            upr_free[sl] = sp
        window1_chunk(m)

    dbg_cap("scl2", scl[:, :], [t_bc2], 32, parts=1)
    dbg_cap("P_0", gP(0)[:, 0:512], w1_toks, 512)
    t_mxu = global_max((0, 96), 6, up_ev)
    t_mxP1 = V(lambda e: e.tensor_reduce(mxc2[:, 95:96]
                                         if False else mxc[:, 97:98],
                                         mxc2[:, 0:96], AX, Alu.max),
               deps=w1_toks, pub=True, tiny=True)
    t_mxP = emit("gpsimd", lambda e: e.tensor_reduce(
        scl[0:1, 7:8], mxc[:, 97:98], mybir.AxisListType.C, Alu.max),
        deps=[t_mxP1, t_mxu], pub=True, tiny=True)
    t_ar3 = allreduce_pair(6, [t_mxu, t_mxP])
    t_sup = V(lambda e: e.tensor_scalar(scl[0:1, 8:9], scl[0:1, 6:7], INV_QMAX,
                                        1e-12, Alu.mult, Alu.max),
              deps=[t_ar3], pub=True, tiny=True)
    t_iup = V(lambda e: e.reciprocal(scl[0:1, 9:10], scl[0:1, 8:9]),
              deps=[t_sup], pub=True, tiny=True)
    t_cg = V(lambda e: e.tensor_scalar_mul(scl[0:1, 12:13], scl[0:1, 4:5],
                                           1.0 / 65536.0), deps=[t_sg], pub=True, tiny=True)
    t_mg2 = V(lambda e: e.tensor_tensor(scl[0:1, 20:21], scl[0:1, 7:8],
                                        scl[0:1, 12:13], Alu.mult),
              deps=[t_ar3, t_cg], pub=True, tiny=True)
    t_s4 = V(lambda e: e.tensor_scalar(scl[0:1, 10:11], scl[0:1, 20:21],
                                       INV_QMAX, 1e-12, Alu.mult, Alu.max),
             deps=[t_mg2], pub=True, tiny=True)
    t_i4 = V(lambda e: e.reciprocal(scl[0:1, 11:12], scl[0:1, 10:11]),
             deps=[t_s4], pub=True, tiny=True)
    t_c4 = V(lambda e: e.tensor_tensor(scl[0:1, 13:14], scl[0:1, 12:13],
                                       scl[0:1, 11:12], Alu.mult),
             deps=[t_i4], pub=True, tiny=True)
    t_s4u = V(lambda e: e.tensor_tensor(scl[0:1, 14:15], scl[0:1, 10:11],
                                        scl[0:1, 8:9], Alu.mult),
              deps=[t_s4, t_sup], pub=True, tiny=True)
    t_bc3 = bcast_scl([t_iup, t_c4, t_s4u])

    dbg_cap("scl3", scl[:, :], [t_bc3, t_s4u, t_c4], 32, parts=1)
    o_toks = []
    upr_free2 = list(up_last_mxe)
    w2_gu = [None] * 2
    w2_gg = [None] * 2
    for m in range(24):
        for n in range(4):
            i = m * 4 + n
            sl = i % 8
            ru = rtring[i % 2]
            rg = rtring[2 + i % 2]
            rd = dma("sync", f"upr{sl}", uprng[sl][:, :], upspill.ap()[m, n],
                     deps=[up_spill_toks[(m, n)], upr_free2[sl]])
            t1 = S(lambda e, sl=sl, ru=ru: e.activation(
                ru[:, :], uprng[sl][:, :], Act.Copy, bias=MAGIC,
                scale=sclb[:, 9:10]),
                deps=[rd, t_bc3, w2_gu[i % 2]], pub=True)
            t2 = V(lambda e, sl=sl, ru=ru: e.tensor_scalar(
                uprng[sl][:, :], ru[:, :], MAGIC, None, Alu.subtract),
                deps=[t1], pub=True)
            w2_gu[i % 2] = t2
            t3 = S(lambda e, m=m, n=n, rg=rg: e.activation(
                rg[:, :], gP(m)[:, 512 * n:512 * n + 512], Act.Copy,
                bias=MAGIC, scale=sclb[:, 13:14]),
                deps=[t_bc3, w2_gg[i % 2]] + w1_toks, pub=True)
            t4 = V(lambda e, m=m, n=n, sl=sl, rg=rg: e.scalar_tensor_tensor(
                gP(m)[:, 512 * n:512 * n + 512], rg[:, :], MAGIC,
                uprng[sl][:, :], Alu.subtract, Alu.mult),
                deps=[t2, t3], pub=True)
            w2_gg[i % 2] = t4
            upr_free2[sl] = t4
            o_toks.append(V(lambda e, m=m, n=n: e.tensor_reduce(
                mxc[:, 4 * m + n:4 * m + n + 1], gP(m)[:, 512 * n:512 * n + 512],
                AX, Alu.max, apply_absolute_value=True), deps=[t4], pub=True, tiny=True))
    dbg_cap("o_0", gP(0)[:, 0:512], o_toks, 512)
    t_mxo = global_max((0, 96), 15, o_toks)
    t_ar4 = allreduce_pair(15, [t_mxo])
    t_mo = V(lambda e: e.tensor_tensor(scl[0:1, 16:17], scl[0:1, 15:16],
                                       scl[0:1, 14:15], Alu.mult),
             deps=[t_ar4, t_s4u], pub=True, tiny=True)
    t_s5 = V(lambda e: e.tensor_scalar(scl[0:1, 17:18], scl[0:1, 16:17],
                                       INV_QMAX, 1e-12, Alu.mult, Alu.max),
             deps=[t_mo], pub=True, tiny=True)
    t_i5 = V(lambda e: e.reciprocal(scl[0:1, 18:19], scl[0:1, 17:18]),
             deps=[t_s5], pub=True, tiny=True)
    t_t5 = V(lambda e: e.tensor_tensor(scl[0:1, 19:20], scl[0:1, 14:15],
                                       scl[0:1, 18:19], Alu.mult),
             deps=[t_i5], pub=True, tiny=True)
    t_bc4 = bcast_scl([t_t5])
    t_gb5 = V(lambda e: e.tensor_scalar(adp[:, 48:56], adp[:, 40:48],
                                        sclb[:, 17:18], None, Alu.mult),
              deps=[t_bc4, t_gbq], pub=True, tiny=True)

    dbg_cap("scl4", scl[:, :], [t_bc4, t_gb5], 32, parts=1)
    dbg_cap("gb5", adp[:, 48:56], [t_gb5], 8)
    oq_toks = {}
    w3_rt_guard = [None] * 4
    for m in range(24):
        for n in range(4):
            hh = (m * 4 + n) % 4
            rs = rtring[hh]
            t1 = S(lambda e, m=m, n=n, rs=rs: e.activation(
                rs[:, 0:512], gP(m)[:, 512 * n:512 * n + 512], Act.Copy,
                bias=MAGIC, scale=sclb[:, 19:20]),
                deps=[t_bc4, w3_rt_guard[hh]] + o_toks, pub=True)
            oq_toks[(m, n)] = V(lambda e, m=m, n=n, rs=rs: e.tensor_scalar(
                gP(m)[:, 512 * n:512 * n + 512], rs[:, 0:512], MAGIC, None,
                Alu.subtract), deps=[t1], pub=True)
            w3_rt_guard[hh] = oq_toks[(m, n)]

    dbg_cap("oq_0", gP(0)[:, 0:512], [oq_toks[(0, 0)], oq_toks[(0, 1)]], 512)
    st_toks = []
    xr_toks = [None] * 8
    up_mm_done = up_ev[-1]
    dpend = [None]

    def flush_dpend(pnn, pmo, pev, psl):
        for q in range(4):
            si = tr_ctr[0] % 8
            tr_ctr[0] += 1
            t_tr = TT(lambda e, psl=psl, q=q, si=si: e.transpose(
                tr_slot(si), ctile[psl][:, 128 * q:128 * q + 128], id_b[:, :]),
                deps=[pev, tr_guard[si % 2]], pub=True)
            s8 = 4 * pnn + q
            ra = V(lambda e, s8=s8, pmo=pmo, si=si: e.tensor_tensor(
                xrr[s8][:, 128 * pmo:128 * pmo + 128],
                xrr[s8][:, 128 * pmo:128 * pmo + 128], tr_slot(si), Alu.add),
                deps=[t_tr, xr_toks[s8]], pub=True, tiny=True)
            tr_guard[si % 2] = ra
            xr_toks[s8] = ra
            ctile_free[psl] = t_tr

    for half in range(2):
        for q8 in range(8):
            n = 2 * half + q8 // 4
            q = q8 % 4
            xr_toks[q8] = dma("sync", f"xrr{q8}", xrr[q8][:, :],
                              xspill.ap()[4 * n + q],
                              deps=xsp_toks + (st_toks[-8:] if half else [])
                              + [xq_toks[-1], up_mm_done])
        for mo in range(8):
            pb = mo % 2
            wff_tok[pb] = ffn_col_load(48 + mo, pb, [wff_free[pb]])
            for nn in range(2):
                n = 2 * half + nn
                bk = (mo * 2 + nn) % 5
                mmt = None
                for k in range(24):
                    mmt = TT(lambda e, pb=pb, bk=bk, k=k, n=n: e.matmul(
                        psk(bk)[:, :], wffcol[pb][:, 128 * k:128 * k + 128],
                        gP(k)[:, 512 * n:512 * n + 512],
                        start=(k == 0), stop=(k == 23)),
                        deps=[wff_tok[pb], bank_free[bk], oq_toks[(k, n)]],
                        pub=(k == 23))
                sl = (mo * 2 + nn) % 4
                ev = S(lambda e, bk=bk, mo=mo, sl=sl: e.activation(
                    ctile[sl][:, :], psk(bk)[:, :], Act.Copy,
                    scale=adp[:, 48 + mo:49 + mo]),
                    deps=[mmt, t_gb5, ctile_free[sl]], pub=True, tiny=True)
                bank_free[bk] = ev
                wff_free[pb] = mmt
                if dpend[0] is not None:
                    flush_dpend(*dpend[0])
                dpend[0] = (nn, mo, ev, sl)
        if dpend[0] is not None:
            flush_dpend(*dpend[0])
            dpend[0] = None
        for q8 in range(8):
            n = 2 * half + q8 // 4
            q = q8 % 4
            st_toks.append(dma("sync", f"ost{q8 % 4}",
                               out_h[512 * n + 128 * q:512 * n + 128 * q + 128, :],
                               xrr[q8][:, :], deps=[xr_toks[q8]]))

    wait_list = st_toks[-4:]
    if dbg:
        wait_list = wait_list + [last_dma.get("dbgk")]
    emit("sync", lambda e: None, deps=wait_list)

    build_program.n_ops = len(ops)
    build_program.ops_dbg = [(o["eng"], o["fn"].__code__.co_firstlineno,
                              o["waits"], o["incs"]) for o in ops]
    if trunc is not None:
        ops[:] = ops[:trunc]
    with nc.Block() as block:
        def mk(eng_name):
            def run(e):
                for op in ops:
                    if op["eng"] != eng_name:
                        continue
                    for s, v in op["waits"]:
                        e.wait_ge(s, v)
                    ins = op["fn"](e)
                    for s, nn in op["incs"]:
                        assert ins is not None, "inc on op returning no instruction"
                        ins = ins.then_inc(s, nn)
            return run

        block.sync(mk("sync"))
        block.gpsimd(mk("gpsimd"))
        block.vector(mk("vector"))
        block.scalar(mk("scalar"))
        block.tensor(mk("tensor"))
    return nc


_NC_CACHE = None


def _get_nc():
    global _NC_CACHE
    if _NC_CACHE is None:
        _NC_CACHE = build_program(debug_sync=DEBUG_SYNC)
    return _NC_CACHE


def make_inmaps(L0_attn_x, L0_attn_in_w, L0_attn_in_b, L0_attn_conv_w,
                L0_attn_conv_b, L0_attn_xp_w, L0_attn_xp_b, L0_attn_out_w,
                L0_attn_out_b, L0_ffn_w_gate, L0_ffn_w_up, L0_ffn_w_down,
                L0_c, L0_wmod):
    f = np.float32
    x = np.ascontiguousarray(L0_attn_x, dtype=f).reshape(16384, D)
    in_w = np.ascontiguousarray(L0_attn_in_w, dtype=f)
    winT = np.ascontiguousarray(
        in_w.reshape(16, 128, 8, 128).transpose(0, 3, 2, 1))
    inb = np.ascontiguousarray(
        np.asarray(L0_attn_in_b, dtype=f).reshape(16, 128).T)
    convw = np.ascontiguousarray(
        np.asarray(L0_attn_conv_w, dtype=f).reshape(16, 128, 4)
        .transpose(1, 0, 2).reshape(128, 64))
    convb = np.ascontiguousarray(
        np.asarray(L0_attn_conv_b, dtype=f).reshape(16, 128).T)
    xpwT = np.ascontiguousarray(
        np.ascontiguousarray(L0_attn_xp_w, dtype=f).T.reshape(16, 128, 32))
    xpba = np.ascontiguousarray(np.asarray(L0_attn_xp_b, dtype=f)[:16].reshape(16, 1))
    xpbb = np.ascontiguousarray(np.asarray(L0_attn_xp_b, dtype=f)[16:].reshape(16, 1))
    outwT = np.ascontiguousarray(
        np.asarray(L0_attn_out_w, dtype=f).T.reshape(16, 8, 128).transpose(1, 0, 2))
    outb = np.ascontiguousarray(
        np.asarray(L0_attn_out_b, dtype=f).reshape(8, 128).T)
    wg = np.ascontiguousarray(L0_ffn_w_gate, dtype=f)
    wu = np.ascontiguousarray(L0_ffn_w_up, dtype=f)
    wd = np.ascontiguousarray(L0_ffn_w_down, dtype=f)
    cvec = np.ascontiguousarray(np.asarray(L0_c, dtype=f).reshape(8, 128).T)
    wmod = np.ascontiguousarray(L0_wmod, dtype=f)

    identf = np.eye(128, dtype=f)
    identb = np.eye(128, dtype=ml_dtypes.bfloat16)
    egu = np.zeros((32, 8 * 128), dtype=f)
    for kc in range(8):
        for kp in range(128):
            egu[4 * kc + kp // 32, 128 * kc + kp] = 1.0
    ed = np.zeros((96, 24 * 128), dtype=f)
    for kc in range(24):
        for kp in range(128):
            ed[4 * kc + kp // 32, 128 * kc + kp] = 1.0

    in_maps = []
    for r in range(NCORES):
        xs = np.empty((TH, D), dtype=f)
        t0 = r * T
        if r == 0:
            xs[:HALO] = 0.0
        else:
            xs[:HALO] = x[t0 - HALO:t0]
        xs[HALO:] = x[t0:t0 + T]
        rows_gu = np.empty((6, 128, D), dtype=f)
        lhsT_gu = np.empty((6, 128, 8, 128), dtype=f)
        for s in range(6):
            j = r + 8 * s
            if j < 24:
                W, m = wg, j
            else:
                W, m = wu, j - 24
            blk = W[128 * m:128 * m + 128, :]
            rows_gu[s] = blk
            lhsT_gu[s] = blk.reshape(128, 8, 128).transpose(2, 1, 0)
        md = (r + 48) - 48
        blkd = wd[128 * md:128 * md + 128, :]
        rows_d = np.ascontiguousarray(blkd)
        lhsT_d = np.ascontiguousarray(blkd.reshape(128, 24, 128).transpose(2, 1, 0))
        wms = wmod[:, 768 * r:768 * r + 768]
        wmod_t = np.ascontiguousarray(
            wms.reshape(8, 128, 6, 128).transpose(2, 0, 1, 3))
        onehot = np.zeros((NS, 8), dtype=f)
        if r > 0:
            onehot[:, r - 1] = 1.0
        hmask = np.full((128, 1), 0.0 if r == 0 else 1.0, dtype=f)
        in_maps.append({
            "x": np.ascontiguousarray(xs),
            "winT": winT, "inb": inb, "convw": convw, "convb": convb,
            "xpwT": xpwT, "xpba": xpba, "xpbb": xpbb, "outwT": outwT, "outb": outb,
            "lpq_rows_gu": np.ascontiguousarray(rows_gu),
            "lpq_lhsT_gu": np.ascontiguousarray(lhsT_gu),
            "lpq_rows_d": rows_d, "lpq_lhsT_d": lhsT_d,
            "cvec": cvec, "wmod": wmod_t,
            "identf": identf, "identb": identb, "egu": egu, "ed": ed,
            "onehot": onehot, "hmask": hmask,
            "ones1": np.ones((1, 128), dtype=f),
        })

    return in_maps


def kernel(**inputs):
    in_maps = make_inmaps(**inputs)
    nc = _get_nc()
    res = run_bass_kernel_spmd(nc, in_maps, core_ids=list(range(NCORES)))
    out = np.concatenate([np.asarray(r["out"], dtype=np.float32)
                          for r in res.results], axis=0)
    return out.reshape(1, 16384, D)
